# version 07
# speedup vs baseline: 1.8093x; 1.0552x over previous
import contextlib
import numpy as np
import ml_dtypes
import concourse.bass as bass
import concourse.mybir as mybir
from concourse.bass import IndirectOffsetOnAxis
from concourse.bass_utils import run_bass_kernel_spmd

F32 = mybir.dt.float32
BF16 = mybir.dt.bfloat16
U32 = mybir.dt.uint32
AF = mybir.ActivationFunctionType
ALU = mybir.AluOpType
AX = mybir.AxisListType

NCORE = 8
DM = 1024
NH = 16
DH = 64
NEXP = 8
FF = 4096
BIGF = 1.0e7

NT = 32
PPAD = 768
DEBUG = False

RG = [list(range(NCORE))]


def _splits(total, step=512):
    out, o = [], 0
    while o < total:
        out.append((o, min(step, total - o)))
        o += step
    return out


class Sem:
    def __init__(self, nc, es, name):
        self.h = es.enter_context(nc.semaphore(name))
        self.n = 0
        self.last_chained = None

    def dma(self, instr):
        instr.then_inc(self.h, 16)
        self.n += 16
        return self.n

    def one(self, instr):
        if instr is self.last_chained:
            return self.n
        instr.then_inc(self.h, 1)
        self.n += 1
        return self.n


class Chain:
    PASSTHRU = ("wait_ge", "dma_start", "indirect_dma_start",
                "collective_compute", "engine_nop")

    def __init__(self, e, sem):
        self._e = e
        self._s = sem

    def __getattr__(self, name):
        fn = getattr(self._e, name)
        if name in Chain.PASSTHRU or not callable(fn):
            return fn
        s = self._s
        raw = self._e

        def wrapped(*a, **k):
            raw.wait_ge(s.h, s.n)
            ins = fn(*a, **k)
            ins.then_inc(s.h, 1)
            s.n += 1
            s.last_chained = ins
            return ins
        return wrapped


def build(nt=None, ppad=None, debug=None, stop_after=None):
    nt = NT if nt is None else nt
    ppad = PPAD if ppad is None else ppad
    debug = DEBUG if debug is None else debug

    T = 128 * nt
    S = T // 2
    ST = S // 128
    CAP = T
    PSUB = ppad // 128
    NSORT = NEXP * ppad
    FCN = FF // 128
    DCN = DM // 128
    QCH = _splits(S, 1024)

    nc = bass.Bass("TRN2", target_bir_lowering=False, debug=False)

    x_d = nc.dram_tensor("x", [T, DM], F32, kind="ExternalInput").ap()
    wr_d = nc.dram_tensor("wr", [DM, NEXP], F32, kind="ExternalInput").ap()
    w1_d = nc.dram_tensor("w1", [NEXP, FF // 256, 128, (DM // 128) * 256], F32, kind="ExternalInput").ap()
    w2_d = nc.dram_tensor("w2", [NEXP, FF, DM], F32, kind="ExternalInput").ap()
    wqkv_d = nc.dram_tensor("wqkv", [2, 128, 3 * DH], F32, kind="ExternalInput").ap()
    woutT_d = nc.dram_tensor("woutT", [DM, DM], F32, kind="ExternalInput").ap()
    bout_d = nc.dram_tensor("bout", [1, DM], F32, kind="ExternalInput").ap()
    iota8_d = nc.dram_tensor("iota8", [128, NEXP], F32, kind="ExternalInput").ap()
    tri_d = nc.dram_tensor("tri", [128, 128], F32, kind="ExternalInput").ap()
    idf_d = nc.dram_tensor("idf", [128, 128], F32, kind="ExternalInput").ap()
    idb_d = nc.dram_tensor("idb", [128, 128], BF16, kind="ExternalInput").ap()
    pmask_d = nc.dram_tensor("pmask", [NCORE, 1], F32, kind="ExternalInput").ap()
    ones_d = nc.dram_tensor("onesin", [128, 128], F32, kind="ExternalInput").ap()

    o_d = nc.dram_tensor("o", [T, DM], F32, kind="ExternalOutput").ap()
    dbg = {}
    if debug:
        for nm, shp, dt in [("eidx", [128, nt], F32), ("p", [128, nt], F32),
                            ("lrank", [128, nt], F32), ("pscale", [128, nt], F32),
                            ("gidx", [128, nt], U32), ("sidx", [128, nt], U32),
                            ("hist", [NCORE, NEXP], F32),
                            ("x1", [T, DM], BF16), ("x2", [T, DM], BF16)]:
            dbg[nm] = nc.dram_tensor("dbg_" + nm, shp, dt, kind="ExternalOutput").ap()

    xres_b = nc.dram_tensor("xres_b", [T, DM], BF16).ap()
    sort_b = nc.dram_tensor("sort_b", [NSORT, DM], BF16).ap()
    yeout_b = nc.dram_tensor("yeout_b", [NSORT, DM], BF16).ap()
    zsum_b = nc.dram_tensor("zsum_b", [1, NH * S], BF16).ap()
    ag_in = [nc.dram_tensor(f"ag_in{k}", [1, NEXP], F32).ap() for k in range(2)]
    ag_out = [nc.dram_tensor(f"ag_out{k}", [NCORE, NEXP], F32).ap() for k in range(2)]

    es = contextlib.ExitStack()
    with es:
        E = es.enter_context
        cpool = E(nc.sbuf_tensor("cpool", [128, 8], F32))
        iota8_sb = E(nc.sbuf_tensor("iota8_sb", [128, NEXP], F32))
        tri_sb = E(nc.sbuf_tensor("tri_sb", [128, 128], F32))
        idf_sb = E(nc.sbuf_tensor("idf_sb", [128, 128], F32))
        idb_sb = E(nc.sbuf_tensor("idb_sb", [128, 128], BF16))
        ones_sb = E(nc.sbuf_tensor("ones_sb", [128, 128], F32))
        onesb_sb = E(nc.sbuf_tensor("onesb_sb", [1, 128], BF16))
        pmask_sb = E(nc.sbuf_tensor("pmask_sb", [NCORE, 1], F32))
        wr_sb = E(nc.sbuf_tensor("wr_sb", [128, DCN * NEXP], F32))
        wqkv_sb = E(nc.sbuf_tensor("wqkv_sb", [128, 2 * 3 * DH], BF16))
        bout_sb = E(nc.sbuf_tensor("bout_sb", [1, DM], BF16))
        agout_sb = [E(nc.sbuf_tensor(f"agout_sb{k}", [NCORE, NEXP], F32)) for k in range(2)]
        p_v = E(nc.sbuf_tensor("p_v", [128, nt], F32))
        eidx_v = E(nc.sbuf_tensor("eidx_v", [128, nt], F32))
        lrank_v = E(nc.sbuf_tensor("lrank_v", [128, nt], F32))
        pscale_v = E(nc.sbuf_tensor("pscale_v", [128, nt], F32))
        sidx_v = E(nc.sbuf_tensor("sidx_v", [128, nt], U32))
        gidx_v = E(nc.sbuf_tensor("gidx_v", [128, nt], U32))
        oh_v = E(nc.sbuf_tensor("oh_v", [128, NEXP * nt], F32))
        lgall = E(nc.sbuf_tensor("lgall", [128, NEXP * nt], F32))
        exall = E(nc.sbuf_tensor("exall", [128, NEXP * nt], F32))
        mx8all = E(nc.sbuf_tensor("mx8all", [128, NEXP * nt], F32))
        pref_v = E(nc.sbuf_tensor("pref_v", [128, NEXP * nt], F32))
        totrow = E(nc.sbuf_tensor("totrow", [1, NEXP * nt], F32))
        runrow = E(nc.sbuf_tensor("runrow", [1, NEXP * (nt + 1)], F32))
        combrow = E(nc.sbuf_tensor("combrow", [1, NEXP * nt], F32))
        offs_row = E(nc.sbuf_tensor("offs_row", [1, NEXP], F32))
        hist_row = E(nc.sbuf_tensor("hist_row", [1, NEXP], F32))
        scr = E(nc.sbuf_tensor("scr", [128, 64], F32))
        scru = E(nc.sbuf_tensor("scru", [128, 16], U32))
        dump_sb = E(nc.sbuf_tensor("dump_sb", [128, DM], BF16)) if debug else None

        LD = Sem(nc, es, "LD")
        GD = Sem(nc, es, "GD")
        SD = Sem(nc, es, "SD")
        CC = Sem(nc, es, "CC")
        TE = Sem(nc, es, "TE")
        VE = Sem(nc, es, "VE")
        AE = Sem(nc, es, "AE")
        LDX = [Sem(nc, es, f"LDX{i}") for i in range(2)]
        XE = [Sem(nc, es, f"XE{i}") for i in range(2)]
        SC = [Sem(nc, es, f"SC{i}") for i in range(2)]
        GX = [Sem(nc, es, f"GX{i}") for i in range(2)]
        GA = [Sem(nc, es, f"GA{i}") for i in range(2)]
        SDY = [Sem(nc, es, f"SDY{i}") for i in range(2)]
        SDS = [Sem(nc, es, f"SDS{i}") for i in range(2)]
        W1D = [Sem(nc, es, f"W1D{i}") for i in range(2)]
        W2D = Sem(nc, es, "W2D")

        for cv, ci in [(0.0, 0), (1.0, 1), (0.5, 2), (0.03125, 3), (-0.5, 4)]:
            nc.const_aps.register(F32, cv, cpool[:, ci:ci + 1])

        blk = E(nc.Block())

        def BV(f):
            blk.vector(lambda e: f(Chain(e, VE)))

        def BA(f):
            blk.scalar(lambda e: f(Chain(e, AE)))

        @blk.sync
        def _(e):
            for dst, src in [(iota8_sb, iota8_d), (tri_sb, tri_d), (idf_sb, idf_d),
                             (idb_sb, idb_d), (ones_sb, ones_d), (pmask_sb, pmask_d)]:
                LD.dma(e.dma_start(out=dst[:], in_=src[:]))
            for dc in range(DCN):
                LD.dma(e.dma_start(out=wr_sb[:, dc * NEXP:(dc + 1) * NEXP],
                                   in_=wr_d[dc * 128:(dc + 1) * 128, :]))
        init_ld = LD.n

        bnd_reg = nc.gpsimd.alloc_register("bnd")

        @blk.gpsimd
        def _(e):
            e.reg_mov(bnd_reg, NSORT - 1)
            GD.dma(e.dma_start(out=wqkv_sb[:, 0:192], in_=wqkv_d[0]))
            GD.dma(e.dma_start(out=wqkv_sb[:, 192:384], in_=wqkv_d[1]))
            GD.dma(e.dma_start(out=bout_sb[:], in_=bout_d[:]))
        init_gd = GD.n

        @BV
        def _(e):
            for cv, ci in [(0.0, 0), (1.0, 1), (0.5, 2), (0.03125, 3), (-0.5, 4)]:
                e.memset(cpool[:, ci:ci + 1], cv)
            e.memset(runrow[:, 0:NEXP], 0.0)
            e.wait_ge(LD.h, init_ld)
            VE.one(e.tensor_copy(onesb_sb[:], ones_sb[0:1, :]))
        init_ve = VE.n

        G = dict(nc=nc, blk=blk, nt=nt, ppad=ppad, debug=debug, dbg=dbg,
                 T=T, S=S, ST=ST, CAP=CAP, PSUB=PSUB, NSORT=NSORT, FCN=FCN,
                 DCN=DCN, QCH=QCH,
                 x_d=x_d, o_d=o_d, w1_d=w1_d, w2_d=w2_d, woutT_d=woutT_d,
                 xres_b=xres_b, sort_b=sort_b, yeout_b=yeout_b, zsum_b=zsum_b,
                 ag_in=ag_in, ag_out=ag_out, agout_sb=agout_sb,
                 cpool=cpool, iota8_sb=iota8_sb, tri_sb=tri_sb, idf_sb=idf_sb,
                 idb_sb=idb_sb, ones_sb=ones_sb, onesb_sb=onesb_sb,
                 pmask_sb=pmask_sb, wr_sb=wr_sb, wqkv_sb=wqkv_sb, bout_sb=bout_sb,
                 p_v=p_v, eidx_v=eidx_v, lrank_v=lrank_v, pscale_v=pscale_v,
                 sidx_v=sidx_v, gidx_v=gidx_v, oh_v=oh_v, pref_v=pref_v,
                 totrow=totrow, runrow=runrow, combrow=combrow, offs_row=offs_row,
                 hist_row=hist_row, scr=scr, scru=scru,
                 lgall=lgall, exall=exall, mx8all=mx8all,
                 dump_sb=dump_sb,
                 LD=LD, GD=GD, SD=SD, CC=CC, TE=TE, VE=VE, AE=AE,
                 LDX=LDX, XE=XE, SC=SC, GX=GX, GA=GA, SDY=SDY, SDS=SDS,
                 W1D=W1D, W2D=W2D, BV=BV, BA=BA, bnd_reg=bnd_reg,
                 init_ld=init_ld, init_gd=init_gd, init_ve=init_ve)

        G["stop_after"] = stop_after
        emit_ffn(G, 0)
        if stop_after not in ("p1", "p2", "ffn0"):
            emit_attention(G)
            if stop_after not in ("attn", "aln", "ahead", "anorm"):
                emit_ffn(G, 1)

        @blk.gpsimd
        def _(e):
            for s in [SD, GD, LD, W2D] + W1D + LDX + XE + SC + GX + GA + SDY + SDS:
                e.wait_ge(s.h, s.n)
    return nc


def emit_dump(G, src_dram, dst_dram, nrows, waits):
    nc, blk, LD, SD = G["nc"], G["blk"], G["LD"], G["SD"]
    BA = G["BA"]
    bsb = G["dump_sb"]
    sd_pre = SD.n
    lds = []
    sds = []
    for r in range(nrows // 128):
        @blk.sync
        def _(e, r=r):
            for sv, vv in waits:
                e.wait_ge(sv.h, vv)
            e.wait_ge(SD.h, sds[r - 1] if r >= 1 else sd_pre)
            lds.append(LD.dma(e.dma_start(out=bsb[:],
                                          in_=src_dram[r * 128:(r + 1) * 128, :])))

        @BA
        def _(e, r=r):
            e.wait_ge(LD.h, lds[r])
            sds.append(SD.dma(e.dma_start(out=dst_dram[r * 128:(r + 1) * 128, :],
                                          in_=bsb[:])))


def emit_ffn(G, call):
    nc, blk = G["nc"], G["blk"]
    nt, ppad, PSUB, NSORT = G["nt"], G["ppad"], G["PSUB"], G["NSORT"]
    T, CAP, FCN, DCN = G["T"], G["CAP"], G["FCN"], G["DCN"]
    LD, GD, SD, CC, TE, VE, AE = (G[k] for k in ["LD", "GD", "SD", "CC", "TE", "VE", "AE"])
    LDX, XE, SC, GX, GA, SDY, SDS = (G[k] for k in ["LDX", "XE", "SC", "GX", "GA", "SDY", "SDS"])
    W1D, W2D = G["W1D"], G["W2D"]
    BV, BA = G["BV"], G["BA"]
    (x_d, xres_b, sort_b, yeout_b, o_d, w1_d, w2_d, ag_in, ag_out, agout_sb) = (
        G[k] for k in ["x_d", "xres_b", "sort_b", "yeout_b", "o_d", "w1_d", "w2_d",
                       "ag_in", "ag_out", "agout_sb"])
    (iota8_sb, tri_sb, idf_sb, idb_sb, ones_sb, pmask_sb, wr_sb) = (
        G[k] for k in ["iota8_sb", "tri_sb", "idf_sb", "idb_sb", "ones_sb",
                       "pmask_sb", "wr_sb"])
    (p_v, eidx_v, lrank_v, pscale_v, sidx_v, gidx_v, oh_v, pref_v) = (
        G[k] for k in ["p_v", "eidx_v", "lrank_v", "pscale_v", "sidx_v", "gidx_v",
                       "oh_v", "pref_v"])
    (totrow, runrow, combrow, offs_row, hist_row, scr, scru) = (
        G[k] for k in ["totrow", "runrow", "combrow", "offs_row", "hist_row",
                       "scr", "scru"])
    dbg, debug = G["dbg"], G["debug"]
    bnd_reg = G["bnd_reg"]
    final = (call == 1)
    src_fp32 = (call == 0)
    xsrc = x_d if src_fp32 else xres_b

    with contextlib.ExitStack() as ph:
        E = ph.enter_context
        ybuf_sb = [E(nc.sbuf_tensor(f"c{call}_yb{i}", [128, DM], BF16)) for i in range(2)]
        xr_sb = [E(nc.sbuf_tensor(f"c{call}_xr{i}", [128, DM], BF16)) for i in range(2)]
        out_sb = [E(nc.sbuf_tensor(f"c{call}_out{i}", [128, DM], F32 if final else BF16))
                  for i in range(2)]
        p1s = ph.enter_context(contextlib.ExitStack())
        E1 = p1s.enter_context
        if src_fp32:
            xf_sb = [E1(nc.sbuf_tensor(f"c{call}_xf{i}", [128, DM], F32)) for i in range(2)]
        else:
            xf_sb = None
        xb_sb = [E1(nc.sbuf_tensor(f"c{call}_xb{i}", [128, DM], BF16)) for i in range(2)]
        xln_sb = [E1(nc.sbuf_tensor(f"c{call}_xln{i}", [128, DM], F32)) for i in range(2)]
        xlnb_sb = [E1(nc.sbuf_tensor(f"c{call}_xlnb{i}", [128, DM], BF16)) for i in range(nt)]
        xlnT_sb = [E1(nc.sbuf_tensor(f"c{call}_xlnT{i}", [128, DM], F32)) for i in range(2)]
        zero_sb = E1(nc.sbuf_tensor(f"c{call}_zero", [128, DM], BF16))
        tr_ps = [E1(nc.psum_tensor(f"c{call}_tr{i}", [128, DM], F32)) for i in range(2)]
        lg_ps = E1(nc.psum_tensor(f"c{call}_lg", [128, NEXP], F32))

        ldA = [0] * nt
        aeSQ = [0] * nt
        veLN = [0] * nt
        teTR = [0] * nt
        aeTR = [0] * nt
        teLG = [0] * nt
        aeEX = [0] * nt
        veOH = [0] * nt
        tePR = [0] * nt
        veSX = [0] * nt
        gdSC = [0] * nt
        gdXR = [0] * nt

        sd_prefill = SD.n
        if call == 0:
            @BV
            def _(e):
                VE.one(e.memset(zero_sb[:], 0.0))
            zv = VE.n

            @BA
            def _(e):
                e.wait_ge(VE.h, zv)
                for r in range(NSORT // 128):
                    SD.dma(e.dma_start(out=sort_b[r * 128:(r + 1) * 128, :], in_=zero_sb[:]))
            sd_prefill = SD.n

        veTR = [0] * nt
        veLG = [0] * nt
        lgall = G["lgall"]
        exall = G["exall"]
        mx8all = G["mx8all"]
        preB_ps = E1(nc.psum_tensor(f"c{call}_preB", [128, 32], F32))
        totB_ps = E1(nc.psum_tensor(f"c{call}_totB", [1, 32], F32))
        bc_ps = E1(nc.psum_tensor(f"c{call}_bc", [128, 32], F32))

        for j in range(nt):
            b = j % 2
            xin = xf_sb[b] if src_fp32 else xb_sb[b]
            sc = 32 * b

            @blk.sync
            def _(e, j=j, b=b, xin=xin):
                e.wait_ge(VE.h, G["init_ve"])
                if j >= 2:
                    e.wait_ge(VE.h, veLN[j - 2])
                    if src_fp32:
                        e.wait_ge(GX[b].h, gdXR[j - 2])
                if not src_fp32:
                    for sv, vv in G["sd_xres_done"]:
                        e.wait_ge(sv.h, vv)
                ldA[j] = LDX[b].dma(e.dma_start(out=xin[:], in_=xsrc[j * 128:(j + 1) * 128, :]))

            @BV
            def _(e, j=j, b=b, xin=xin, sc=sc):
                e.wait_ge(LDX[b].h, ldA[j])
                e.bn_stats(scr[:, sc + 0:sc + 6], xin[:, 0:512])
                e.bn_stats(scr[:, sc + 6:sc + 12], xin[:, 512:1024])
                e.bn_aggr(scr[:, sc + 12:sc + 14], scr[:, sc + 0:sc + 12])
                VE.one(e.tensor_scalar_add(scr[:, sc + 14:sc + 15],
                                           scr[:, sc + 13:sc + 14], 1e-5))
            vStat = VE.n

            @BA
            def _(e, j=j, sc=sc, vStat=vStat):
                e.wait_ge(VE.h, vStat)
                e.activation(scr[:, sc + 15:sc + 16], scr[:, sc + 14:sc + 15], AF.Ln)
                aeSQ[j] = AE.one(e.activation(scr[:, sc + 16:sc + 17],
                                              scr[:, sc + 15:sc + 16], AF.Exp,
                                              scale=-0.5))

            @BV
            def _(e, j=j, b=b, xin=xin, sc=sc):
                e.wait_ge(AE.h, aeSQ[j])
                if j >= 2:
                    e.wait_ge(TE.h, teTR[j - 2])
                e.tensor_scalar(xln_sb[b][:], xin[:], scr[:, sc + 12:sc + 13],
                                scr[:, sc + 16:sc + 17], ALU.subtract, ALU.mult)
                veLN[j] = VE.one(e.tensor_copy(xlnb_sb[j][:], xln_sb[b][:]))

            @blk.tensor
            def _(e, j=j, b=b):
                e.wait_ge(VE.h, veLN[j])
                if j >= 2:
                    e.wait_ge(VE.h, veTR[j - 2])
                for dc in range(DCN):
                    ins = e.transpose(tr_ps[b][:, dc * 128:(dc + 1) * 128],
                                      xln_sb[b][:, dc * 128:(dc + 1) * 128], idf_sb[:])
                teTR[j] = TE.one(ins)

            @BV
            def _(e, j=j, b=b):
                e.wait_ge(TE.h, teTR[j])
                veTR[j] = VE.one(e.tensor_copy(xlnT_sb[b][:], tr_ps[b][:]))

            @blk.tensor
            def _(e, j=j, b=b):
                e.wait_ge(VE.h, veTR[j])
                if j >= 1:
                    e.wait_ge(VE.h, veLG[j - 1])
                for dc in range(DCN):
                    ins = e.matmul(lg_ps[:], xlnT_sb[b][:, dc * 128:(dc + 1) * 128],
                                   wr_sb[:, dc * NEXP:(dc + 1) * NEXP],
                                   start=(dc == 0), stop=(dc == DCN - 1))
                teLG[j] = TE.one(ins)

            @BV
            def _(e, j=j):
                e.wait_ge(TE.h, teLG[j])
                veLG[j] = VE.one(e.tensor_copy(lgall[:, j * NEXP:(j + 1) * NEXP], lg_ps[:]))

            if src_fp32:
                @blk.gpsimd
                def _(e, j=j, b=b):
                    e.wait_ge(VE.h, veLN[j])
                    gdXR[j] = GX[b].dma(e.dma_start(out=xres_b[j * 128:(j + 1) * 128, :],
                                                    in_=xf_sb[b][:]))

        @BA
        def _(e):
            e.wait_ge(VE.h, veLG[nt - 1])
            AE.one(e.activation(exall[:], lgall[:], AF.Exp))
        aeEXall = AE.n

        nblk = nt // 4
        veOH4 = [0] * nblk
        veSX4 = [0] * nblk
        tePR4 = [0] * nblk
        teBC4 = [0] * nblk
        veRR4 = [0] * nblk
        for B in range(nblk):
            @BV
            def _(e, B=B):
                if B == 0:
                    e.wait_ge(AE.h, aeEXall)
                for j in range(4 * B, 4 * B + 4):
                    bb8 = 8 * (j % 2)
                    e.max(mx8all[:, j * NEXP:(j + 1) * NEXP],
                          exall[:, j * NEXP:(j + 1) * NEXP])
                    e.max_index(scru[:, bb8:bb8 + 8], mx8all[:, j * NEXP:(j + 1) * NEXP],
                                exall[:, j * NEXP:(j + 1) * NEXP])
                    e.tensor_copy(eidx_v[:, j:j + 1], scru[:, bb8:bb8 + 1])
                    e.tensor_single_scalar(oh_v[:, j * NEXP:(j + 1) * NEXP],
                                           iota8_sb[:], eidx_v[:, j:j + 1], ALU.is_equal)
                s0 = 32 * B
                e.tensor_reduce(scr[:, 0:4].rearrange("p (g one) -> p g one", one=1),
                                exall[:, s0:s0 + 32].rearrange("p (g x) -> p g x", x=8),
                                axis=AX.X, op=ALU.add)
                e.reciprocal(scr[:, 4:8], scr[:, 0:4])
                VE.one(e.tensor_tensor(
                    p_v[:, 4 * B:4 * B + 4],
                    mx8all[:, s0:s0 + 32].rearrange("p (g x) -> p g x", x=8)[:, :, 0],
                    scr[:, 4:8], ALU.mult))
            veOH4[B] = VE.n

            @blk.tensor
            def _(e, B=B):
                e.wait_ge(VE.h, veOH4[B])
                if B >= 1:
                    e.wait_ge(VE.h, veRR4[B - 1])
                e.matmul(preB_ps[:], tri_sb[:], oh_v[:, 32 * B:32 * B + 32],
                         start=True, stop=True)
                tePR4[B] = TE.one(e.matmul(totB_ps[:], ones_sb[:, 0:1],
                                           oh_v[:, 32 * B:32 * B + 32],
                                           start=True, stop=True))

            @BV
            def _(e, B=B):
                e.wait_ge(TE.h, tePR4[B])
                e.tensor_copy(pref_v[:, 32 * B:32 * B + 32], preB_ps[:])
                e.tensor_copy(totrow[:, 32 * B:32 * B + 32], totB_ps[:])
                last = None
                for j in range(4 * B, 4 * B + 4):
                    last = e.tensor_add(runrow[:, (j + 1) * NEXP:(j + 2) * NEXP],
                                        runrow[:, j * NEXP:(j + 1) * NEXP],
                                        totrow[:, j * NEXP:(j + 1) * NEXP])
                VE.one(last)
            veRR4[B] = VE.n

            @blk.tensor
            def _(e, B=B):
                e.wait_ge(VE.h, veRR4[B])
                if B >= 1:
                    e.wait_ge(VE.h, veSX4[B - 1])
                teBC4[B] = TE.one(e.matmul(bc_ps[:], ones_sb[0:1, :],
                                           runrow[:, 32 * B:32 * B + 32],
                                           start=True, stop=True))

            @BV
            def _(e, B=B):
                e.wait_ge(TE.h, teBC4[B])
                s0 = 32 * B
                e.tensor_add(pref_v[:, s0:s0 + 32], pref_v[:, s0:s0 + 32], bc_ps[:])
                e.tensor_mul(scr[:, 32:64], oh_v[:, s0:s0 + 32], pref_v[:, s0:s0 + 32])
                e.tensor_reduce(lrank_v[:, 4 * B:4 * B + 4].rearrange(
                                    "p (g one) -> p g one", one=1),
                                scr[:, 32:64].rearrange("p (g x) -> p g x", x=8),
                                axis=AX.X, op=ALU.add)
                e.tensor_single_scalar(scr[:, 0:4], lrank_v[:, 4 * B:4 * B + 4],
                                       float(ppad), ALU.is_lt)
                e.tensor_scalar_mul(scr[:, 4:8], eidx_v[:, 4 * B:4 * B + 4], float(ppad))
                e.tensor_add(scr[:, 4:8], scr[:, 4:8], lrank_v[:, 4 * B:4 * B + 4])
                e.tensor_scalar(scr[:, 8:12], scr[:, 0:4], -BIGF, BIGF,
                                ALU.mult, ALU.add)
                e.tensor_add(scr[:, 4:8], scr[:, 4:8], scr[:, 8:12])
                VE.one(e.tensor_copy(sidx_v[:, 4 * B:4 * B + 4], scr[:, 4:8]))
            veSX4[B] = VE.n

            for j in range(4 * B, 4 * B + 4):
                @blk.gpsimd
                def _(e, j=j, B=B):
                    e.wait_ge(VE.h, veSX4[B])
                    if call == 0:
                        e.wait_ge(SD.h, sd_prefill)
                    if j >= 1:
                        e.wait_ge(SC[1 - j % 2].h, gdSC[j - 1])
                    gdSC[j] = SC[j % 2].dma(e.indirect_dma_start(
                        out=sort_b, out_offset=IndirectOffsetOnAxis(sidx_v[:, j:j + 1], 0),
                        in_=xlnb_sb[j][:], in_offset=None,
                        bounds_check=bnd_reg, oob_is_err=False))
        veSXall = VE.n

        @BV
        def _(e):
            VE.one(e.tensor_copy(hist_row[:], runrow[:, nt * NEXP:(nt + 1) * NEXP]))
        veHist = VE.n

        @blk.gpsimd
        def _(e):
            e.wait_ge(VE.h, veHist)
            GD.dma(e.dma_start(out=ag_in[call], in_=hist_row[:]))
            e.wait_ge(GD.h, GD.n)
            CC.one(e.collective_compute("AllGather", ALU.bypass, replica_groups=RG,
                                        ins=[ag_in[call]], outs=[ag_out[call]]))
            e.wait_ge(CC.h, CC.n)
            GD.dma(e.dma_start(out=agout_sb[call][:], in_=ag_out[call]))
        gdAg = GD.n

        @blk.tensor
        def _(e):
            e.wait_ge(GD.h, gdAg)
            e.wait_ge(VE.h, veSXall)
            TE.one(e.matmul(totB_ps[0:1, 0:NEXP], pmask_sb[:], agout_sb[call][:],
                            start=True, stop=True))
        teOffs = TE.n

        @BV
        def _(e):
            e.wait_ge(TE.h, teOffs)
            last = None
            for r in range(4):
                last = e.tensor_copy(offs_row[:] if False else
                                     combrow[:, r * NEXP:(r + 1) * NEXP],
                                     totB_ps[0:1, 0:NEXP])
            VE.one(last)
        veOffsC = VE.n

        @blk.tensor
        def _(e):
            e.wait_ge(VE.h, veOffsC)
            TE.one(e.matmul(bc_ps[:], ones_sb[0:1, :], combrow[:, 0:32],
                            start=True, stop=True))
        teBCoffs = TE.n

        vePC4 = [0] * nblk
        for B in range(nblk):
            @BV
            def _(e, B=B):
                e.wait_ge(TE.h, teBCoffs)
                s0 = 32 * B
                e.tensor_mul(scr[:, 32:64], oh_v[:, s0:s0 + 32], bc_ps[:])
                e.tensor_reduce(scr[:, 0:4].rearrange("p (g one) -> p g one", one=1),
                                scr[:, 32:64].rearrange("p (g x) -> p g x", x=8),
                                axis=AX.X, op=ALU.add)
                e.tensor_add(scr[:, 0:4], scr[:, 0:4], lrank_v[:, 4 * B:4 * B + 4])
                e.tensor_single_scalar(scr[:, 4:8], scr[:, 0:4], float(CAP), ALU.is_lt)
                e.tensor_single_scalar(scr[:, 8:12], lrank_v[:, 4 * B:4 * B + 4],
                                       float(ppad), ALU.is_lt)
                e.tensor_mul(scr[:, 4:8], scr[:, 4:8], scr[:, 8:12])
                e.tensor_mul(scr[:, 8:12], scr[:, 4:8], p_v[:, 4 * B:4 * B + 4])
                e.tensor_scalar_mul(pscale_v[:, 4 * B:4 * B + 4], scr[:, 8:12], 0.5)
                e.tensor_scalar_mul(scr[:, 8:12], eidx_v[:, 4 * B:4 * B + 4], float(ppad))
                e.tensor_add(scr[:, 8:12], scr[:, 8:12], lrank_v[:, 4 * B:4 * B + 4])
                e.tensor_scalar(scr[:, 12:16], scr[:, 4:8], -BIGF, BIGF,
                                ALU.mult, ALU.add)
                e.tensor_add(scr[:, 8:12], scr[:, 8:12], scr[:, 12:16])
                vePC4[B] = VE.one(e.tensor_copy(gidx_v[:, 4 * B:4 * B + 4], scr[:, 8:12]))
        vePCall = VE.n

        if debug and call == 0:
            @BA
            def _(e):
                e.wait_ge(VE.h, vePCall)
                for nm, src in [("p", p_v), ("eidx", eidx_v), ("lrank", lrank_v),
                                ("pscale", pscale_v), ("gidx", gidx_v), ("sidx", sidx_v)]:
                    SD.dma(e.dma_start(out=dbg[nm], in_=src[:]))
                SD.dma(e.dma_start(out=dbg["hist"], in_=agout_sb[call][:]))

        if G.get("stop_after") == "p1" and call == 0:
            return
        sctot = [(SC[0], SC[0].n), (SC[1], SC[1].n),
                 (GX[0], GX[0].n), (GX[1], GX[1].n)]
        p1s.close()
        with contextlib.ExitStack() as p2:
            E2 = p2.enter_context
            NXB = max(PSUB, 4)
            xe_sb = [E2(nc.sbuf_tensor(f"c{call}_xe{i}", [128, DM], BF16))
                     for i in range(NXB)]
            xeT_sb2 = [[E2(nc.sbuf_tensor(f"c{call}_xeT{p}_{d}", [128, ppad], BF16))
                        for d in range(DCN)] for p in range(2)]
            hT_sb = [E2(nc.sbuf_tensor(f"c{call}_hT{f}", [128, ppad], BF16))
                     for f in range(FCN)]
            w1b_sb = [E2(nc.sbuf_tensor(f"c{call}_w1b{i}", [128, DCN * 256], BF16))
                      for i in range(2)]
            w2a_sb = E2(nc.sbuf_tensor(f"c{call}_w2a", [128, FCN, DM], BF16))
            yeb_sb = [E2(nc.sbuf_tensor(f"c{call}_yeb{i}", [128, DM], BF16))
                      for i in range(2)]
            tr2_ps = E2(nc.psum_tensor(f"c{call}_tr2", [128, ppad], BF16))
            h_ps = [E2(nc.psum_tensor(f"c{call}_h{i}", [128, ppad], F32)) for i in range(2)]
            ye_ps = E2(nc.psum_tensor(f"c{call}_ye", [128, DM], F32))

            ngrp = FCN // 2
            ldXE = {}
            teXT = {}
            veXT = {}
            teH = {}
            evH = {}
            gdW1 = {}
            gdW2 = {}
            teYE = {}
            veYE = {}
            sdYE = {}
            sch = _splits(ppad, 512)

            for ex in range(NEXP):
                @blk.sync
                def _(e, ex=ex):
                    for sv, vv in sctot:
                        e.wait_ge(sv.h, vv)
                    if ex >= 1:
                        e.wait_ge(TE.h, teXT[(ex - 1, DCN - 1)])
                    for s in range(PSUB):
                        k = ex * PSUB + s
                        bb = k % NXB
                        ldXE[(ex, s)] = XE[ex % 2].dma(e.dma_start(
                            out=xe_sb[bb][:],
                            in_=sort_b[(ex * ppad + s * 128):(ex * ppad + (s + 1) * 128), :]))

                for dc in range(DCN):
                    @blk.tensor
                    def _(e, ex=ex, dc=dc):
                        if dc == 0:
                            e.wait_ge(XE[ex % 2].h, ldXE[(ex, PSUB - 1)])
                            if ex >= 2:
                                e.wait_ge(TE.h, teH[(ex - 2, 2 * ngrp - 1)])
                        if (ex, dc - 1) in veXT:
                            e.wait_ge(VE.h, veXT[(ex, dc - 1)])
                        elif ex >= 1:
                            e.wait_ge(VE.h, veXT[(ex - 1, DCN - 1)])
                        for s in range(PSUB):
                            bb = (ex * PSUB + s) % NXB
                            ins = e.transpose(tr2_ps[:, s * 128:(s + 1) * 128],
                                              xe_sb[bb][:, dc * 128:(dc + 1) * 128],
                                              idb_sb[:])
                        teXT[(ex, dc)] = TE.one(ins)

                    @BV
                    def _(e, ex=ex, dc=dc):
                        e.wait_ge(TE.h, teXT[(ex, dc)])
                        veXT[(ex, dc)] = VE.one(e.tensor_copy(
                            xeT_sb2[ex % 2][dc][:], tr2_ps[:]))


                for g in range(ngrp):
                    @blk.gpsimd
                    def _(e, ex=ex, g=g):
                        k = ex * ngrp + g
                        if k >= 2:
                            pk = k - 2
                            e.wait_ge(TE.h, teH[(pk // ngrp, 2 * (pk % ngrp) + 1)])
                        else:
                            for sv, vv in sctot:
                                e.wait_ge(sv.h, vv)
                        gdW1[(ex, g)] = W1D[k % 2].dma(e.dma_start(
                            out=w1b_sb[k % 2][:], in_=w1_d[ex, g]))
                        if g % 2 == 0 and g // 2 < FCN // 4:
                            f4 = (g // 2) * 4
                            if g == 0 and ex >= 1:
                                e.wait_ge(TE.h, teYE[(ex - 1, PSUB - 1)])
                            gdW2[(ex, f4)] = W2D.dma(e.dma_start(
                                out=w2a_sb[:, f4:f4 + 4, :],
                                in_=w2_d[ex, f4 * 128:(f4 + 4) * 128, :].rearrange(
                                    "(i p) c -> p i c", p=128)))

                    for fi in range(2):
                        fc = 2 * g + fi

                        @blk.tensor
                        def _(e, ex=ex, g=g, fi=fi, fc=fc):
                            if g == 0 and fi == 0:
                                e.wait_ge(VE.h, veXT[(ex, DCN - 1)])
                            prev = (ex, fc - 2)
                            if prev in evH:
                                eng, cnt = evH[prev]
                                e.wait_ge(eng.h, cnt)
                            elif ex >= 1:
                                eng, cnt = evH[(ex - 1, 2 * ngrp - 2 + fi)]
                                e.wait_ge(eng.h, cnt)
                            kb = (ex * ngrp + g) % 2
                            if fi == 0:
                                e.wait_ge(W1D[kb].h, gdW1[(ex, g)])
                            for dc in range(DCN):
                                for (so, sw) in sch:
                                    ins = e.matmul(h_ps[fc % 2][:, so:so + sw],
                                                   w1b_sb[kb][:, dc * 256 + fi * 128:dc * 256 + (fi + 1) * 128],
                                                   xeT_sb2[ex % 2][dc][:, so:so + sw],
                                                   start=(dc == 0), stop=(dc == DCN - 1))
                            teH[(ex, fc)] = TE.one(ins)

                        @BV
                        def _(e, ex=ex, fc=fc):
                            e.wait_ge(TE.h, teH[(ex, fc)])
                            evH[(ex, fc)] = (VE, VE.one(
                                e.tensor_scalar_max(hT_sb[fc][:], h_ps[fc % 2][:], 0.0)))

                for s in range(PSUB):
                    @blk.tensor
                    def _(e, ex=ex, s=s):
                        if s == 0:
                            eng, cnt = evH[(ex, 2 * ngrp - 1)]
                            e.wait_ge(eng.h, cnt)
                            e.wait_ge(W2D.h, gdW2[(ex, FCN - 4)])
                            e.wait_ge(W1D[0].h, W1D[0].n if False else gdW1[(ex, 0)])
                        if (ex, s - 1) in veYE:
                            e.wait_ge(VE.h, veYE[(ex, s - 1)])
                        elif ex >= 1:
                            e.wait_ge(VE.h, veYE[(ex - 1, PSUB - 1)])
                        for fc in range(FCN):
                            for (do, dw) in _splits(DM, 512):
                                ins = e.matmul(ye_ps[:, do:do + dw],
                                               hT_sb[fc][:, s * 128:(s + 1) * 128],
                                               w2a_sb[:, fc, do:do + dw],
                                               start=(fc == 0), stop=(fc == FCN - 1))
                        teYE[(ex, s)] = TE.one(ins)

                    @BV
                    def _(e, ex=ex, s=s):
                        e.wait_ge(TE.h, teYE[(ex, s)])
                        k = ex * PSUB + s
                        if k >= 2:
                            e.wait_ge(SDY[k % 2].h, sdYE[k - 2])
                        veYE[(ex, s)] = VE.one(e.tensor_copy(yeb_sb[k % 2][:], ye_ps[:]))

                    @BA
                    def _(e, ex=ex, s=s):
                        e.wait_ge(VE.h, veYE[(ex, s)])
                        k = ex * PSUB + s
                        sdYE[k] = SDY[k % 2].dma(e.dma_start(
                            out=yeout_b[(ex * ppad + s * 128):(ex * ppad + (s + 1) * 128), :],
                            in_=yeb_sb[k % 2][:]))

            yetot = [(SDY[0], SDY[0].n), (SDY[1], SDY[1].n)]

        if G.get("stop_after") == "p2" and call == 0:
            return
        veMS = [0] * nt
        gdGA = [0] * nt
        ldXR = [0] * nt
        veAD = [0] * nt
        sdST = [0] * nt
        for j in range(nt):
            b = j % 2

            @BV
            def _(e, j=j, b=b):
                e.wait_ge(VE.h, vePCall)
                if j >= 2:
                    e.wait_ge(VE.h, veAD[j - 2])
                veMS[j] = VE.one(e.memset(ybuf_sb[b][:], 0.0))

            @blk.gpsimd
            def _(e, j=j, b=b):
                e.wait_ge(VE.h, veMS[j])
                for sv, vv in yetot:
                    e.wait_ge(sv.h, vv)
                gdGA[j] = GA[b].dma(e.indirect_dma_start(
                    out=ybuf_sb[b][:], out_offset=None, in_=yeout_b,
                    in_offset=IndirectOffsetOnAxis(gidx_v[:, j:j + 1], 0),
                    bounds_check=bnd_reg, oob_is_err=False))

            @blk.sync
            def _(e, j=j, b=b):
                if j >= 2:
                    e.wait_ge(VE.h, veAD[j - 2])
                if call == 0:
                    e.wait_ge(GX[0].h, GX[0].n)
                    e.wait_ge(GX[1].h, GX[1].n)
                ldXR[j] = LDX[b].dma(e.dma_start(out=xr_sb[b][:],
                                                 in_=xres_b[j * 128:(j + 1) * 128, :]))

            @BV
            def _(e, j=j, b=b):
                e.wait_ge(GA[b].h, gdGA[j])
                e.wait_ge(LDX[b].h, ldXR[j])
                if j >= 2:
                    e.wait_ge(SDS[b].h, sdST[j - 2])
                veAD[j] = VE.one(e.scalar_tensor_tensor(
                    out_sb[b][:], ybuf_sb[b][:], pscale_v[:, j:j + 1], xr_sb[b][:],
                    ALU.mult, ALU.add))

            @BA
            def _(e, j=j, b=b):
                e.wait_ge(VE.h, veAD[j])
                dst = o_d if final else xres_b
                sdST[j] = SDS[b].dma(e.dma_start(out=dst[j * 128:(j + 1) * 128, :],
                                                 in_=out_sb[b][:]))

        if not final:
            G["sd_x1_done"] = [(SDS[0], SDS[0].n), (SDS[1], SDS[1].n)]
            if debug:
                emit_dump(G, xres_b, dbg["x1"], T, G["sd_x1_done"])


def emit_attention(G):
    nc, blk = G["nc"], G["blk"]
    nt, T, S, ST, QCH, DCN = G["nt"], G["T"], G["S"], G["ST"], G["QCH"], G["DCN"]
    LD, GD, SD, TE, VE, AE = (G[k] for k in ["LD", "GD", "SD", "TE", "VE", "AE"])
    LDX, SDS = G["LDX"], G["SDS"]
    BV, BA = G["BV"], G["BA"]
    xres_b, zsum_b = G["xres_b"], G["zsum_b"]
    idb_sb, ones_sb, onesb_sb, wqkv_sb, bout_sb = (
        G[k] for k in ["idb_sb", "ones_sb", "onesb_sb", "wqkv_sb", "bout_sb"])
    woutT_d = G["woutT_d"]
    scr = G["scr"]
    dbg, debug = G["dbg"], G["debug"]
    x1_done = G["sd_x1_done"]

    with contextlib.ExitStack() as ph:
        E = ph.enter_context
        xlnT_sb = [E(nc.sbuf_tensor(f"a_xlnT{d}", [128, S], BF16)) for d in range(DCN)]
        attnT_sb = [E(nc.sbuf_tensor(f"a_attnT{d}", [128, S], BF16)) for d in range(DCN)]
        wout_sb = [E(nc.sbuf_tensor(f"a_wout{d}", [128, DM], BF16)) for d in range(DCN)]
        qt_sb = E(nc.sbuf_tensor("a_qt", [128, S], BF16))
        kt_sb = E(nc.sbuf_tensor("a_kt", [128, S], BF16))
        vx_sb = E(nc.sbuf_tensor("a_vx", [128, ST, 65], BF16))
        pb_sb = [E(nc.sbuf_tensor(f"a_pb{i}", [128, 1024], BF16)) for i in range(6)]
        xt_sb = [E(nc.sbuf_tensor(f"a_xt{i}", [128, DM], BF16)) for i in range(2)]
        xlnb_sb = [E(nc.sbuf_tensor(f"a_xlnb{i}", [128, DM], BF16)) for i in range(2)]
        zrow_sb = E(nc.sbuf_tensor("a_zrow", [1, (8 if NH % 8 == 0 else NH) * S], BF16))
        rrow_sb = E(nc.sbuf_tensor("a_rrow", [1, (8 if NH % 8 == 0 else NH) * S], BF16))
        zT_sb = E(nc.sbuf_tensor("a_zT", [128, (8 if NH % 8 == 0 else NH) * ST], BF16))
        rT_sb = E(nc.sbuf_tensor("a_rT", [128, (8 if NH % 8 == 0 else NH) * ST], F32))
        rTb_sb = E(nc.sbuf_tensor("a_rTb", [128, (8 if NH % 8 == 0 else NH) * ST], BF16))
        xo_sb = [E(nc.sbuf_tensor(f"a_xo{i}", [128, DM], BF16)) for i in range(2)]

        gd_prior = GD.n

        @blk.gpsimd
        def _(e):
            e.wait_ge(GD.h, gd_prior)
            for sv, vv in x1_done:
                e.wait_ge(sv.h, vv)
            for d in range(DCN):
                GD.dma(e.dma_start(out=wout_sb[d][:], in_=woutT_d[d * 128:(d + 1) * 128, :]))
        gdWo = GD.n

        prev_batch_ve = G["init_ve"]
        sdXO_prev = {0: 0, 1: 0}
        for b in range(2):
            base = b * ST

            with contextlib.ExitStack() as pa:
                trb_ps = pa.enter_context(nc.psum_tensor(f"a{b}_trb", [128, 1024], BF16))
                ldX = [0] * ST
                aeS2 = [0] * ST
                veL2 = [0] * ST
                teT2 = [0] * ST
                veE2 = [0] * ST
                for t in range(ST):
                    j = base + t
                    bb = t % 2
                    sc = 32 * bb

                    @blk.sync
                    def _(e, t=t, j=j, bb=bb):
                        for sv, vv in x1_done:
                            e.wait_ge(sv.h, vv)
                        e.wait_ge(VE.h, prev_batch_ve)
                        if t >= 2:
                            e.wait_ge(VE.h, veL2[t - 2])
                        ldX[t] = LDX[bb].dma(e.dma_start(out=xt_sb[bb][:],
                                                         in_=xres_b[j * 128:(j + 1) * 128, :]))

                    @BV
                    def _(e, t=t, bb=bb, sc=sc):
                        e.wait_ge(LDX[bb].h, ldX[t])
                        e.bn_stats(scr[:, sc + 0:sc + 6], xt_sb[bb][:, 0:512])
                        e.bn_stats(scr[:, sc + 6:sc + 12], xt_sb[bb][:, 512:1024])
                        e.bn_aggr(scr[:, sc + 12:sc + 14], scr[:, sc + 0:sc + 12])
                        VE.one(e.tensor_scalar_add(scr[:, sc + 14:sc + 15],
                                                   scr[:, sc + 13:sc + 14], 1e-5))
                    vS = VE.n

                    @BA
                    def _(e, t=t, sc=sc, vS=vS):
                        e.wait_ge(VE.h, vS)
                        aeS2[t] = AE.one(e.activation(scr[:, sc + 15:sc + 16],
                                                      scr[:, sc + 14:sc + 15], AF.Sqrt))

                    @BV
                    def _(e, t=t, bb=bb, sc=sc):
                        e.wait_ge(AE.h, aeS2[t])
                        if t >= 2:
                            e.wait_ge(TE.h, teT2[t - 2])
                        e.reciprocal(scr[:, sc + 16:sc + 17], scr[:, sc + 15:sc + 16])
                        veL2[t] = VE.one(e.tensor_scalar(
                            xlnb_sb[bb][:], xt_sb[bb][:], scr[:, sc + 12:sc + 13],
                            scr[:, sc + 16:sc + 17], ALU.subtract, ALU.mult))

                    @blk.tensor
                    def _(e, t=t, bb=bb):
                        e.wait_ge(VE.h, veL2[t])
                        if t >= 1:
                            e.wait_ge(VE.h, veE2[t - 1])
                        for dc in range(DCN):
                            ins = e.transpose(trb_ps[:, dc * 128:(dc + 1) * 128],
                                              xlnb_sb[bb][:, dc * 128:(dc + 1) * 128],
                                              idb_sb[:])
                        teT2[t] = TE.one(ins)

                    @BV
                    def _(e, t=t):
                        e.wait_ge(TE.h, teT2[t])
                        for dc in range(DCN):
                            ins = e.tensor_copy(xlnT_sb[dc][:, t * 128:(t + 1) * 128],
                                                trb_ps[:, dc * 128:(dc + 1) * 128])
                        veE2[t] = VE.one(ins)
                xlnT_done = VE.n
            if G.get("stop_after") == "aln":
                return

            HPG = 8 if NH % 8 == 0 else NH
            NHG = NH // HPG
            pvEv = {}
            nmLast = xlnT_done
            veOnes_done = False
            for hg in range(NHG):
                with contextlib.ExitStack() as pb_:
                    Eb = pb_.enter_context
                    if not veOnes_done:
                        @BV
                        def _(e):
                            e.wait_ge(VE.h, xlnT_done)
                            e.wait_ge(GD.h, gdWo)
                            e.memset(vx_sb[:, :, 64:65], 1.0)
                            e.memset(qt_sb[64:128, :], 0.0)
                            VE.one(e.memset(kt_sb[64:128, :], 0.0))
                        veOnes = VE.n
                        veOnes_done = True

                    for h in range(hg * HPG, (hg + 1) * HPG):
                        p64 = (h % 2) * 64
                        ct = h // 2
                        hh = h % HPG

                        with contextlib.ExitStack() as pq:
                            qv_ps = [pq.enter_context(
                                nc.psum_tensor(f"a{b}h{h}_qv{i}", [128, 512], F32))
                                for i in range(2)]
                            teQ = []
                            veQ = []
                            qsteps = []
                            for (qo, qw) in _splits(S, 512):
                                qsteps.append(("q", qo, qw))
                                qsteps.append(("k", qo, qw))
                            for t in range(ST):
                                qsteps.append(("v", t, 0))
                            for qi, (kind, qo, qw) in enumerate(qsteps):
                                @blk.tensor
                                def _(e, h=h, p64=p64, ct=ct, kind=kind, qo=qo,
                                      qw=qw, qi=qi, nmLast=nmLast):
                                    if qi == 0:
                                        if h % HPG == 0:
                                            e.wait_ge(VE.h, nmLast)
                                        if h == 0:
                                            e.wait_ge(VE.h, veOnes)
                                        if h > 0:
                                            e.wait_ge(TE.h, G["a_pvdone"][(b, h - 1)])
                                            e.wait_ge(VE.h, G["a_pvev_last"][(b, h - 1)])
                                    if qi >= 2:
                                        e.wait_ge(VE.h, veQ[qi - 2])
                                    buf = qv_ps[qi % 2]
                                    zb = 192 * ((h % 2))
                                    if kind == "q":
                                        ins = e.matmul(buf[0:64, 0:qw],
                                                       wqkv_sb[:, zb + 0:zb + 64],
                                                       xlnT_sb[ct][:, qo:qo + qw],
                                                       start=True, stop=True)
                                    elif kind == "k":
                                        ins = e.matmul(buf[0:64, 0:qw],
                                                       wqkv_sb[:, zb + 64:zb + 128],
                                                       xlnT_sb[ct][:, qo:qo + qw],
                                                       start=True, stop=True)
                                    else:
                                        t = qo
                                        ins = e.matmul(buf[:, 0:64],
                                                       xlnT_sb[ct][:, t * 128:(t + 1) * 128],
                                                       wqkv_sb[:, zb + 128:zb + 192],
                                                       start=True, stop=True)
                                    teQ.append(TE.one(ins))

                                @BV
                                def _(e, kind=kind, qo=qo, qw=qw, qi=qi):
                                    e.wait_ge(TE.h, teQ[qi])
                                    buf = qv_ps[qi % 2]
                                    if kind == "q":
                                        ins = e.tensor_scalar_mul(
                                            qt_sb[0:64, qo:qo + qw], buf[0:64, 0:qw], 0.03125)
                                    elif kind == "k":
                                        ins = e.tensor_copy(kt_sb[0:64, qo:qo + qw],
                                                            buf[0:64, 0:qw])
                                    else:
                                        ins = e.tensor_copy(vx_sb[:, qo, 0:64],
                                                            buf[:, 0:64])
                                    veQ.append(VE.one(ins))
                            qkvDone = veQ[-1]

                        with contextlib.ExitStack() as psl:
                            s_ps = [psl.enter_context(
                                nc.psum_tensor(f"a{b}h{h}_s{i}", [128, 1024], F32))
                                for i in range(3)]
                            pv_ps = psl.enter_context(
                                nc.psum_tensor(f"a{b}h{h}_pv", [128, 1024], F32))

                            for ci, (qo, qw) in enumerate(QCH):
                                teS = [0] * ST
                                tePV = [0] * ST
                                aeP = [AE.n + kt + 1 for kt in range(ST)]

                                def emit_pv(ktp, qw=qw):
                                    @blk.tensor
                                    def _(e, ktp=ktp, qw=qw):
                                        e.wait_ge(AE.h, aeP[ktp])
                                        for (o2, w2) in _splits(qw, 512):
                                            ins = e.matmul(pv_ps[0:65, o2:o2 + w2],
                                                           vx_sb[:, ktp, :],
                                                           pb_sb[ktp % 6][:, o2:o2 + w2],
                                                           start=(ktp == 0),
                                                           stop=(ktp == ST - 1))
                                        tePV[ktp] = TE.one(ins)

                                for kt in range(ST):
                                    @blk.tensor
                                    def _(e, kt=kt, qo=qo, qw=qw, ci=ci, h=h,
                                          qkvDone=qkvDone):
                                        if kt == 0:
                                            e.wait_ge(VE.h, qkvDone)
                                            if ci > 0:
                                                e.wait_ge(VE.h, pvEv[(b, h, ci - 1)])
                                        if kt >= 3:
                                            e.wait_ge(AE.h, aeP[kt - 3])
                                        for (o2, w2) in _splits(qw, 512):
                                            ins = e.matmul(s_ps[kt % 3][:, o2:o2 + w2],
                                                           kt_sb[:, kt * 128:(kt + 1) * 128],
                                                           qt_sb[:, qo + o2:qo + o2 + w2],
                                                           start=True, stop=True)
                                        teS[kt] = TE.one(ins)

                                    if kt >= 2:
                                        emit_pv(kt - 2)
                                emit_pv(ST - 2)
                                emit_pv(ST - 1)

                                @BA
                                def _(e, qw=qw, teS=teS, tePV=tePV):
                                    for kt in range(ST):
                                        e.wait_ge(TE.h, teS[kt])
                                        if kt >= 6:
                                            e.wait_ge(TE.h, tePV[kt - 6])
                                        e.activation(pb_sb[kt % 6][:, 0:qw],
                                                     s_ps[kt % 3][:, 0:qw], AF.Exp)
                                assert AE.n == aeP[ST - 1], (AE.n, aeP[ST - 1])
                                G.setdefault("a_pvdone", {})[(b, h)] = tePV[ST - 1]

                                @BV
                                def _(e, h=h, ct=ct, p64=p64, qo=qo, qw=qw, ci=ci,
                                      hh=hh, last=tePV[ST - 1]):
                                    e.wait_ge(TE.h, last)
                                    e.tensor_copy(attnT_sb[ct][p64:p64 + 64, qo:qo + qw],
                                                  pv_ps[0:64, 0:qw])
                                    pvEv[(b, h, ci)] = VE.one(e.tensor_copy(
                                        zrow_sb[:, hh * S + qo:hh * S + qo + qw],
                                        pv_ps[64:65, 0:qw]))
                            G.setdefault("a_pvev_last", {})[(b, h)] = pvEv[(b, h, len(QCH) - 1)]
                    zEvAll = VE.n

                with contextlib.ExitStack() as pn:
                    En = pn.enter_context
                    rb_ps = [En(nc.psum_tensor(f"a{b}g{hg}_rb{i}", [1, 128], BF16))
                             for i in range(2)]
                    bc_ps = En(nc.psum_tensor(f"a{b}g{hg}_bc", [64, 1024], F32))
                    zreg = zsum_b[:, hg * HPG * S:(hg + 1) * HPG * S]

                    @blk.gpsimd
                    def _(e, zreg=zreg):
                        e.wait_ge(VE.h, zEvAll)
                        e.wait_ge(LD.h, LD.n)
                        GD.dma(e.dma_start(out=zreg, in_=zrow_sb[:]))
                    gdZ = GD.n

                    @blk.sync
                    def _(e, zreg=zreg):
                        e.wait_ge(GD.h, gdZ)
                        LD.dma(e.dma_start(
                            out=zT_sb[:],
                            in_=zreg.rearrange("one (r c) -> (one r) c", c=128),
                            transpose=True))
                    ldZ = LD.n

                    @BV
                    def _(e):
                        e.wait_ge(LD.h, ldZ)
                        e.reciprocal(rT_sb[:], zT_sb[:])
                        VE.one(e.tensor_copy(rTb_sb[:], rT_sb[:]))
                    veR = VE.n

                    teRT = []
                    veRT = []
                    for hh in range(HPG):
                        for t in range(ST):
                            col = hh * ST + t

                            @blk.tensor
                            def _(e, col=col, idx=len(teRT)):
                                if idx == 0:
                                    e.wait_ge(VE.h, veR)
                                if idx >= 2:
                                    e.wait_ge(VE.h, veRT[idx - 2])
                                teRT.append(TE.one(e.transpose(
                                    rb_ps[idx % 2][0:1, 0:128],
                                    rTb_sb[:, col:col + 1], idb_sb[:])))

                            @BV
                            def _(e, hh=hh, t=t, idx=len(veRT)):
                                e.wait_ge(TE.h, teRT[idx])
                                veRT.append(VE.one(e.tensor_copy(
                                    rrow_sb[:, hh * S + t * 128:hh * S + (t + 1) * 128],
                                    rb_ps[idx % 2][0:1, 0:128])))
                    rrowDone = veRT[-1]

                    teBCn = []
                    veNM = []
                    for h in range(hg * HPG, (hg + 1) * HPG):
                        p64 = (h % 2) * 64
                        ct = h // 2
                        hh = h % HPG
                        for (qo, qw) in _splits(S, 1024):
                            @blk.tensor
                            def _(e, hh=hh, qo=qo, qw=qw, idx=len(teBCn)):
                                e.wait_ge(VE.h, rrowDone)
                                if idx >= 1:
                                    e.wait_ge(VE.h, veNM[idx - 1])
                                for (o2, w2) in _splits(qw, 512):
                                    ins = e.matmul(
                                        bc_ps[:, o2:o2 + w2], onesb_sb[0:1, 0:64],
                                        rrow_sb[:, hh * S + qo + o2:hh * S + qo + o2 + w2],
                                        start=True, stop=True)
                                teBCn.append(TE.one(ins))

                            @BV
                            def _(e, ct=ct, p64=p64, qo=qo, qw=qw, idx=len(veNM)):
                                e.wait_ge(TE.h, teBCn[idx])
                                veNM.append(VE.one(e.tensor_mul(
                                    attnT_sb[ct][p64:p64 + 64, qo:qo + qw],
                                    attnT_sb[ct][p64:p64 + 64, qo:qo + qw],
                                    bc_ps[:, 0:qw])))
                    nmLast = veNM[-1]
            nmDone = nmLast

            with contextlib.ExitStack() as po:
                op_ps = po.enter_context(nc.psum_tensor(f"a{b}_op", [128, 1024], F32))
                ldXO = [0] * ST
                teOP = [0] * ST
                veAO = [0] * ST
                sdXO = [0] * ST
                for t in range(ST):
                    j = base + t
                    bb = t % 2

                    @blk.sync
                    def _(e, t=t, j=j, bb=bb):
                        e.wait_ge(VE.h, nmDone)
                        if t >= 2:
                            e.wait_ge(VE.h, veAO[t - 2])
                        ldXO[t] = LDX[bb].dma(e.dma_start(out=xt_sb[bb][:],
                                                          in_=xres_b[j * 128:(j + 1) * 128, :]))

                    @blk.tensor
                    def _(e, t=t, bb=bb):
                        e.wait_ge(VE.h, nmDone)
                        e.wait_ge(GD.h, gdWo)
                        if t >= 1:
                            e.wait_ge(VE.h, veAO[t - 1])
                        for (do, dw) in _splits(DM, 512):
                            for dc in range(DCN):
                                e.matmul(op_ps[:, do:do + dw],
                                         attnT_sb[dc][:, t * 128:(t + 1) * 128],
                                         wout_sb[dc][:, do:do + dw],
                                         start=(dc == 0), stop=False)
                            ins = e.matmul(op_ps[:, do:do + dw], onesb_sb[:],
                                           bout_sb[:, do:do + dw],
                                           start=False, stop=True)
                        teOP[t] = TE.one(ins)

                    @BV
                    def _(e, t=t, bb=bb):
                        e.wait_ge(TE.h, teOP[t])
                        e.wait_ge(LDX[bb].h, ldXO[t])
                        e.wait_ge(SDS[bb].h, sdXO[t - 2] if t >= 2 else sdXO_prev[bb])
                        veAO[t] = VE.one(e.tensor_add(xo_sb[bb][:], xt_sb[bb][:], op_ps[:]))

                    @BA
                    def _(e, t=t, j=j, bb=bb):
                        e.wait_ge(VE.h, veAO[t])
                        sdXO[t] = SDS[bb].dma(e.dma_start(out=xres_b[j * 128:(j + 1) * 128, :],
                                                          in_=xo_sb[bb][:]))

            prev_batch_ve = VE.n
            for bb2 in range(2):
                last = [sdXO[t2] for t2 in range(ST) if t2 % 2 == bb2]
                if last:
                    sdXO_prev[bb2] = last[-1]
        G["sd_xres_done"] = [(SDS[0], SDS[0].n), (SDS[1], SDS[1].n)]
        if debug:
            emit_dump(G, xres_b, dbg["x2"], T, G["sd_xres_done"])


def kernel(**inputs):
    x = np.asarray(inputs["L0_attn_x"], np.float32)
    Wq = np.asarray(inputs["L0_attn_Wq"], np.float32)
    Wk = np.asarray(inputs["L0_attn_Wk"], np.float32)
    Wv = np.asarray(inputs["L0_attn_Wv"], np.float32)
    Wout = np.asarray(inputs["L0_attn_W_out"], np.float32)
    bout = np.asarray(inputs["L0_attn_b_out"], np.float32)
    Wr = np.asarray(inputs["L0_ffn_Wr"], np.float32)
    W1 = np.asarray(inputs["L0_ffn_W1"], np.float32)
    W2 = np.asarray(inputs["L0_ffn_W2"], np.float32)

    nt = NT
    T = 128 * nt
    nc = build(nt=nt, ppad=PPAD, debug=False)
    in_maps = make_inputs(x, Wq, Wk, Wv, Wout, bout, Wr, W1, W2, nt)
    res = run_bass_kernel_spmd(nc, in_maps, core_ids=list(range(NCORE)))
    out = np.stack([r["o"].reshape(2, T // 2, DM) for r in res.results], axis=0)
    return np.ascontiguousarray(out.reshape(16, T // 2, DM)).astype(np.float32)


def make_inputs(x, Wq, Wk, Wv, Wout, bout, Wr, W1, W2, nt):
    T = 128 * nt
    w3 = np.concatenate([Wq.T, Wk.T, Wv.T], axis=1)
    z = np.zeros_like(w3)
    wqkv = np.stack([np.concatenate([w3, z], axis=0),
                     np.concatenate([z, w3], axis=0)])
    ngrp = FF // 256
    w1p = np.ascontiguousarray(
        W1.reshape(NEXP, DM // 128, 128, ngrp, 256).transpose(0, 3, 2, 1, 4)
    ).reshape(NEXP, ngrp, 128, (DM // 128) * 256)
    in_common = dict(
        wr=np.ascontiguousarray(Wr),
        w1=w1p,
        w2=np.ascontiguousarray(W2),
        wqkv=np.ascontiguousarray(wqkv),
        woutT=np.ascontiguousarray(Wout.T),
        bout=np.ascontiguousarray(bout.reshape(1, DM)),
        iota8=np.tile(np.arange(NEXP, dtype=np.float32), (128, 1)),
        tri=(np.arange(128)[:, None] < np.arange(128)[None, :]).astype(np.float32),
        idf=np.eye(128, dtype=np.float32),
        idb=np.eye(128).astype(ml_dtypes.bfloat16),
        onesin=np.ones((128, 128), np.float32),
    )
    in_maps = []
    for c in range(NCORE):
        m = dict(in_common)
        m["x"] = np.ascontiguousarray(x[2 * c:2 * c + 2].reshape(T, DM))
        m["pmask"] = (np.arange(NCORE) < c).astype(np.float32).reshape(NCORE, 1)
        in_maps.append(m)
    return in_maps


# version 08
# speedup vs baseline: 1.8954x; 1.0476x over previous
import contextlib
import numpy as np
import ml_dtypes
import concourse.bass as bass
import concourse.mybir as mybir
from concourse.bass import IndirectOffsetOnAxis
from concourse.bass_utils import run_bass_kernel_spmd

F32 = mybir.dt.float32
BF16 = mybir.dt.bfloat16
U32 = mybir.dt.uint32
AF = mybir.ActivationFunctionType
ALU = mybir.AluOpType
AX = mybir.AxisListType

NCORE = 8
DM = 1024
NH = 16
DH = 64
NEXP = 8
FF = 4096
BIGF = 1.0e7

NT = 32
PPAD = 768
DEBUG = False

RG = [list(range(NCORE))]


def _splits(total, step=512):
    out, o = [], 0
    while o < total:
        out.append((o, min(step, total - o)))
        o += step
    return out


class Sem:
    def __init__(self, nc, es, name):
        self.h = es.enter_context(nc.semaphore(name))
        self.n = 0
        self.last_chained = None

    def dma(self, instr):
        instr.then_inc(self.h, 16)
        self.n += 16
        return self.n

    def one(self, instr):
        if instr is self.last_chained:
            return self.n
        instr.then_inc(self.h, 1)
        self.n += 1
        return self.n


class Chain:
    PASSTHRU = ("wait_ge", "dma_start", "indirect_dma_start",
                "collective_compute", "engine_nop")

    def __init__(self, e, sem):
        self._e = e
        self._s = sem

    def __getattr__(self, name):
        fn = getattr(self._e, name)
        if name in Chain.PASSTHRU or not callable(fn):
            return fn
        s = self._s
        raw = self._e

        def wrapped(*a, **k):
            raw.wait_ge(s.h, s.n)
            ins = fn(*a, **k)
            ins.then_inc(s.h, 1)
            s.n += 1
            s.last_chained = ins
            return ins
        return wrapped


def build(nt=None, ppad=None, debug=None, stop_after=None):
    nt = NT if nt is None else nt
    ppad = PPAD if ppad is None else ppad
    debug = DEBUG if debug is None else debug

    T = 128 * nt
    S = T // 2
    ST = S // 128
    CAP = T
    PSUB = ppad // 128
    NSORT = NEXP * ppad
    FCN = FF // 128
    DCN = DM // 128
    QCH = _splits(S, 1024)

    nc = bass.Bass("TRN2", target_bir_lowering=False, debug=False)

    x_d = nc.dram_tensor("x", [T, DM], F32, kind="ExternalInput").ap()
    wr_d = nc.dram_tensor("wr", [DM, NEXP], F32, kind="ExternalInput").ap()
    w1_d = nc.dram_tensor("w1", [NEXP, FF // 256, 128, (DM // 128) * 256], F32, kind="ExternalInput").ap()
    w2_d = nc.dram_tensor("w2", [NEXP, FF, DM], F32, kind="ExternalInput").ap()
    wqkv_d = nc.dram_tensor("wqkv", [2, 128, 3 * DH], F32, kind="ExternalInput").ap()
    woutT_d = nc.dram_tensor("woutT", [DM, DM], F32, kind="ExternalInput").ap()
    bout_d = nc.dram_tensor("bout", [1, DM], F32, kind="ExternalInput").ap()
    iota8_d = nc.dram_tensor("iota8", [128, NEXP], F32, kind="ExternalInput").ap()
    tri_d = nc.dram_tensor("tri", [128, 128], F32, kind="ExternalInput").ap()
    idf_d = nc.dram_tensor("idf", [128, 128], F32, kind="ExternalInput").ap()
    idb_d = nc.dram_tensor("idb", [128, 128], BF16, kind="ExternalInput").ap()
    pmask_d = nc.dram_tensor("pmask", [NCORE, 1], F32, kind="ExternalInput").ap()
    ones_d = nc.dram_tensor("onesin", [128, 128], F32, kind="ExternalInput").ap()

    o_d = nc.dram_tensor("o", [T, DM], F32, kind="ExternalOutput").ap()
    dbg = {}
    if debug:
        for nm, shp, dt in [("eidx", [128, nt], F32), ("p", [128, nt], F32),
                            ("lrank", [128, nt], F32), ("pscale", [128, nt], F32),
                            ("gidx", [128, nt], U32), ("sidx", [128, nt], U32),
                            ("hist", [NCORE, NEXP], F32),
                            ("x1", [T, DM], BF16), ("x2", [T, DM], BF16)]:
            dbg[nm] = nc.dram_tensor("dbg_" + nm, shp, dt, kind="ExternalOutput").ap()

    xres_b = nc.dram_tensor("xres_b", [T, DM], BF16).ap()
    sort_b = nc.dram_tensor("sort_b", [NSORT, DM], BF16).ap()
    yeout_b = nc.dram_tensor("yeout_b", [NSORT, DM], BF16).ap()
    zsum_b = nc.dram_tensor("zsum_b", [1, NH * S], BF16).ap()
    ag_in = [nc.dram_tensor(f"ag_in{k}", [1, NEXP], F32).ap() for k in range(2)]
    ag_out = [nc.dram_tensor(f"ag_out{k}", [NCORE, NEXP], F32).ap() for k in range(2)]

    es = contextlib.ExitStack()
    with es:
        E = es.enter_context
        cpool = E(nc.sbuf_tensor("cpool", [128, 8], F32))
        iota8_sb = E(nc.sbuf_tensor("iota8_sb", [128, NEXP], F32))
        tri_sb = E(nc.sbuf_tensor("tri_sb", [128, 128], F32))
        idf_sb = E(nc.sbuf_tensor("idf_sb", [128, 128], F32))
        idb_sb = E(nc.sbuf_tensor("idb_sb", [128, 128], BF16))
        ones_sb = E(nc.sbuf_tensor("ones_sb", [128, 128], F32))
        onesb_sb = E(nc.sbuf_tensor("onesb_sb", [1, 128], BF16))
        pmask_sb = E(nc.sbuf_tensor("pmask_sb", [NCORE, 1], F32))
        wr_sb = E(nc.sbuf_tensor("wr_sb", [128, DCN * NEXP], F32))
        wqkv_sb = E(nc.sbuf_tensor("wqkv_sb", [128, 2 * 3 * DH], BF16))
        bout_sb = E(nc.sbuf_tensor("bout_sb", [1, DM], BF16))
        agout_sb = [E(nc.sbuf_tensor(f"agout_sb{k}", [NCORE, NEXP], F32)) for k in range(2)]
        p_v = E(nc.sbuf_tensor("p_v", [128, nt], F32))
        eidx_v = E(nc.sbuf_tensor("eidx_v", [128, nt], F32))
        lrank_v = E(nc.sbuf_tensor("lrank_v", [128, nt], F32))
        pscale_v = E(nc.sbuf_tensor("pscale_v", [128, nt], F32))
        sidx_v = E(nc.sbuf_tensor("sidx_v", [128, nt], U32))
        gidx_v = E(nc.sbuf_tensor("gidx_v", [128, nt], U32))
        oh_v = E(nc.sbuf_tensor("oh_v", [128, NEXP * nt], F32))
        lgall = E(nc.sbuf_tensor("lgall", [128, NEXP * nt], F32))
        exall = E(nc.sbuf_tensor("exall", [128, NEXP * nt], F32))
        mx8all = E(nc.sbuf_tensor("mx8all", [128, NEXP * nt], F32))
        pref_v = E(nc.sbuf_tensor("pref_v", [128, NEXP * nt], F32))
        totrow = E(nc.sbuf_tensor("totrow", [1, NEXP * nt], F32))
        runrow = E(nc.sbuf_tensor("runrow", [1, NEXP * (nt + 1)], F32))
        combrow = E(nc.sbuf_tensor("combrow", [1, NEXP * nt], F32))
        offs_row = E(nc.sbuf_tensor("offs_row", [1, NEXP], F32))
        hist_row = E(nc.sbuf_tensor("hist_row", [1, NEXP], F32))
        scr = E(nc.sbuf_tensor("scr", [128, 128], F32))
        scru = E(nc.sbuf_tensor("scru", [128, 16], U32))
        dump_sb = E(nc.sbuf_tensor("dump_sb", [128, DM], BF16)) if debug else None

        LD = Sem(nc, es, "LD")
        GD = Sem(nc, es, "GD")
        SD = Sem(nc, es, "SD")
        CC = Sem(nc, es, "CC")
        TE = Sem(nc, es, "TE")
        VE = Sem(nc, es, "VE")
        AE = Sem(nc, es, "AE")
        LDX = [Sem(nc, es, f"LDX{i}") for i in range(4)]
        XE = [Sem(nc, es, f"XE{i}") for i in range(2)]
        SC = [Sem(nc, es, f"SC{i}") for i in range(2)]
        GX = [Sem(nc, es, f"GX{i}") for i in range(2)]
        GA = [Sem(nc, es, f"GA{i}") for i in range(2)]
        SDY = [Sem(nc, es, f"SDY{i}") for i in range(2)]
        SDS = [Sem(nc, es, f"SDS{i}") for i in range(2)]
        W1D = [Sem(nc, es, f"W1D{i}") for i in range(2)]
        W2D = Sem(nc, es, "W2D")

        for cv, ci in [(0.0, 0), (1.0, 1), (0.5, 2), (0.03125, 3), (-0.5, 4)]:
            nc.const_aps.register(F32, cv, cpool[:, ci:ci + 1])

        blk = E(nc.Block())

        def BV(f):
            blk.vector(lambda e: f(Chain(e, VE)))

        def BA(f):
            blk.scalar(lambda e: f(Chain(e, AE)))

        @blk.sync
        def _(e):
            for dst, src in [(iota8_sb, iota8_d), (tri_sb, tri_d), (idf_sb, idf_d),
                             (idb_sb, idb_d), (ones_sb, ones_d), (pmask_sb, pmask_d)]:
                LD.dma(e.dma_start(out=dst[:], in_=src[:]))
            for dc in range(DCN):
                LD.dma(e.dma_start(out=wr_sb[:, dc * NEXP:(dc + 1) * NEXP],
                                   in_=wr_d[dc * 128:(dc + 1) * 128, :]))
        init_ld = LD.n

        bnd_reg = nc.gpsimd.alloc_register("bnd")

        @blk.gpsimd
        def _(e):
            e.reg_mov(bnd_reg, NSORT - 1)
            GD.dma(e.dma_start(out=wqkv_sb[:, 0:192], in_=wqkv_d[0]))
            GD.dma(e.dma_start(out=wqkv_sb[:, 192:384], in_=wqkv_d[1]))
            GD.dma(e.dma_start(out=bout_sb[:], in_=bout_d[:]))
        init_gd = GD.n

        @BV
        def _(e):
            for cv, ci in [(0.0, 0), (1.0, 1), (0.5, 2), (0.03125, 3), (-0.5, 4)]:
                e.memset(cpool[:, ci:ci + 1], cv)
            e.memset(runrow[:, 0:NEXP], 0.0)
            e.wait_ge(LD.h, init_ld)
            VE.one(e.tensor_copy(onesb_sb[:], ones_sb[0:1, :]))
        init_ve = VE.n

        G = dict(nc=nc, blk=blk, nt=nt, ppad=ppad, debug=debug, dbg=dbg,
                 T=T, S=S, ST=ST, CAP=CAP, PSUB=PSUB, NSORT=NSORT, FCN=FCN,
                 DCN=DCN, QCH=QCH,
                 x_d=x_d, o_d=o_d, w1_d=w1_d, w2_d=w2_d, woutT_d=woutT_d,
                 xres_b=xres_b, sort_b=sort_b, yeout_b=yeout_b, zsum_b=zsum_b,
                 ag_in=ag_in, ag_out=ag_out, agout_sb=agout_sb,
                 cpool=cpool, iota8_sb=iota8_sb, tri_sb=tri_sb, idf_sb=idf_sb,
                 idb_sb=idb_sb, ones_sb=ones_sb, onesb_sb=onesb_sb,
                 pmask_sb=pmask_sb, wr_sb=wr_sb, wqkv_sb=wqkv_sb, bout_sb=bout_sb,
                 p_v=p_v, eidx_v=eidx_v, lrank_v=lrank_v, pscale_v=pscale_v,
                 sidx_v=sidx_v, gidx_v=gidx_v, oh_v=oh_v, pref_v=pref_v,
                 totrow=totrow, runrow=runrow, combrow=combrow, offs_row=offs_row,
                 hist_row=hist_row, scr=scr, scru=scru,
                 lgall=lgall, exall=exall, mx8all=mx8all,
                 dump_sb=dump_sb,
                 LD=LD, GD=GD, SD=SD, CC=CC, TE=TE, VE=VE, AE=AE,
                 LDX=LDX, XE=XE, SC=SC, GX=GX, GA=GA, SDY=SDY, SDS=SDS,
                 W1D=W1D, W2D=W2D, BV=BV, BA=BA, bnd_reg=bnd_reg,
                 init_ld=init_ld, init_gd=init_gd, init_ve=init_ve)

        G["stop_after"] = stop_after
        emit_ffn(G, 0)
        if stop_after not in ("p1", "p2", "ffn0"):
            emit_attention(G)
            if stop_after not in ("attn", "aln", "ahead", "anorm"):
                emit_ffn(G, 1)

        @blk.gpsimd
        def _(e):
            for s in [SD, GD, LD, W2D] + W1D + LDX + XE + SC + GX + GA + SDY + SDS:
                e.wait_ge(s.h, s.n)
    return nc


def emit_dump(G, src_dram, dst_dram, nrows, waits):
    nc, blk, LD, SD = G["nc"], G["blk"], G["LD"], G["SD"]
    BA = G["BA"]
    bsb = G["dump_sb"]
    sd_pre = SD.n
    lds = []
    sds = []
    for r in range(nrows // 128):
        @blk.sync
        def _(e, r=r):
            for sv, vv in waits:
                e.wait_ge(sv.h, vv)
            e.wait_ge(SD.h, sds[r - 1] if r >= 1 else sd_pre)
            lds.append(LD.dma(e.dma_start(out=bsb[:],
                                          in_=src_dram[r * 128:(r + 1) * 128, :])))

        @BA
        def _(e, r=r):
            e.wait_ge(LD.h, lds[r])
            sds.append(SD.dma(e.dma_start(out=dst_dram[r * 128:(r + 1) * 128, :],
                                          in_=bsb[:])))


def emit_ffn(G, call):
    nc, blk = G["nc"], G["blk"]
    nt, ppad, PSUB, NSORT = G["nt"], G["ppad"], G["PSUB"], G["NSORT"]
    T, CAP, FCN, DCN = G["T"], G["CAP"], G["FCN"], G["DCN"]
    LD, GD, SD, CC, TE, VE, AE = (G[k] for k in ["LD", "GD", "SD", "CC", "TE", "VE", "AE"])
    LDX, XE, SC, GX, GA, SDY, SDS = (G[k] for k in ["LDX", "XE", "SC", "GX", "GA", "SDY", "SDS"])
    W1D, W2D = G["W1D"], G["W2D"]
    BV, BA = G["BV"], G["BA"]
    (x_d, xres_b, sort_b, yeout_b, o_d, w1_d, w2_d, ag_in, ag_out, agout_sb) = (
        G[k] for k in ["x_d", "xres_b", "sort_b", "yeout_b", "o_d", "w1_d", "w2_d",
                       "ag_in", "ag_out", "agout_sb"])
    (iota8_sb, tri_sb, idf_sb, idb_sb, ones_sb, pmask_sb, wr_sb) = (
        G[k] for k in ["iota8_sb", "tri_sb", "idf_sb", "idb_sb", "ones_sb",
                       "pmask_sb", "wr_sb"])
    (p_v, eidx_v, lrank_v, pscale_v, sidx_v, gidx_v, oh_v, pref_v) = (
        G[k] for k in ["p_v", "eidx_v", "lrank_v", "pscale_v", "sidx_v", "gidx_v",
                       "oh_v", "pref_v"])
    (totrow, runrow, combrow, offs_row, hist_row, scr, scru) = (
        G[k] for k in ["totrow", "runrow", "combrow", "offs_row", "hist_row",
                       "scr", "scru"])
    dbg, debug = G["dbg"], G["debug"]
    bnd_reg = G["bnd_reg"]
    final = (call == 1)
    src_fp32 = (call == 0)
    xsrc = x_d if src_fp32 else xres_b

    with contextlib.ExitStack() as ph:
        E = ph.enter_context
        ybuf_sb = [E(nc.sbuf_tensor(f"c{call}_yb{i}", [128, DM], BF16)) for i in range(2)]
        xr_sb = [E(nc.sbuf_tensor(f"c{call}_xr{i}", [128, DM], BF16)) for i in range(2)]
        out_sb = [E(nc.sbuf_tensor(f"c{call}_out{i}", [128, DM], F32 if final else BF16))
                  for i in range(2)]
        p1s = ph.enter_context(contextlib.ExitStack())
        E1 = p1s.enter_context
        if src_fp32:
            xf_sb = [E1(nc.sbuf_tensor(f"c{call}_xf{i}", [128, DM], F32)) for i in range(4)]
        else:
            xf_sb = None
        xb_sb = [E1(nc.sbuf_tensor(f"c{call}_xb{i}", [128, DM], BF16)) for i in range(4)]
        xln_sb = [E1(nc.sbuf_tensor(f"c{call}_xln{i}", [128, DM], F32)) for i in range(4)]
        xlnb_sb = [E1(nc.sbuf_tensor(f"c{call}_xlnb{i}", [128, DM], BF16)) for i in range(nt)]
        xlnT_sb = [E1(nc.sbuf_tensor(f"c{call}_xlnT{i}", [128, DM], F32)) for i in range(2)]
        zero_sb = E1(nc.sbuf_tensor(f"c{call}_zero", [128, DM], BF16))
        tr_ps = [E1(nc.psum_tensor(f"c{call}_tr{i}", [128, DM], F32)) for i in range(2)]
        lg_ps = E1(nc.psum_tensor(f"c{call}_lg", [128, NEXP], F32))

        ldA = [0] * nt
        aeSQ = [0] * nt
        veLN = [0] * nt
        teTR = [0] * nt
        aeTR = [0] * nt
        teLG = [0] * nt
        aeEX = [0] * nt
        veOH = [0] * nt
        tePR = [0] * nt
        veSX = [0] * nt
        gdSC = [0] * nt
        gdXR = [0] * nt

        sd_prefill = SD.n
        if call == 0:
            @BV
            def _(e):
                VE.one(e.memset(zero_sb[:], 0.0))
            zv = VE.n

            @BA
            def _(e):
                e.wait_ge(VE.h, zv)
                for r in range(NSORT // 128):
                    SD.dma(e.dma_start(out=sort_b[r * 128:(r + 1) * 128, :], in_=zero_sb[:]))
            sd_prefill = SD.n

        veTR = [0] * nt
        veLG = [0] * nt
        lgall = G["lgall"]
        exall = G["exall"]
        mx8all = G["mx8all"]
        preB_ps = E1(nc.psum_tensor(f"c{call}_preB", [128, 32], F32))
        totB_ps = E1(nc.psum_tensor(f"c{call}_totB", [1, 32], F32))
        bc_ps = E1(nc.psum_tensor(f"c{call}_bc", [128, 32], F32))

        def stage_a(j):
            b4 = j % 4
            xin = xf_sb[b4] if src_fp32 else xb_sb[b4]
            sc = 32 * b4

            @blk.sync
            def _(e, j=j, b4=b4, xin=xin):
                e.wait_ge(VE.h, G["init_ve"])
                if j >= 4:
                    e.wait_ge(VE.h, veLN[j - 4])
                    if src_fp32:
                        e.wait_ge(GX[j % 2].h, gdXR[j - 4])
                if not src_fp32:
                    for sv, vv in G["sd_xres_done"]:
                        e.wait_ge(sv.h, vv)
                ldA[j] = LDX[j % 4].dma(e.dma_start(out=xin[:],
                                                    in_=xsrc[j * 128:(j + 1) * 128, :]))

            @BV
            def _(e, j=j, xin=xin, sc=sc):
                e.wait_ge(LDX[j % 4].h, ldA[j])
                e.bn_stats(scr[:, sc + 0:sc + 6], xin[:, 0:512])
                e.bn_stats(scr[:, sc + 6:sc + 12], xin[:, 512:1024])
                e.bn_aggr(scr[:, sc + 12:sc + 14], scr[:, sc + 0:sc + 12])
                VE.one(e.tensor_scalar_add(scr[:, sc + 14:sc + 15],
                                           scr[:, sc + 13:sc + 14], 1e-5))
            vStat = VE.n

            @BA
            def _(e, j=j, sc=sc, vStat=vStat):
                e.wait_ge(VE.h, vStat)
                e.activation(scr[:, sc + 15:sc + 16], scr[:, sc + 14:sc + 15], AF.Ln)
                aeSQ[j] = AE.one(e.activation(scr[:, sc + 16:sc + 17],
                                              scr[:, sc + 15:sc + 16], AF.Exp,
                                              scale=-0.5))

            @BV
            def _(e, j=j, b4=b4, xin=xin, sc=sc):
                e.wait_ge(AE.h, aeSQ[j])
                if j >= 4:
                    e.wait_ge(TE.h, teTR[j - 4])
                e.tensor_scalar(xln_sb[b4][:], xin[:], scr[:, sc + 12:sc + 13],
                                scr[:, sc + 16:sc + 17], ALU.subtract, ALU.mult)
                veLN[j] = VE.one(e.tensor_copy(xlnb_sb[j][:], xln_sb[b4][:]))

        def stage_b(j):
            b4 = j % 4
            b2 = j % 2

            @blk.tensor
            def _(e, j=j, b4=b4, b2=b2):
                e.wait_ge(VE.h, veLN[j])
                if j >= 2:
                    e.wait_ge(VE.h, veTR[j - 2])
                for dc in range(DCN):
                    ins = e.transpose(tr_ps[b2][:, dc * 128:(dc + 1) * 128],
                                      xln_sb[b4][:, dc * 128:(dc + 1) * 128], idf_sb[:])
                teTR[j] = TE.one(ins)

            @BV
            def _(e, j=j, b2=b2):
                e.wait_ge(TE.h, teTR[j])
                veTR[j] = VE.one(e.tensor_copy(xlnT_sb[b2][:], tr_ps[b2][:]))

        def stage_c(j):
            b2 = j % 2

            @blk.tensor
            def _(e, j=j, b2=b2):
                e.wait_ge(VE.h, veTR[j])
                if j >= 1:
                    e.wait_ge(VE.h, veLG[j - 1])
                for dc in range(DCN):
                    ins = e.matmul(lg_ps[:], xlnT_sb[b2][:, dc * 128:(dc + 1) * 128],
                                   wr_sb[:, dc * NEXP:(dc + 1) * NEXP],
                                   start=(dc == 0), stop=(dc == DCN - 1))
                teLG[j] = TE.one(ins)

            @BV
            def _(e, j=j):
                e.wait_ge(TE.h, teLG[j])
                veLG[j] = VE.one(e.tensor_copy(lgall[:, j * NEXP:(j + 1) * NEXP],
                                               lg_ps[:]))

            if src_fp32:
                @blk.gpsimd
                def _(e, j=j):
                    e.wait_ge(VE.h, veLN[j])
                    if j >= 2:
                        e.wait_ge(GX[j % 2].h, gdXR[j - 2])
                    gdXR[j] = GX[j % 2].dma(e.dma_start(
                        out=xres_b[j * 128:(j + 1) * 128, :], in_=xf_sb[j % 4][:]))

        for step in range(nt + 3):
            if step < nt:
                stage_a(step)
            if 0 <= step - 2 < nt:
                stage_b(step - 2)
            if 0 <= step - 3 < nt:
                stage_c(step - 3)

        @BA
        def _(e):
            e.wait_ge(VE.h, veLG[nt - 1])
            AE.one(e.activation(exall[:], lgall[:], AF.Exp))
        aeEXall = AE.n

        nblk = nt // 4
        veOH4 = [0] * nblk
        veSX4 = [0] * nblk
        tePR4 = [0] * nblk
        teBC4 = [0] * nblk
        veRR4 = [0] * nblk
        for B in range(nblk):
            @BV
            def _(e, B=B):
                if B == 0:
                    e.wait_ge(AE.h, aeEXall)
                for j in range(4 * B, 4 * B + 4):
                    bb8 = 8 * (j % 2)
                    e.max(mx8all[:, j * NEXP:(j + 1) * NEXP],
                          exall[:, j * NEXP:(j + 1) * NEXP])
                    e.max_index(scru[:, bb8:bb8 + 8], mx8all[:, j * NEXP:(j + 1) * NEXP],
                                exall[:, j * NEXP:(j + 1) * NEXP])
                    e.tensor_copy(eidx_v[:, j:j + 1], scru[:, bb8:bb8 + 1])
                    e.tensor_single_scalar(oh_v[:, j * NEXP:(j + 1) * NEXP],
                                           iota8_sb[:], eidx_v[:, j:j + 1], ALU.is_equal)
                s0 = 32 * B
                e.tensor_reduce(scr[:, 0:4].rearrange("p (g one) -> p g one", one=1),
                                exall[:, s0:s0 + 32].rearrange("p (g x) -> p g x", x=8),
                                axis=AX.X, op=ALU.add)
                e.reciprocal(scr[:, 4:8], scr[:, 0:4])
                VE.one(e.tensor_tensor(
                    p_v[:, 4 * B:4 * B + 4],
                    mx8all[:, s0:s0 + 32].rearrange("p (g x) -> p g x", x=8)[:, :, 0],
                    scr[:, 4:8], ALU.mult))
            veOH4[B] = VE.n

            @blk.tensor
            def _(e, B=B):
                e.wait_ge(VE.h, veOH4[B])
                if B >= 1:
                    e.wait_ge(VE.h, veRR4[B - 1])
                e.matmul(preB_ps[:], tri_sb[:], oh_v[:, 32 * B:32 * B + 32],
                         start=True, stop=True)
                tePR4[B] = TE.one(e.matmul(totB_ps[:], ones_sb[:, 0:1],
                                           oh_v[:, 32 * B:32 * B + 32],
                                           start=True, stop=True))

            @BV
            def _(e, B=B):
                e.wait_ge(TE.h, tePR4[B])
                e.tensor_copy(pref_v[:, 32 * B:32 * B + 32], preB_ps[:])
                e.tensor_copy(totrow[:, 32 * B:32 * B + 32], totB_ps[:])
                last = None
                for j in range(4 * B, 4 * B + 4):
                    last = e.tensor_add(runrow[:, (j + 1) * NEXP:(j + 2) * NEXP],
                                        runrow[:, j * NEXP:(j + 1) * NEXP],
                                        totrow[:, j * NEXP:(j + 1) * NEXP])
                VE.one(last)
            veRR4[B] = VE.n

            @blk.tensor
            def _(e, B=B):
                e.wait_ge(VE.h, veRR4[B])
                if B >= 1:
                    e.wait_ge(VE.h, veSX4[B - 1])
                teBC4[B] = TE.one(e.matmul(bc_ps[:], ones_sb[0:1, :],
                                           runrow[:, 32 * B:32 * B + 32],
                                           start=True, stop=True))

            @BV
            def _(e, B=B):
                e.wait_ge(TE.h, teBC4[B])
                s0 = 32 * B
                e.tensor_add(pref_v[:, s0:s0 + 32], pref_v[:, s0:s0 + 32], bc_ps[:])
                e.tensor_mul(scr[:, 32:64], oh_v[:, s0:s0 + 32], pref_v[:, s0:s0 + 32])
                e.tensor_reduce(lrank_v[:, 4 * B:4 * B + 4].rearrange(
                                    "p (g one) -> p g one", one=1),
                                scr[:, 32:64].rearrange("p (g x) -> p g x", x=8),
                                axis=AX.X, op=ALU.add)
                e.tensor_single_scalar(scr[:, 0:4], lrank_v[:, 4 * B:4 * B + 4],
                                       float(ppad), ALU.is_lt)
                e.tensor_scalar_mul(scr[:, 4:8], eidx_v[:, 4 * B:4 * B + 4], float(ppad))
                e.tensor_add(scr[:, 4:8], scr[:, 4:8], lrank_v[:, 4 * B:4 * B + 4])
                e.tensor_scalar(scr[:, 8:12], scr[:, 0:4], -BIGF, BIGF,
                                ALU.mult, ALU.add)
                e.tensor_add(scr[:, 4:8], scr[:, 4:8], scr[:, 8:12])
                VE.one(e.tensor_copy(sidx_v[:, 4 * B:4 * B + 4], scr[:, 4:8]))
            veSX4[B] = VE.n

            for j in range(4 * B, 4 * B + 4):
                @blk.gpsimd
                def _(e, j=j, B=B):
                    e.wait_ge(VE.h, veSX4[B])
                    if call == 0:
                        e.wait_ge(SD.h, sd_prefill)
                    if j >= 1:
                        e.wait_ge(SC[1 - j % 2].h, gdSC[j - 1])
                    gdSC[j] = SC[j % 2].dma(e.indirect_dma_start(
                        out=sort_b, out_offset=IndirectOffsetOnAxis(sidx_v[:, j:j + 1], 0),
                        in_=xlnb_sb[j][:], in_offset=None,
                        bounds_check=bnd_reg, oob_is_err=False))
        veSXall = VE.n

        @BV
        def _(e):
            VE.one(e.tensor_copy(hist_row[:], runrow[:, nt * NEXP:(nt + 1) * NEXP]))
        veHist = VE.n

        @blk.gpsimd
        def _(e):
            e.wait_ge(VE.h, veHist)
            GD.dma(e.dma_start(out=ag_in[call], in_=hist_row[:]))
            e.wait_ge(GD.h, GD.n)
            CC.one(e.collective_compute("AllGather", ALU.bypass, replica_groups=RG,
                                        ins=[ag_in[call]], outs=[ag_out[call]]))
            e.wait_ge(CC.h, CC.n)
            GD.dma(e.dma_start(out=agout_sb[call][:], in_=ag_out[call]))
        gdAg = GD.n

        @blk.tensor
        def _(e):
            e.wait_ge(GD.h, gdAg)
            e.wait_ge(VE.h, veSXall)
            TE.one(e.matmul(totB_ps[0:1, 0:NEXP], pmask_sb[:], agout_sb[call][:],
                            start=True, stop=True))
        teOffs = TE.n

        @BV
        def _(e):
            e.wait_ge(TE.h, teOffs)
            last = None
            for r in range(4):
                last = e.tensor_copy(offs_row[:] if False else
                                     combrow[:, r * NEXP:(r + 1) * NEXP],
                                     totB_ps[0:1, 0:NEXP])
            VE.one(last)
        veOffsC = VE.n

        @blk.tensor
        def _(e):
            e.wait_ge(VE.h, veOffsC)
            TE.one(e.matmul(bc_ps[:], ones_sb[0:1, :], combrow[:, 0:32],
                            start=True, stop=True))
        teBCoffs = TE.n

        vePC4 = [0] * nblk
        for B in range(nblk):
            @BV
            def _(e, B=B):
                e.wait_ge(TE.h, teBCoffs)
                s0 = 32 * B
                e.tensor_mul(scr[:, 32:64], oh_v[:, s0:s0 + 32], bc_ps[:])
                e.tensor_reduce(scr[:, 0:4].rearrange("p (g one) -> p g one", one=1),
                                scr[:, 32:64].rearrange("p (g x) -> p g x", x=8),
                                axis=AX.X, op=ALU.add)
                e.tensor_add(scr[:, 0:4], scr[:, 0:4], lrank_v[:, 4 * B:4 * B + 4])
                e.tensor_single_scalar(scr[:, 4:8], scr[:, 0:4], float(CAP), ALU.is_lt)
                e.tensor_single_scalar(scr[:, 8:12], lrank_v[:, 4 * B:4 * B + 4],
                                       float(ppad), ALU.is_lt)
                e.tensor_mul(scr[:, 4:8], scr[:, 4:8], scr[:, 8:12])
                e.tensor_mul(scr[:, 8:12], scr[:, 4:8], p_v[:, 4 * B:4 * B + 4])
                e.tensor_scalar_mul(pscale_v[:, 4 * B:4 * B + 4], scr[:, 8:12], 0.5)
                e.tensor_scalar_mul(scr[:, 8:12], eidx_v[:, 4 * B:4 * B + 4], float(ppad))
                e.tensor_add(scr[:, 8:12], scr[:, 8:12], lrank_v[:, 4 * B:4 * B + 4])
                e.tensor_scalar(scr[:, 12:16], scr[:, 4:8], -BIGF, BIGF,
                                ALU.mult, ALU.add)
                e.tensor_add(scr[:, 8:12], scr[:, 8:12], scr[:, 12:16])
                vePC4[B] = VE.one(e.tensor_copy(gidx_v[:, 4 * B:4 * B + 4], scr[:, 8:12]))
        vePCall = VE.n

        if debug and call == 0:
            @BA
            def _(e):
                e.wait_ge(VE.h, vePCall)
                for nm, src in [("p", p_v), ("eidx", eidx_v), ("lrank", lrank_v),
                                ("pscale", pscale_v), ("gidx", gidx_v), ("sidx", sidx_v)]:
                    SD.dma(e.dma_start(out=dbg[nm], in_=src[:]))
                SD.dma(e.dma_start(out=dbg["hist"], in_=agout_sb[call][:]))

        if G.get("stop_after") == "p1" and call == 0:
            return
        sctot = [(SC[0], SC[0].n), (SC[1], SC[1].n),
                 (GX[0], GX[0].n), (GX[1], GX[1].n)]
        p1s.close()
        with contextlib.ExitStack() as p2:
            E2 = p2.enter_context
            NXB = max(PSUB, 4)
            xe_sb = [E2(nc.sbuf_tensor(f"c{call}_xe{i}", [128, DM], BF16))
                     for i in range(NXB)]
            xeT_sb2 = [[E2(nc.sbuf_tensor(f"c{call}_xeT{p}_{d}", [128, ppad], BF16))
                        for d in range(DCN)] for p in range(2)]
            hT_sb = [E2(nc.sbuf_tensor(f"c{call}_hT{f}", [128, ppad], BF16))
                     for f in range(FCN)]
            w1b_sb = [E2(nc.sbuf_tensor(f"c{call}_w1b{i}", [128, DCN * 256], BF16))
                      for i in range(2)]
            w2a_sb = E2(nc.sbuf_tensor(f"c{call}_w2a", [128, FCN, DM], BF16))
            yeb_sb = [E2(nc.sbuf_tensor(f"c{call}_yeb{i}", [128, DM], BF16))
                      for i in range(2)]
            tr2_ps = E2(nc.psum_tensor(f"c{call}_tr2", [128, ppad], BF16))
            h_ps = [E2(nc.psum_tensor(f"c{call}_h{i}", [128, ppad], F32)) for i in range(2)]
            ye_ps = E2(nc.psum_tensor(f"c{call}_ye", [128, DM], F32))

            ngrp = FCN // 2
            ldXE = {}
            teXT = {}
            veXT = {}
            teH = {}
            evH = {}
            gdW1 = {}
            gdW2 = {}
            teYE = {}
            veYE = {}
            sdYE = {}
            sch = _splits(ppad, 512)

            for ex in range(NEXP):
                @blk.sync
                def _(e, ex=ex):
                    for sv, vv in sctot:
                        e.wait_ge(sv.h, vv)
                    if ex >= 1:
                        e.wait_ge(TE.h, teXT[(ex - 1, DCN - 1)])
                    for s in range(PSUB):
                        k = ex * PSUB + s
                        bb = k % NXB
                        ldXE[(ex, s)] = XE[ex % 2].dma(e.dma_start(
                            out=xe_sb[bb][:],
                            in_=sort_b[(ex * ppad + s * 128):(ex * ppad + (s + 1) * 128), :]))

                for dc in range(DCN):
                    @blk.tensor
                    def _(e, ex=ex, dc=dc):
                        if dc == 0:
                            e.wait_ge(XE[ex % 2].h, ldXE[(ex, PSUB - 1)])
                            if ex >= 2:
                                e.wait_ge(TE.h, teH[(ex - 2, 2 * ngrp - 1)])
                        if (ex, dc - 1) in veXT:
                            e.wait_ge(VE.h, veXT[(ex, dc - 1)])
                        elif ex >= 1:
                            e.wait_ge(VE.h, veXT[(ex - 1, DCN - 1)])
                        for s in range(PSUB):
                            bb = (ex * PSUB + s) % NXB
                            ins = e.transpose(tr2_ps[:, s * 128:(s + 1) * 128],
                                              xe_sb[bb][:, dc * 128:(dc + 1) * 128],
                                              idb_sb[:])
                        teXT[(ex, dc)] = TE.one(ins)

                    @BV
                    def _(e, ex=ex, dc=dc):
                        e.wait_ge(TE.h, teXT[(ex, dc)])
                        veXT[(ex, dc)] = VE.one(e.tensor_copy(
                            xeT_sb2[ex % 2][dc][:], tr2_ps[:]))


                for g in range(ngrp):
                    @blk.gpsimd
                    def _(e, ex=ex, g=g):
                        k = ex * ngrp + g
                        if k >= 2:
                            pk = k - 2
                            e.wait_ge(TE.h, teH[(pk // ngrp, 2 * (pk % ngrp) + 1)])
                        else:
                            for sv, vv in sctot:
                                e.wait_ge(sv.h, vv)
                        gdW1[(ex, g)] = W1D[k % 2].dma(e.dma_start(
                            out=w1b_sb[k % 2][:], in_=w1_d[ex, g]))
                        if g % 2 == 0 and g // 2 < FCN // 4:
                            f4 = (g // 2) * 4
                            if g == 0 and ex >= 1:
                                e.wait_ge(TE.h, teYE[(ex - 1, PSUB - 1)])
                            gdW2[(ex, f4)] = W2D.dma(e.dma_start(
                                out=w2a_sb[:, f4:f4 + 4, :],
                                in_=w2_d[ex, f4 * 128:(f4 + 4) * 128, :].rearrange(
                                    "(i p) c -> p i c", p=128)))

                    for fi in range(2):
                        fc = 2 * g + fi

                        @blk.tensor
                        def _(e, ex=ex, g=g, fi=fi, fc=fc):
                            if g == 0 and fi == 0:
                                e.wait_ge(VE.h, veXT[(ex, DCN - 1)])
                            prev = (ex, fc - 2)
                            if prev in evH:
                                eng, cnt = evH[prev]
                                e.wait_ge(eng.h, cnt)
                            elif ex >= 1:
                                eng, cnt = evH[(ex - 1, 2 * ngrp - 2 + fi)]
                                e.wait_ge(eng.h, cnt)
                            kb = (ex * ngrp + g) % 2
                            if fi == 0:
                                e.wait_ge(W1D[kb].h, gdW1[(ex, g)])
                            for dc in range(DCN):
                                for (so, sw) in sch:
                                    ins = e.matmul(h_ps[fc % 2][:, so:so + sw],
                                                   w1b_sb[kb][:, dc * 256 + fi * 128:dc * 256 + (fi + 1) * 128],
                                                   xeT_sb2[ex % 2][dc][:, so:so + sw],
                                                   start=(dc == 0), stop=(dc == DCN - 1))
                            teH[(ex, fc)] = TE.one(ins)

                        @BV
                        def _(e, ex=ex, fc=fc):
                            e.wait_ge(TE.h, teH[(ex, fc)])
                            evH[(ex, fc)] = (VE, VE.one(
                                e.tensor_scalar_max(hT_sb[fc][:], h_ps[fc % 2][:], 0.0)))

                for s in range(PSUB):
                    @blk.tensor
                    def _(e, ex=ex, s=s):
                        if s == 0:
                            eng, cnt = evH[(ex, 2 * ngrp - 1)]
                            e.wait_ge(eng.h, cnt)
                            e.wait_ge(W2D.h, gdW2[(ex, FCN - 4)])
                            e.wait_ge(W1D[0].h, W1D[0].n if False else gdW1[(ex, 0)])
                        if (ex, s - 1) in veYE:
                            e.wait_ge(VE.h, veYE[(ex, s - 1)])
                        elif ex >= 1:
                            e.wait_ge(VE.h, veYE[(ex - 1, PSUB - 1)])
                        for fc in range(FCN):
                            for (do, dw) in _splits(DM, 512):
                                ins = e.matmul(ye_ps[:, do:do + dw],
                                               hT_sb[fc][:, s * 128:(s + 1) * 128],
                                               w2a_sb[:, fc, do:do + dw],
                                               start=(fc == 0), stop=(fc == FCN - 1))
                        teYE[(ex, s)] = TE.one(ins)

                    @BV
                    def _(e, ex=ex, s=s):
                        e.wait_ge(TE.h, teYE[(ex, s)])
                        k = ex * PSUB + s
                        if k >= 2:
                            e.wait_ge(SDY[k % 2].h, sdYE[k - 2])
                        veYE[(ex, s)] = VE.one(e.tensor_copy(yeb_sb[k % 2][:], ye_ps[:]))

                    @BA
                    def _(e, ex=ex, s=s):
                        e.wait_ge(VE.h, veYE[(ex, s)])
                        k = ex * PSUB + s
                        sdYE[k] = SDY[k % 2].dma(e.dma_start(
                            out=yeout_b[(ex * ppad + s * 128):(ex * ppad + (s + 1) * 128), :],
                            in_=yeb_sb[k % 2][:]))

            yetot = [(SDY[0], SDY[0].n), (SDY[1], SDY[1].n)]

        if G.get("stop_after") == "p2" and call == 0:
            return
        veMS = [0] * nt
        gdGA = [0] * nt
        ldXR = [0] * nt
        veAD = [0] * nt
        sdST = [0] * nt
        for j in range(nt):
            b = j % 2

            @BV
            def _(e, j=j, b=b):
                e.wait_ge(VE.h, vePCall)
                if j >= 2:
                    e.wait_ge(VE.h, veAD[j - 2])
                veMS[j] = VE.one(e.memset(ybuf_sb[b][:], 0.0))

            @blk.gpsimd
            def _(e, j=j, b=b):
                e.wait_ge(VE.h, veMS[j])
                for sv, vv in yetot:
                    e.wait_ge(sv.h, vv)
                gdGA[j] = GA[b].dma(e.indirect_dma_start(
                    out=ybuf_sb[b][:], out_offset=None, in_=yeout_b,
                    in_offset=IndirectOffsetOnAxis(gidx_v[:, j:j + 1], 0),
                    bounds_check=bnd_reg, oob_is_err=False))

            @blk.sync
            def _(e, j=j, b=b):
                if j >= 2:
                    e.wait_ge(VE.h, veAD[j - 2])
                if call == 0:
                    e.wait_ge(GX[0].h, GX[0].n)
                    e.wait_ge(GX[1].h, GX[1].n)
                ldXR[j] = LDX[b].dma(e.dma_start(out=xr_sb[b][:],
                                                 in_=xres_b[j * 128:(j + 1) * 128, :]))

            @BV
            def _(e, j=j, b=b):
                e.wait_ge(GA[b].h, gdGA[j])
                e.wait_ge(LDX[b].h, ldXR[j])
                if j >= 2:
                    e.wait_ge(SDS[b].h, sdST[j - 2])
                veAD[j] = VE.one(e.scalar_tensor_tensor(
                    out_sb[b][:], ybuf_sb[b][:], pscale_v[:, j:j + 1], xr_sb[b][:],
                    ALU.mult, ALU.add))

            @BA
            def _(e, j=j, b=b):
                e.wait_ge(VE.h, veAD[j])
                dst = o_d if final else xres_b
                sdST[j] = SDS[b].dma(e.dma_start(out=dst[j * 128:(j + 1) * 128, :],
                                                 in_=out_sb[b][:]))

        if not final:
            G["sd_x1_done"] = [(SDS[0], SDS[0].n), (SDS[1], SDS[1].n)]
            if debug:
                emit_dump(G, xres_b, dbg["x1"], T, G["sd_x1_done"])


def emit_attention(G):
    nc, blk = G["nc"], G["blk"]
    nt, T, S, ST, QCH, DCN = G["nt"], G["T"], G["S"], G["ST"], G["QCH"], G["DCN"]
    LD, GD, SD, TE, VE, AE = (G[k] for k in ["LD", "GD", "SD", "TE", "VE", "AE"])
    LDX, SDS = G["LDX"], G["SDS"]
    BV, BA = G["BV"], G["BA"]
    xres_b, zsum_b = G["xres_b"], G["zsum_b"]
    idb_sb, ones_sb, onesb_sb, wqkv_sb, bout_sb = (
        G[k] for k in ["idb_sb", "ones_sb", "onesb_sb", "wqkv_sb", "bout_sb"])
    woutT_d = G["woutT_d"]
    scr = G["scr"]
    dbg, debug = G["dbg"], G["debug"]
    x1_done = G["sd_x1_done"]

    with contextlib.ExitStack() as ph:
        E = ph.enter_context
        xlnT_sb = [E(nc.sbuf_tensor(f"a_xlnT{d}", [128, S], BF16)) for d in range(DCN)]
        attnT_sb = [E(nc.sbuf_tensor(f"a_attnT{d}", [128, S], BF16)) for d in range(DCN)]
        wout_sb = [E(nc.sbuf_tensor(f"a_wout{d}", [128, DM], BF16)) for d in range(DCN)]
        qt_sb = E(nc.sbuf_tensor("a_qt", [128, S], BF16))
        kt_sb = E(nc.sbuf_tensor("a_kt", [128, S], BF16))
        vx_sb = E(nc.sbuf_tensor("a_vx", [128, ST, 65], BF16))
        pb_sb = [E(nc.sbuf_tensor(f"a_pb{i}", [128, 1024], BF16)) for i in range(6)]
        xt_sb = [E(nc.sbuf_tensor(f"a_xt{i}", [128, DM], BF16)) for i in range(2)]
        xlnb_sb = [E(nc.sbuf_tensor(f"a_xlnb{i}", [128, DM], BF16)) for i in range(2)]
        zrow_sb = E(nc.sbuf_tensor("a_zrow", [1, (8 if NH % 8 == 0 else NH) * S], BF16))
        rrow_sb = E(nc.sbuf_tensor("a_rrow", [1, (8 if NH % 8 == 0 else NH) * S], BF16))
        zT_sb = E(nc.sbuf_tensor("a_zT", [128, (8 if NH % 8 == 0 else NH) * ST], BF16))
        rT_sb = E(nc.sbuf_tensor("a_rT", [128, (8 if NH % 8 == 0 else NH) * ST], F32))
        rTb_sb = E(nc.sbuf_tensor("a_rTb", [128, (8 if NH % 8 == 0 else NH) * ST], BF16))
        xo_sb = [E(nc.sbuf_tensor(f"a_xo{i}", [128, DM], BF16)) for i in range(2)]

        gd_prior = GD.n

        @blk.gpsimd
        def _(e):
            e.wait_ge(GD.h, gd_prior)
            for sv, vv in x1_done:
                e.wait_ge(sv.h, vv)
            for d in range(DCN):
                GD.dma(e.dma_start(out=wout_sb[d][:], in_=woutT_d[d * 128:(d + 1) * 128, :]))
        gdWo = GD.n

        prev_batch_ve = G["init_ve"]
        sdXO_prev = {0: 0, 1: 0}
        for b in range(2):
            base = b * ST

            with contextlib.ExitStack() as pa:
                trb_ps = pa.enter_context(nc.psum_tensor(f"a{b}_trb", [128, 1024], BF16))
                ldX = [0] * ST
                aeS2 = [0] * ST
                veL2 = [0] * ST
                teT2 = [0] * ST
                veE2 = [0] * ST
                for t in range(ST):
                    j = base + t
                    bb = t % 2
                    sc = 32 * bb

                    @blk.sync
                    def _(e, t=t, j=j, bb=bb):
                        for sv, vv in x1_done:
                            e.wait_ge(sv.h, vv)
                        e.wait_ge(VE.h, prev_batch_ve)
                        if t >= 2:
                            e.wait_ge(VE.h, veL2[t - 2])
                        ldX[t] = LDX[bb].dma(e.dma_start(out=xt_sb[bb][:],
                                                         in_=xres_b[j * 128:(j + 1) * 128, :]))

                    @BV
                    def _(e, t=t, bb=bb, sc=sc):
                        e.wait_ge(LDX[bb].h, ldX[t])
                        e.bn_stats(scr[:, sc + 0:sc + 6], xt_sb[bb][:, 0:512])
                        e.bn_stats(scr[:, sc + 6:sc + 12], xt_sb[bb][:, 512:1024])
                        e.bn_aggr(scr[:, sc + 12:sc + 14], scr[:, sc + 0:sc + 12])
                        VE.one(e.tensor_scalar_add(scr[:, sc + 14:sc + 15],
                                                   scr[:, sc + 13:sc + 14], 1e-5))
                    vS = VE.n

                    @BA
                    def _(e, t=t, sc=sc, vS=vS):
                        e.wait_ge(VE.h, vS)
                        aeS2[t] = AE.one(e.activation(scr[:, sc + 15:sc + 16],
                                                      scr[:, sc + 14:sc + 15], AF.Sqrt))

                    @BV
                    def _(e, t=t, bb=bb, sc=sc):
                        e.wait_ge(AE.h, aeS2[t])
                        if t >= 2:
                            e.wait_ge(TE.h, teT2[t - 2])
                        e.reciprocal(scr[:, sc + 16:sc + 17], scr[:, sc + 15:sc + 16])
                        veL2[t] = VE.one(e.tensor_scalar(
                            xlnb_sb[bb][:], xt_sb[bb][:], scr[:, sc + 12:sc + 13],
                            scr[:, sc + 16:sc + 17], ALU.subtract, ALU.mult))

                    @blk.tensor
                    def _(e, t=t, bb=bb):
                        e.wait_ge(VE.h, veL2[t])
                        if t >= 1:
                            e.wait_ge(VE.h, veE2[t - 1])
                        for dc in range(DCN):
                            ins = e.transpose(trb_ps[:, dc * 128:(dc + 1) * 128],
                                              xlnb_sb[bb][:, dc * 128:(dc + 1) * 128],
                                              idb_sb[:])
                        teT2[t] = TE.one(ins)

                    @BV
                    def _(e, t=t):
                        e.wait_ge(TE.h, teT2[t])
                        for dc in range(DCN):
                            ins = e.tensor_copy(xlnT_sb[dc][:, t * 128:(t + 1) * 128],
                                                trb_ps[:, dc * 128:(dc + 1) * 128])
                        veE2[t] = VE.one(ins)
                xlnT_done = VE.n
            if G.get("stop_after") == "aln":
                return

            HPG = 8 if NH % 8 == 0 else NH
            NHG = NH // HPG
            pvEv = {}
            nmLast = xlnT_done
            veOnes_done = False
            for hg in range(NHG):
                with contextlib.ExitStack() as pb_:
                    Eb = pb_.enter_context
                    if not veOnes_done:
                        @BV
                        def _(e):
                            e.wait_ge(VE.h, xlnT_done)
                            e.wait_ge(GD.h, gdWo)
                            e.memset(vx_sb[:, :, 64:65], 1.0)
                            e.memset(qt_sb[64:128, :], 0.0)
                            VE.one(e.memset(kt_sb[64:128, :], 0.0))
                        veOnes = VE.n
                        veOnes_done = True

                    for h in range(hg * HPG, (hg + 1) * HPG):
                        p64 = (h % 2) * 64
                        ct = h // 2
                        hh = h % HPG

                        with contextlib.ExitStack() as pq:
                            qv_ps = [pq.enter_context(
                                nc.psum_tensor(f"a{b}h{h}_qv{i}", [128, 512], F32))
                                for i in range(2)]
                            teQ = []
                            veQ = []
                            qsteps = []
                            for (qo, qw) in _splits(S, 512):
                                qsteps.append(("q", qo, qw))
                                qsteps.append(("k", qo, qw))
                            for t in range(ST):
                                qsteps.append(("v", t, 0))
                            for qi, (kind, qo, qw) in enumerate(qsteps):
                                @blk.tensor
                                def _(e, h=h, p64=p64, ct=ct, kind=kind, qo=qo,
                                      qw=qw, qi=qi, nmLast=nmLast):
                                    if qi == 0:
                                        if h % HPG == 0:
                                            e.wait_ge(VE.h, nmLast)
                                        if h == 0:
                                            e.wait_ge(VE.h, veOnes)
                                        if h > 0:
                                            e.wait_ge(TE.h, G["a_pvdone"][(b, h - 1)])
                                            e.wait_ge(VE.h, G["a_pvev_last"][(b, h - 1)])
                                    if qi >= 2:
                                        e.wait_ge(VE.h, veQ[qi - 2])
                                    buf = qv_ps[qi % 2]
                                    zb = 192 * ((h % 2))
                                    if kind == "q":
                                        ins = e.matmul(buf[0:64, 0:qw],
                                                       wqkv_sb[:, zb + 0:zb + 64],
                                                       xlnT_sb[ct][:, qo:qo + qw],
                                                       start=True, stop=True)
                                    elif kind == "k":
                                        ins = e.matmul(buf[0:64, 0:qw],
                                                       wqkv_sb[:, zb + 64:zb + 128],
                                                       xlnT_sb[ct][:, qo:qo + qw],
                                                       start=True, stop=True)
                                    else:
                                        t = qo
                                        ins = e.matmul(buf[:, 0:64],
                                                       xlnT_sb[ct][:, t * 128:(t + 1) * 128],
                                                       wqkv_sb[:, zb + 128:zb + 192],
                                                       start=True, stop=True)
                                    teQ.append(TE.one(ins))

                                @BV
                                def _(e, kind=kind, qo=qo, qw=qw, qi=qi):
                                    e.wait_ge(TE.h, teQ[qi])
                                    buf = qv_ps[qi % 2]
                                    if kind == "q":
                                        ins = e.tensor_scalar_mul(
                                            qt_sb[0:64, qo:qo + qw], buf[0:64, 0:qw], 0.03125)
                                    elif kind == "k":
                                        ins = e.tensor_copy(kt_sb[0:64, qo:qo + qw],
                                                            buf[0:64, 0:qw])
                                    else:
                                        ins = e.tensor_copy(vx_sb[:, qo, 0:64],
                                                            buf[:, 0:64])
                                    veQ.append(VE.one(ins))
                            qkvDone = veQ[-1]

                        with contextlib.ExitStack() as psl:
                            s_ps = [psl.enter_context(
                                nc.psum_tensor(f"a{b}h{h}_s{i}", [128, 1024], F32))
                                for i in range(3)]
                            pv_ps = psl.enter_context(
                                nc.psum_tensor(f"a{b}h{h}_pv", [128, 1024], F32))

                            for ci, (qo, qw) in enumerate(QCH):
                                teS = [0] * ST
                                tePV = [0] * ST
                                aeP = [AE.n + kt + 1 for kt in range(ST)]

                                def emit_pv(ktp, qw=qw):
                                    @blk.tensor
                                    def _(e, ktp=ktp, qw=qw):
                                        e.wait_ge(AE.h, aeP[ktp])
                                        for (o2, w2) in _splits(qw, 512):
                                            ins = e.matmul(pv_ps[0:65, o2:o2 + w2],
                                                           vx_sb[:, ktp, :],
                                                           pb_sb[ktp % 6][:, o2:o2 + w2],
                                                           start=(ktp == 0),
                                                           stop=(ktp == ST - 1))
                                        tePV[ktp] = TE.one(ins)

                                for kt in range(ST):
                                    @blk.tensor
                                    def _(e, kt=kt, qo=qo, qw=qw, ci=ci, h=h,
                                          qkvDone=qkvDone):
                                        if kt == 0:
                                            e.wait_ge(VE.h, qkvDone)
                                            if ci > 0:
                                                e.wait_ge(VE.h, pvEv[(b, h, ci - 1)])
                                        if kt >= 3:
                                            e.wait_ge(AE.h, aeP[kt - 3])
                                        for (o2, w2) in _splits(qw, 512):
                                            ins = e.matmul(s_ps[kt % 3][:, o2:o2 + w2],
                                                           kt_sb[:, kt * 128:(kt + 1) * 128],
                                                           qt_sb[:, qo + o2:qo + o2 + w2],
                                                           start=True, stop=True)
                                        teS[kt] = TE.one(ins)

                                    if kt >= 2:
                                        emit_pv(kt - 2)
                                emit_pv(ST - 2)
                                emit_pv(ST - 1)

                                @BA
                                def _(e, qw=qw, teS=teS, tePV=tePV):
                                    for kt in range(ST):
                                        e.wait_ge(TE.h, teS[kt])
                                        if kt >= 6:
                                            e.wait_ge(TE.h, tePV[kt - 6])
                                        e.activation(pb_sb[kt % 6][:, 0:qw],
                                                     s_ps[kt % 3][:, 0:qw], AF.Exp)
                                assert AE.n == aeP[ST - 1], (AE.n, aeP[ST - 1])
                                G.setdefault("a_pvdone", {})[(b, h)] = tePV[ST - 1]

                                @BV
                                def _(e, h=h, ct=ct, p64=p64, qo=qo, qw=qw, ci=ci,
                                      hh=hh, last=tePV[ST - 1]):
                                    e.wait_ge(TE.h, last)
                                    e.tensor_copy(attnT_sb[ct][p64:p64 + 64, qo:qo + qw],
                                                  pv_ps[0:64, 0:qw])
                                    pvEv[(b, h, ci)] = VE.one(e.tensor_copy(
                                        zrow_sb[:, hh * S + qo:hh * S + qo + qw],
                                        pv_ps[64:65, 0:qw]))
                            G.setdefault("a_pvev_last", {})[(b, h)] = pvEv[(b, h, len(QCH) - 1)]
                    zEvAll = VE.n

                with contextlib.ExitStack() as pn:
                    En = pn.enter_context
                    rb_ps = [En(nc.psum_tensor(f"a{b}g{hg}_rb{i}", [1, 128], F32))
                             for i in range(2)]
                    bc_ps = En(nc.psum_tensor(f"a{b}g{hg}_bc", [64, 1024], F32))
                    zreg = zsum_b[:, hg * HPG * S:(hg + 1) * HPG * S]

                    @blk.gpsimd
                    def _(e, zreg=zreg):
                        e.wait_ge(VE.h, zEvAll)
                        e.wait_ge(LD.h, LD.n)
                        GD.dma(e.dma_start(out=zreg, in_=zrow_sb[:]))
                    gdZ = GD.n

                    @blk.sync
                    def _(e, zreg=zreg):
                        e.wait_ge(GD.h, gdZ)
                        LD.dma(e.dma_start(
                            out=zT_sb[:],
                            in_=zreg.rearrange("one (r c) -> (one r) c", c=128),
                            transpose=True))
                    ldZ = LD.n

                    @BV
                    def _(e):
                        e.wait_ge(LD.h, ldZ)
                        e.reciprocal(rT_sb[:], zT_sb[:])
                        VE.one(e.tensor_copy(rTb_sb[:], rT_sb[:]))
                    veR = VE.n

                    teRT = []
                    veRT = []
                    for hh in range(HPG):
                        for t in range(ST):
                            col = hh * ST + t

                            @blk.tensor
                            def _(e, col=col, idx=len(teRT)):
                                if idx == 0:
                                    e.wait_ge(VE.h, veR)
                                if idx >= 2:
                                    e.wait_ge(VE.h, veRT[idx - 2])
                                teRT.append(TE.one(e.matmul(
                                    rb_ps[idx % 2][0:1, 0:128],
                                    rTb_sb[:, col:col + 1], idb_sb[:],
                                    start=True, stop=True)))

                            @BV
                            def _(e, hh=hh, t=t, idx=len(veRT)):
                                e.wait_ge(TE.h, teRT[idx])
                                veRT.append(VE.one(e.tensor_copy(
                                    rrow_sb[:, hh * S + t * 128:hh * S + (t + 1) * 128],
                                    rb_ps[idx % 2][0:1, 0:128])))
                    rrowDone = veRT[-1]

                    teBCn = []
                    veNM = []
                    for h in range(hg * HPG, (hg + 1) * HPG):
                        p64 = (h % 2) * 64
                        ct = h // 2
                        hh = h % HPG
                        for (qo, qw) in _splits(S, 1024):
                            @blk.tensor
                            def _(e, hh=hh, qo=qo, qw=qw, idx=len(teBCn)):
                                e.wait_ge(VE.h, rrowDone)
                                if idx >= 1:
                                    e.wait_ge(VE.h, veNM[idx - 1])
                                for (o2, w2) in _splits(qw, 512):
                                    ins = e.matmul(
                                        bc_ps[:, o2:o2 + w2], onesb_sb[0:1, 0:64],
                                        rrow_sb[:, hh * S + qo + o2:hh * S + qo + o2 + w2],
                                        start=True, stop=True)
                                teBCn.append(TE.one(ins))

                            @BV
                            def _(e, ct=ct, p64=p64, qo=qo, qw=qw, idx=len(veNM)):
                                e.wait_ge(TE.h, teBCn[idx])
                                veNM.append(VE.one(e.tensor_mul(
                                    attnT_sb[ct][p64:p64 + 64, qo:qo + qw],
                                    attnT_sb[ct][p64:p64 + 64, qo:qo + qw],
                                    bc_ps[:, 0:qw])))
                    nmLast = veNM[-1]
            nmDone = nmLast

            with contextlib.ExitStack() as po:
                op_ps = po.enter_context(nc.psum_tensor(f"a{b}_op", [128, 1024], F32))
                ldXO = [0] * ST
                teOP = [0] * ST
                veAO = [0] * ST
                sdXO = [0] * ST
                for t in range(ST):
                    j = base + t
                    bb = t % 2

                    @blk.sync
                    def _(e, t=t, j=j, bb=bb):
                        e.wait_ge(VE.h, nmDone)
                        if t >= 2:
                            e.wait_ge(VE.h, veAO[t - 2])
                        ldXO[t] = LDX[bb].dma(e.dma_start(out=xt_sb[bb][:],
                                                          in_=xres_b[j * 128:(j + 1) * 128, :]))

                    @blk.tensor
                    def _(e, t=t, bb=bb):
                        e.wait_ge(VE.h, nmDone)
                        e.wait_ge(GD.h, gdWo)
                        if t >= 1:
                            e.wait_ge(VE.h, veAO[t - 1])
                        for (do, dw) in _splits(DM, 512):
                            for dc in range(DCN):
                                e.matmul(op_ps[:, do:do + dw],
                                         attnT_sb[dc][:, t * 128:(t + 1) * 128],
                                         wout_sb[dc][:, do:do + dw],
                                         start=(dc == 0), stop=False)
                            ins = e.matmul(op_ps[:, do:do + dw], onesb_sb[:],
                                           bout_sb[:, do:do + dw],
                                           start=False, stop=True)
                        teOP[t] = TE.one(ins)

                    @BV
                    def _(e, t=t, bb=bb):
                        e.wait_ge(TE.h, teOP[t])
                        e.wait_ge(LDX[bb].h, ldXO[t])
                        e.wait_ge(SDS[bb].h, sdXO[t - 2] if t >= 2 else sdXO_prev[bb])
                        veAO[t] = VE.one(e.tensor_add(xo_sb[bb][:], xt_sb[bb][:], op_ps[:]))

                    @BA
                    def _(e, t=t, j=j, bb=bb):
                        e.wait_ge(VE.h, veAO[t])
                        sdXO[t] = SDS[bb].dma(e.dma_start(out=xres_b[j * 128:(j + 1) * 128, :],
                                                          in_=xo_sb[bb][:]))

            prev_batch_ve = VE.n
            for bb2 in range(2):
                last = [sdXO[t2] for t2 in range(ST) if t2 % 2 == bb2]
                if last:
                    sdXO_prev[bb2] = last[-1]
        G["sd_xres_done"] = [(SDS[0], SDS[0].n), (SDS[1], SDS[1].n)]
        if debug:
            emit_dump(G, xres_b, dbg["x2"], T, G["sd_xres_done"])


def kernel(**inputs):
    x = np.asarray(inputs["L0_attn_x"], np.float32)
    Wq = np.asarray(inputs["L0_attn_Wq"], np.float32)
    Wk = np.asarray(inputs["L0_attn_Wk"], np.float32)
    Wv = np.asarray(inputs["L0_attn_Wv"], np.float32)
    Wout = np.asarray(inputs["L0_attn_W_out"], np.float32)
    bout = np.asarray(inputs["L0_attn_b_out"], np.float32)
    Wr = np.asarray(inputs["L0_ffn_Wr"], np.float32)
    W1 = np.asarray(inputs["L0_ffn_W1"], np.float32)
    W2 = np.asarray(inputs["L0_ffn_W2"], np.float32)

    nt = NT
    T = 128 * nt
    nc = build(nt=nt, ppad=PPAD, debug=False)
    in_maps = make_inputs(x, Wq, Wk, Wv, Wout, bout, Wr, W1, W2, nt)
    res = run_bass_kernel_spmd(nc, in_maps, core_ids=list(range(NCORE)))
    out = np.stack([r["o"].reshape(2, T // 2, DM) for r in res.results], axis=0)
    return np.ascontiguousarray(out.reshape(16, T // 2, DM)).astype(np.float32)


def make_inputs(x, Wq, Wk, Wv, Wout, bout, Wr, W1, W2, nt):
    T = 128 * nt
    w3 = np.concatenate([Wq.T, Wk.T, Wv.T], axis=1)
    z = np.zeros_like(w3)
    wqkv = np.stack([np.concatenate([w3, z], axis=0),
                     np.concatenate([z, w3], axis=0)])
    ngrp = FF // 256
    w1p = np.ascontiguousarray(
        W1.reshape(NEXP, DM // 128, 128, ngrp, 256).transpose(0, 3, 2, 1, 4)
    ).reshape(NEXP, ngrp, 128, (DM // 128) * 256)
    in_common = dict(
        wr=np.ascontiguousarray(Wr),
        w1=w1p,
        w2=np.ascontiguousarray(W2),
        wqkv=np.ascontiguousarray(wqkv),
        woutT=np.ascontiguousarray(Wout.T),
        bout=np.ascontiguousarray(bout.reshape(1, DM)),
        iota8=np.tile(np.arange(NEXP, dtype=np.float32), (128, 1)),
        tri=(np.arange(128)[:, None] < np.arange(128)[None, :]).astype(np.float32),
        idf=np.eye(128, dtype=np.float32),
        idb=np.eye(128).astype(ml_dtypes.bfloat16),
        onesin=np.ones((128, 128), np.float32),
    )
    in_maps = []
    for c in range(NCORE):
        m = dict(in_common)
        m["x"] = np.ascontiguousarray(x[2 * c:2 * c + 2].reshape(T, DM))
        m["pmask"] = (np.arange(NCORE) < c).astype(np.float32).reshape(NCORE, 1)
        in_maps.append(m)
    return in_maps


# version 09
# speedup vs baseline: 1.9775x; 1.0434x over previous
import contextlib
import numpy as np
import ml_dtypes
import concourse.bass as bass
import concourse.mybir as mybir
from concourse.bass import IndirectOffsetOnAxis
from concourse.bass_utils import run_bass_kernel_spmd

F32 = mybir.dt.float32
BF16 = mybir.dt.bfloat16
U32 = mybir.dt.uint32
AF = mybir.ActivationFunctionType
ALU = mybir.AluOpType
AX = mybir.AxisListType

NCORE = 8
DM = 1024
NH = 16
DH = 64
NEXP = 8
FF = 4096
BIGF = 1.0e7

NT = 32
PPAD = 768
DEBUG = False

RG = [list(range(NCORE))]


def _splits(total, step=512):
    out, o = [], 0
    while o < total:
        out.append((o, min(step, total - o)))
        o += step
    return out


class Sem:
    def __init__(self, nc, es, name):
        self.h = es.enter_context(nc.semaphore(name))
        self.n = 0
        self.last_chained = None

    def dma(self, instr):
        instr.then_inc(self.h, 16)
        self.n += 16
        return self.n

    def one(self, instr):
        if instr is self.last_chained:
            return self.n
        instr.then_inc(self.h, 1)
        self.n += 1
        return self.n


class Chain:
    PASSTHRU = ("wait_ge", "dma_start", "indirect_dma_start",
                "collective_compute", "engine_nop")

    def __init__(self, e, sem):
        self._e = e
        self._s = sem

    def __getattr__(self, name):
        fn = getattr(self._e, name)
        if name in Chain.PASSTHRU or not callable(fn):
            return fn
        s = self._s
        raw = self._e

        def wrapped(*a, **k):
            raw.wait_ge(s.h, s.n)
            ins = fn(*a, **k)
            ins.then_inc(s.h, 1)
            s.n += 1
            s.last_chained = ins
            return ins
        return wrapped


def build(nt=None, ppad=None, debug=None, stop_after=None):
    nt = NT if nt is None else nt
    ppad = PPAD if ppad is None else ppad
    debug = DEBUG if debug is None else debug

    T = 128 * nt
    S = T // 2
    ST = S // 128
    CAP = T
    PSUB = ppad // 128
    NSORT = NEXP * ppad
    FCN = FF // 128
    DCN = DM // 128
    QCH = _splits(S, 1024)

    nc = bass.Bass("TRN2", target_bir_lowering=False, debug=False)

    x_d = nc.dram_tensor("x", [T, DM], F32, kind="ExternalInput").ap()
    wr_d = nc.dram_tensor("wr", [DM, NEXP], F32, kind="ExternalInput").ap()
    w1_d = nc.dram_tensor("w1", [NEXP, FF // 256, 128, (DM // 128) * 256], F32, kind="ExternalInput").ap()
    w2_d = nc.dram_tensor("w2", [NEXP, FF, DM], F32, kind="ExternalInput").ap()
    wqkv_d = nc.dram_tensor("wqkv", [2, 128, 3 * DH], F32, kind="ExternalInput").ap()
    woutT_d = nc.dram_tensor("woutT", [DM, DM], F32, kind="ExternalInput").ap()
    bout_d = nc.dram_tensor("bout", [1, DM], F32, kind="ExternalInput").ap()
    iota8_d = nc.dram_tensor("iota8", [128, NEXP], F32, kind="ExternalInput").ap()
    tri_d = nc.dram_tensor("tri", [128, 128], F32, kind="ExternalInput").ap()
    idf_d = nc.dram_tensor("idf", [128, 128], F32, kind="ExternalInput").ap()
    idb_d = nc.dram_tensor("idb", [128, 128], BF16, kind="ExternalInput").ap()
    pmask_d = nc.dram_tensor("pmask", [NCORE, 1], F32, kind="ExternalInput").ap()
    ones_d = nc.dram_tensor("onesin", [128, 128], F32, kind="ExternalInput").ap()

    o_d = nc.dram_tensor("o", [T, DM], F32, kind="ExternalOutput").ap()
    dbg = {}
    if debug:
        for nm, shp, dt in [("eidx", [128, nt], F32), ("p", [128, nt], F32),
                            ("lrank", [128, nt], F32), ("pscale", [128, nt], F32),
                            ("gidx", [128, nt], U32), ("sidx", [128, nt], U32),
                            ("hist", [NCORE, NEXP], F32),
                            ("x1", [T, DM], BF16), ("x2", [T, DM], BF16)]:
            dbg[nm] = nc.dram_tensor("dbg_" + nm, shp, dt, kind="ExternalOutput").ap()

    xres_b = nc.dram_tensor("xres_b", [T, DM], BF16).ap()
    sort_b = nc.dram_tensor("sort_b", [NSORT, DM], BF16).ap()
    yeout_b = nc.dram_tensor("yeout_b", [NSORT, DM], BF16).ap()
    zsum_b = nc.dram_tensor("zsum_b", [1, NH * S], BF16).ap()
    rr_d = nc.dram_tensor("rr_d", [1, 128, 128], BF16).ap()
    ag_in = [nc.dram_tensor(f"ag_in{k}", [1, NEXP], F32).ap() for k in range(2)]
    ag_out = [nc.dram_tensor(f"ag_out{k}", [NCORE, NEXP], F32).ap() for k in range(2)]

    es = contextlib.ExitStack()
    with es:
        E = es.enter_context
        cpool = E(nc.sbuf_tensor("cpool", [128, 8], F32))
        iota8_sb = E(nc.sbuf_tensor("iota8_sb", [128, NEXP], F32))
        tri_sb = E(nc.sbuf_tensor("tri_sb", [128, 128], F32))
        idf_sb = E(nc.sbuf_tensor("idf_sb", [128, 128], F32))
        idb_sb = E(nc.sbuf_tensor("idb_sb", [128, 128], BF16))
        ones_sb = E(nc.sbuf_tensor("ones_sb", [128, 128], F32))
        onesb_sb = E(nc.sbuf_tensor("onesb_sb", [1, 128], BF16))
        pmask_sb = E(nc.sbuf_tensor("pmask_sb", [NCORE, 1], F32))
        wr_sb = E(nc.sbuf_tensor("wr_sb", [128, DCN * NEXP], F32))
        wqkv_sb = E(nc.sbuf_tensor("wqkv_sb", [128, 2 * 3 * DH], BF16))
        bout_sb = E(nc.sbuf_tensor("bout_sb", [1, DM], BF16))
        agout_sb = [E(nc.sbuf_tensor(f"agout_sb{k}", [NCORE, NEXP], F32)) for k in range(2)]
        p_v = E(nc.sbuf_tensor("p_v", [128, nt], F32))
        eidx_v = E(nc.sbuf_tensor("eidx_v", [128, nt], F32))
        lrank_v = E(nc.sbuf_tensor("lrank_v", [128, nt], F32))
        pscale_v = E(nc.sbuf_tensor("pscale_v", [128, nt], F32))
        sidx_v = E(nc.sbuf_tensor("sidx_v", [128, nt], U32))
        gidx_v = E(nc.sbuf_tensor("gidx_v", [128, nt], U32))
        oh_v = E(nc.sbuf_tensor("oh_v", [128, NEXP * nt], F32))
        lgall = E(nc.sbuf_tensor("lgall", [128, NEXP * nt], F32))
        exall = E(nc.sbuf_tensor("exall", [128, NEXP * nt], F32))
        mx8all = E(nc.sbuf_tensor("mx8all", [128, NEXP * nt], F32))
        pref_v = E(nc.sbuf_tensor("pref_v", [128, NEXP * nt], F32))
        totrow = E(nc.sbuf_tensor("totrow", [1, NEXP * nt], F32))
        runrow = E(nc.sbuf_tensor("runrow", [1, NEXP * (nt + 1)], F32))
        combrow = E(nc.sbuf_tensor("combrow", [1, NEXP * nt], F32))
        offs_row = E(nc.sbuf_tensor("offs_row", [1, NEXP], F32))
        hist_row = E(nc.sbuf_tensor("hist_row", [1, NEXP], F32))
        scr = E(nc.sbuf_tensor("scr", [128, 128], F32))
        scru = E(nc.sbuf_tensor("scru", [128, 16], U32))
        dump_sb = E(nc.sbuf_tensor("dump_sb", [128, DM], BF16)) if debug else None

        LD = Sem(nc, es, "LD")
        GD = Sem(nc, es, "GD")
        SD = Sem(nc, es, "SD")
        CC = Sem(nc, es, "CC")
        TE = Sem(nc, es, "TE")
        VE = Sem(nc, es, "VE")
        AE = Sem(nc, es, "AE")
        LDX = [Sem(nc, es, f"LDX{i}") for i in range(4)]
        XE = [Sem(nc, es, f"XE{i}") for i in range(2)]
        SC = [Sem(nc, es, f"SC{i}") for i in range(2)]
        GX = [Sem(nc, es, f"GX{i}") for i in range(2)]
        GA = [Sem(nc, es, f"GA{i}") for i in range(2)]
        SDY = [Sem(nc, es, f"SDY{i}") for i in range(2)]
        SDS = [Sem(nc, es, f"SDS{i}") for i in range(2)]
        W1D = [Sem(nc, es, f"W1D{i}") for i in range(2)]
        W2D = Sem(nc, es, "W2D")

        for cv, ci in [(0.0, 0), (1.0, 1), (0.5, 2), (0.03125, 3), (-0.5, 4)]:
            nc.const_aps.register(F32, cv, cpool[:, ci:ci + 1])

        blk = E(nc.Block())

        def BV(f):
            blk.vector(lambda e: f(Chain(e, VE)))

        def BA(f):
            blk.scalar(lambda e: f(Chain(e, AE)))

        @blk.sync
        def _(e):
            for dst, src in [(iota8_sb, iota8_d), (tri_sb, tri_d), (idf_sb, idf_d),
                             (idb_sb, idb_d), (ones_sb, ones_d), (pmask_sb, pmask_d)]:
                LD.dma(e.dma_start(out=dst[:], in_=src[:]))
            for dc in range(DCN):
                LD.dma(e.dma_start(out=wr_sb[:, dc * NEXP:(dc + 1) * NEXP],
                                   in_=wr_d[dc * 128:(dc + 1) * 128, :]))
        init_ld = LD.n

        bnd_reg = nc.gpsimd.alloc_register("bnd")

        @blk.gpsimd
        def _(e):
            e.reg_mov(bnd_reg, NSORT - 1)
            GD.dma(e.dma_start(out=wqkv_sb[:, 0:192], in_=wqkv_d[0]))
            GD.dma(e.dma_start(out=wqkv_sb[:, 192:384], in_=wqkv_d[1]))
            GD.dma(e.dma_start(out=bout_sb[:], in_=bout_d[:]))
        init_gd = GD.n

        @BV
        def _(e):
            for cv, ci in [(0.0, 0), (1.0, 1), (0.5, 2), (0.03125, 3), (-0.5, 4)]:
                e.memset(cpool[:, ci:ci + 1], cv)
            e.memset(runrow[:, 0:NEXP], 0.0)
            e.wait_ge(LD.h, init_ld)
            VE.one(e.tensor_copy(onesb_sb[:], ones_sb[0:1, :]))
        init_ve = VE.n

        G = dict(nc=nc, blk=blk, nt=nt, ppad=ppad, debug=debug, dbg=dbg,
                 T=T, S=S, ST=ST, CAP=CAP, PSUB=PSUB, NSORT=NSORT, FCN=FCN,
                 DCN=DCN, QCH=QCH,
                 x_d=x_d, o_d=o_d, w1_d=w1_d, w2_d=w2_d, woutT_d=woutT_d,
                 xres_b=xres_b, sort_b=sort_b, yeout_b=yeout_b, zsum_b=zsum_b,
                 rr_d=rr_d,
                 ag_in=ag_in, ag_out=ag_out, agout_sb=agout_sb,
                 cpool=cpool, iota8_sb=iota8_sb, tri_sb=tri_sb, idf_sb=idf_sb,
                 idb_sb=idb_sb, ones_sb=ones_sb, onesb_sb=onesb_sb,
                 pmask_sb=pmask_sb, wr_sb=wr_sb, wqkv_sb=wqkv_sb, bout_sb=bout_sb,
                 p_v=p_v, eidx_v=eidx_v, lrank_v=lrank_v, pscale_v=pscale_v,
                 sidx_v=sidx_v, gidx_v=gidx_v, oh_v=oh_v, pref_v=pref_v,
                 totrow=totrow, runrow=runrow, combrow=combrow, offs_row=offs_row,
                 hist_row=hist_row, scr=scr, scru=scru,
                 lgall=lgall, exall=exall, mx8all=mx8all,
                 dump_sb=dump_sb,
                 LD=LD, GD=GD, SD=SD, CC=CC, TE=TE, VE=VE, AE=AE,
                 LDX=LDX, XE=XE, SC=SC, GX=GX, GA=GA, SDY=SDY, SDS=SDS,
                 W1D=W1D, W2D=W2D, BV=BV, BA=BA, bnd_reg=bnd_reg,
                 init_ld=init_ld, init_gd=init_gd, init_ve=init_ve)

        G["stop_after"] = stop_after
        emit_ffn(G, 0)
        if stop_after not in ("p1", "p2", "ffn0"):
            emit_attention(G)
            if stop_after not in ("attn", "aln", "ahead", "anorm"):
                emit_ffn(G, 1)

        @blk.gpsimd
        def _(e):
            for s in [SD, GD, LD, W2D] + W1D + LDX + XE + SC + GX + GA + SDY + SDS:
                e.wait_ge(s.h, s.n)
    return nc


def emit_dump(G, src_dram, dst_dram, nrows, waits):
    nc, blk, LD, SD = G["nc"], G["blk"], G["LD"], G["SD"]
    BA = G["BA"]
    bsb = G["dump_sb"]
    sd_pre = SD.n
    lds = []
    sds = []
    for r in range(nrows // 128):
        @blk.sync
        def _(e, r=r):
            for sv, vv in waits:
                e.wait_ge(sv.h, vv)
            e.wait_ge(SD.h, sds[r - 1] if r >= 1 else sd_pre)
            lds.append(LD.dma(e.dma_start(out=bsb[:],
                                          in_=src_dram[r * 128:(r + 1) * 128, :])))

        @BA
        def _(e, r=r):
            e.wait_ge(LD.h, lds[r])
            sds.append(SD.dma(e.dma_start(out=dst_dram[r * 128:(r + 1) * 128, :],
                                          in_=bsb[:])))


def emit_ffn(G, call):
    nc, blk = G["nc"], G["blk"]
    nt, ppad, PSUB, NSORT = G["nt"], G["ppad"], G["PSUB"], G["NSORT"]
    T, CAP, FCN, DCN = G["T"], G["CAP"], G["FCN"], G["DCN"]
    LD, GD, SD, CC, TE, VE, AE = (G[k] for k in ["LD", "GD", "SD", "CC", "TE", "VE", "AE"])
    LDX, XE, SC, GX, GA, SDY, SDS = (G[k] for k in ["LDX", "XE", "SC", "GX", "GA", "SDY", "SDS"])
    W1D, W2D = G["W1D"], G["W2D"]
    BV, BA = G["BV"], G["BA"]
    (x_d, xres_b, sort_b, yeout_b, o_d, w1_d, w2_d, ag_in, ag_out, agout_sb) = (
        G[k] for k in ["x_d", "xres_b", "sort_b", "yeout_b", "o_d", "w1_d", "w2_d",
                       "ag_in", "ag_out", "agout_sb"])
    (iota8_sb, tri_sb, idf_sb, idb_sb, ones_sb, pmask_sb, wr_sb) = (
        G[k] for k in ["iota8_sb", "tri_sb", "idf_sb", "idb_sb", "ones_sb",
                       "pmask_sb", "wr_sb"])
    (p_v, eidx_v, lrank_v, pscale_v, sidx_v, gidx_v, oh_v, pref_v) = (
        G[k] for k in ["p_v", "eidx_v", "lrank_v", "pscale_v", "sidx_v", "gidx_v",
                       "oh_v", "pref_v"])
    (totrow, runrow, combrow, offs_row, hist_row, scr, scru) = (
        G[k] for k in ["totrow", "runrow", "combrow", "offs_row", "hist_row",
                       "scr", "scru"])
    dbg, debug = G["dbg"], G["debug"]
    bnd_reg = G["bnd_reg"]
    final = (call == 1)
    src_fp32 = (call == 0)
    xsrc = x_d if src_fp32 else xres_b

    with contextlib.ExitStack() as ph:
        E = ph.enter_context
        ybuf_sb = [E(nc.sbuf_tensor(f"c{call}_yb{i}", [128, DM], BF16)) for i in range(2)]
        xr_sb = [E(nc.sbuf_tensor(f"c{call}_xr{i}", [128, DM], BF16)) for i in range(2)]
        out_sb = [E(nc.sbuf_tensor(f"c{call}_out{i}", [128, DM], F32 if final else BF16))
                  for i in range(2)]
        p1s = ph.enter_context(contextlib.ExitStack())
        E1 = p1s.enter_context
        if src_fp32:
            xf_sb = [E1(nc.sbuf_tensor(f"c{call}_xf{i}", [128, DM], F32)) for i in range(4)]
        else:
            xf_sb = None
        xb_sb = [E1(nc.sbuf_tensor(f"c{call}_xb{i}", [128, DM], BF16)) for i in range(4)]
        xln_sb = [E1(nc.sbuf_tensor(f"c{call}_xln{i}", [128, DM], F32)) for i in range(4)]
        xlnb_sb = [E1(nc.sbuf_tensor(f"c{call}_xlnb{i}", [128, DM], BF16)) for i in range(nt)]
        xlnT_sb = [E1(nc.sbuf_tensor(f"c{call}_xlnT{i}", [128, DM], F32)) for i in range(2)]
        zero_sb = E1(nc.sbuf_tensor(f"c{call}_zero", [128, DM], BF16))
        tr_ps = [E1(nc.psum_tensor(f"c{call}_tr{i}", [128, DM], F32)) for i in range(2)]
        lg_ps = E1(nc.psum_tensor(f"c{call}_lg", [128, NEXP], F32))

        ldA = [0] * nt
        aeSQ = [0] * nt
        veLN = [0] * nt
        teTR = [0] * nt
        aeTR = [0] * nt
        teLG = [0] * nt
        aeEX = [0] * nt
        veOH = [0] * nt
        tePR = [0] * nt
        veSX = [0] * nt
        gdSC = [0] * nt
        gdXR = [0] * nt

        sd_prefill = SD.n
        if call == 0:
            @BV
            def _(e):
                VE.one(e.memset(zero_sb[:], 0.0))
            zv = VE.n

            @BA
            def _(e):
                e.wait_ge(VE.h, zv)
                for r in range(NSORT // 128):
                    SD.dma(e.dma_start(out=sort_b[r * 128:(r + 1) * 128, :], in_=zero_sb[:]))
            sd_prefill = SD.n

        veTR = [0] * nt
        veLG = [0] * nt
        lgall = G["lgall"]
        exall = G["exall"]
        mx8all = G["mx8all"]
        preB_ps = E1(nc.psum_tensor(f"c{call}_preB", [128, 32], F32))
        totB_ps = E1(nc.psum_tensor(f"c{call}_totB", [1, 32], F32))
        bc_ps = E1(nc.psum_tensor(f"c{call}_bc", [128, 32], F32))

        def stage_a(j):
            b4 = j % 4
            xin = xf_sb[b4] if src_fp32 else xb_sb[b4]
            sc = 32 * b4

            @blk.sync
            def _(e, j=j, b4=b4, xin=xin):
                e.wait_ge(VE.h, G["init_ve"])
                if j >= 4:
                    e.wait_ge(VE.h, veLN[j - 4])
                    if src_fp32:
                        e.wait_ge(GX[j % 2].h, gdXR[j - 4])
                if not src_fp32:
                    for sv, vv in G["sd_xres_done"]:
                        e.wait_ge(sv.h, vv)
                ldA[j] = LDX[j % 4].dma(e.dma_start(out=xin[:],
                                                    in_=xsrc[j * 128:(j + 1) * 128, :]))

            @BV
            def _(e, j=j, xin=xin, sc=sc):
                e.wait_ge(LDX[j % 4].h, ldA[j])
                e.bn_stats(scr[:, sc + 0:sc + 6], xin[:, 0:512])
                e.bn_stats(scr[:, sc + 6:sc + 12], xin[:, 512:1024])
                e.bn_aggr(scr[:, sc + 12:sc + 14], scr[:, sc + 0:sc + 12])
                VE.one(e.tensor_scalar_add(scr[:, sc + 14:sc + 15],
                                           scr[:, sc + 13:sc + 14], 1e-5))
            vStat = VE.n

            @BA
            def _(e, j=j, sc=sc, vStat=vStat):
                e.wait_ge(VE.h, vStat)
                e.activation(scr[:, sc + 15:sc + 16], scr[:, sc + 14:sc + 15], AF.Ln)
                aeSQ[j] = AE.one(e.activation(scr[:, sc + 16:sc + 17],
                                              scr[:, sc + 15:sc + 16], AF.Exp,
                                              scale=-0.5))

            @BV
            def _(e, j=j, b4=b4, xin=xin, sc=sc):
                e.wait_ge(AE.h, aeSQ[j])
                if j >= 4:
                    e.wait_ge(TE.h, teTR[j - 4])
                e.tensor_scalar(xln_sb[b4][:], xin[:], scr[:, sc + 12:sc + 13],
                                scr[:, sc + 16:sc + 17], ALU.subtract, ALU.mult)
                veLN[j] = VE.one(e.tensor_copy(xlnb_sb[j][:], xln_sb[b4][:]))

        def stage_b(j):
            b4 = j % 4
            b2 = j % 2

            @blk.tensor
            def _(e, j=j, b4=b4, b2=b2):
                e.wait_ge(VE.h, veLN[j])
                if j >= 2:
                    e.wait_ge(VE.h, veTR[j - 2])
                for dc in range(DCN):
                    ins = e.transpose(tr_ps[b2][:, dc * 128:(dc + 1) * 128],
                                      xln_sb[b4][:, dc * 128:(dc + 1) * 128], idf_sb[:])
                teTR[j] = TE.one(ins)

            @BV
            def _(e, j=j, b2=b2):
                e.wait_ge(TE.h, teTR[j])
                veTR[j] = VE.one(e.tensor_copy(xlnT_sb[b2][:], tr_ps[b2][:]))

        def stage_c(j):
            b2 = j % 2

            @blk.tensor
            def _(e, j=j, b2=b2):
                e.wait_ge(VE.h, veTR[j])
                if j >= 1:
                    e.wait_ge(VE.h, veLG[j - 1])
                for dc in range(DCN):
                    ins = e.matmul(lg_ps[:], xlnT_sb[b2][:, dc * 128:(dc + 1) * 128],
                                   wr_sb[:, dc * NEXP:(dc + 1) * NEXP],
                                   start=(dc == 0), stop=(dc == DCN - 1))
                teLG[j] = TE.one(ins)

            @BV
            def _(e, j=j):
                e.wait_ge(TE.h, teLG[j])
                veLG[j] = VE.one(e.tensor_copy(lgall[:, j * NEXP:(j + 1) * NEXP],
                                               lg_ps[:]))

            if src_fp32:
                @blk.gpsimd
                def _(e, j=j):
                    e.wait_ge(VE.h, veLN[j])
                    if j >= 2:
                        e.wait_ge(GX[j % 2].h, gdXR[j - 2])
                    gdXR[j] = GX[j % 2].dma(e.dma_start(
                        out=xres_b[j * 128:(j + 1) * 128, :], in_=xf_sb[j % 4][:]))

        for step in range(nt + 3):
            if step < nt:
                stage_a(step)
            if 0 <= step - 2 < nt:
                stage_b(step - 2)
            if 0 <= step - 3 < nt:
                stage_c(step - 3)

        @BA
        def _(e):
            e.wait_ge(VE.h, veLG[nt - 1])
            AE.one(e.activation(exall[:], lgall[:], AF.Exp))
        aeEXall = AE.n

        nblk = nt // 4
        veOH4 = [0] * nblk
        veSX4 = [0] * nblk
        tePR4 = [0] * nblk
        teBC4 = [0] * nblk
        veRR4 = [0] * nblk
        for B in range(nblk):
            @BV
            def _(e, B=B):
                if B == 0:
                    e.wait_ge(AE.h, aeEXall)
                for j in range(4 * B, 4 * B + 4):
                    bb8 = 8 * (j % 2)
                    e.max(mx8all[:, j * NEXP:(j + 1) * NEXP],
                          exall[:, j * NEXP:(j + 1) * NEXP])
                    e.max_index(scru[:, bb8:bb8 + 8], mx8all[:, j * NEXP:(j + 1) * NEXP],
                                exall[:, j * NEXP:(j + 1) * NEXP])
                    e.tensor_copy(eidx_v[:, j:j + 1], scru[:, bb8:bb8 + 1])
                    e.tensor_single_scalar(oh_v[:, j * NEXP:(j + 1) * NEXP],
                                           iota8_sb[:], eidx_v[:, j:j + 1], ALU.is_equal)
                s0 = 32 * B
                e.tensor_reduce(scr[:, 0:4].rearrange("p (g one) -> p g one", one=1),
                                exall[:, s0:s0 + 32].rearrange("p (g x) -> p g x", x=8),
                                axis=AX.X, op=ALU.add)
                e.reciprocal(scr[:, 4:8], scr[:, 0:4])
                VE.one(e.tensor_tensor(
                    p_v[:, 4 * B:4 * B + 4],
                    mx8all[:, s0:s0 + 32].rearrange("p (g x) -> p g x", x=8)[:, :, 0],
                    scr[:, 4:8], ALU.mult))
            veOH4[B] = VE.n

            @blk.tensor
            def _(e, B=B):
                e.wait_ge(VE.h, veOH4[B])
                if B >= 1:
                    e.wait_ge(VE.h, veRR4[B - 1])
                e.matmul(preB_ps[:], tri_sb[:], oh_v[:, 32 * B:32 * B + 32],
                         start=True, stop=True)
                tePR4[B] = TE.one(e.matmul(totB_ps[:], ones_sb[:, 0:1],
                                           oh_v[:, 32 * B:32 * B + 32],
                                           start=True, stop=True))

            @BV
            def _(e, B=B):
                e.wait_ge(TE.h, tePR4[B])
                e.tensor_copy(pref_v[:, 32 * B:32 * B + 32], preB_ps[:])
                e.tensor_copy(totrow[:, 32 * B:32 * B + 32], totB_ps[:])
                last = None
                for j in range(4 * B, 4 * B + 4):
                    last = e.tensor_add(runrow[:, (j + 1) * NEXP:(j + 2) * NEXP],
                                        runrow[:, j * NEXP:(j + 1) * NEXP],
                                        totrow[:, j * NEXP:(j + 1) * NEXP])
                VE.one(last)
            veRR4[B] = VE.n

            @blk.tensor
            def _(e, B=B):
                e.wait_ge(VE.h, veRR4[B])
                if B >= 1:
                    e.wait_ge(VE.h, veSX4[B - 1])
                teBC4[B] = TE.one(e.matmul(bc_ps[:], ones_sb[0:1, :],
                                           runrow[:, 32 * B:32 * B + 32],
                                           start=True, stop=True))

            @BV
            def _(e, B=B):
                e.wait_ge(TE.h, teBC4[B])
                s0 = 32 * B
                e.tensor_add(pref_v[:, s0:s0 + 32], pref_v[:, s0:s0 + 32], bc_ps[:])
                e.tensor_mul(scr[:, 32:64], oh_v[:, s0:s0 + 32], pref_v[:, s0:s0 + 32])
                e.tensor_reduce(lrank_v[:, 4 * B:4 * B + 4].rearrange(
                                    "p (g one) -> p g one", one=1),
                                scr[:, 32:64].rearrange("p (g x) -> p g x", x=8),
                                axis=AX.X, op=ALU.add)
                e.tensor_single_scalar(scr[:, 0:4], lrank_v[:, 4 * B:4 * B + 4],
                                       float(ppad), ALU.is_lt)
                e.tensor_scalar_mul(scr[:, 4:8], eidx_v[:, 4 * B:4 * B + 4], float(ppad))
                e.tensor_add(scr[:, 4:8], scr[:, 4:8], lrank_v[:, 4 * B:4 * B + 4])
                e.tensor_scalar(scr[:, 8:12], scr[:, 0:4], -BIGF, BIGF,
                                ALU.mult, ALU.add)
                e.tensor_add(scr[:, 4:8], scr[:, 4:8], scr[:, 8:12])
                VE.one(e.tensor_copy(sidx_v[:, 4 * B:4 * B + 4], scr[:, 4:8]))
            veSX4[B] = VE.n

            for j in range(4 * B, 4 * B + 4):
                @blk.gpsimd
                def _(e, j=j, B=B):
                    e.wait_ge(VE.h, veSX4[B])
                    if call == 0:
                        e.wait_ge(SD.h, sd_prefill)
                    if j >= 1:
                        e.wait_ge(SC[1 - j % 2].h, gdSC[j - 1])
                    gdSC[j] = SC[j % 2].dma(e.indirect_dma_start(
                        out=sort_b, out_offset=IndirectOffsetOnAxis(sidx_v[:, j:j + 1], 0),
                        in_=xlnb_sb[j][:], in_offset=None,
                        bounds_check=bnd_reg, oob_is_err=False))
        veSXall = VE.n

        @BV
        def _(e):
            VE.one(e.tensor_copy(hist_row[:], runrow[:, nt * NEXP:(nt + 1) * NEXP]))
        veHist = VE.n

        @blk.gpsimd
        def _(e):
            e.wait_ge(VE.h, veHist)
            GD.dma(e.dma_start(out=ag_in[call], in_=hist_row[:]))
            e.wait_ge(GD.h, GD.n)
            CC.one(e.collective_compute("AllGather", ALU.bypass, replica_groups=RG,
                                        ins=[ag_in[call]], outs=[ag_out[call]]))
            e.wait_ge(CC.h, CC.n)
            GD.dma(e.dma_start(out=agout_sb[call][:], in_=ag_out[call]))
        gdAg = GD.n

        @blk.tensor
        def _(e):
            e.wait_ge(GD.h, gdAg)
            e.wait_ge(VE.h, veSXall)
            TE.one(e.matmul(totB_ps[0:1, 0:NEXP], pmask_sb[:], agout_sb[call][:],
                            start=True, stop=True))
        teOffs = TE.n

        @BV
        def _(e):
            e.wait_ge(TE.h, teOffs)
            last = None
            for r in range(4):
                last = e.tensor_copy(offs_row[:] if False else
                                     combrow[:, r * NEXP:(r + 1) * NEXP],
                                     totB_ps[0:1, 0:NEXP])
            VE.one(last)
        veOffsC = VE.n

        @blk.tensor
        def _(e):
            e.wait_ge(VE.h, veOffsC)
            TE.one(e.matmul(bc_ps[:], ones_sb[0:1, :], combrow[:, 0:32],
                            start=True, stop=True))
        teBCoffs = TE.n

        vePC4 = [0] * nblk
        for B in range(nblk):
            @BV
            def _(e, B=B):
                e.wait_ge(TE.h, teBCoffs)
                s0 = 32 * B
                e.tensor_mul(scr[:, 32:64], oh_v[:, s0:s0 + 32], bc_ps[:])
                e.tensor_reduce(scr[:, 0:4].rearrange("p (g one) -> p g one", one=1),
                                scr[:, 32:64].rearrange("p (g x) -> p g x", x=8),
                                axis=AX.X, op=ALU.add)
                e.tensor_add(scr[:, 0:4], scr[:, 0:4], lrank_v[:, 4 * B:4 * B + 4])
                e.tensor_single_scalar(scr[:, 4:8], scr[:, 0:4], float(CAP), ALU.is_lt)
                e.tensor_single_scalar(scr[:, 8:12], lrank_v[:, 4 * B:4 * B + 4],
                                       float(ppad), ALU.is_lt)
                e.tensor_mul(scr[:, 4:8], scr[:, 4:8], scr[:, 8:12])
                e.tensor_mul(scr[:, 8:12], scr[:, 4:8], p_v[:, 4 * B:4 * B + 4])
                e.tensor_scalar_mul(pscale_v[:, 4 * B:4 * B + 4], scr[:, 8:12], 0.5)
                e.tensor_scalar_mul(scr[:, 8:12], eidx_v[:, 4 * B:4 * B + 4], float(ppad))
                e.tensor_add(scr[:, 8:12], scr[:, 8:12], lrank_v[:, 4 * B:4 * B + 4])
                e.tensor_scalar(scr[:, 12:16], scr[:, 4:8], -BIGF, BIGF,
                                ALU.mult, ALU.add)
                e.tensor_add(scr[:, 8:12], scr[:, 8:12], scr[:, 12:16])
                vePC4[B] = VE.one(e.tensor_copy(gidx_v[:, 4 * B:4 * B + 4], scr[:, 8:12]))
        vePCall = VE.n

        if debug and call == 0:
            @BA
            def _(e):
                e.wait_ge(VE.h, vePCall)
                for nm, src in [("p", p_v), ("eidx", eidx_v), ("lrank", lrank_v),
                                ("pscale", pscale_v), ("gidx", gidx_v), ("sidx", sidx_v)]:
                    SD.dma(e.dma_start(out=dbg[nm], in_=src[:]))
                SD.dma(e.dma_start(out=dbg["hist"], in_=agout_sb[call][:]))

        if G.get("stop_after") == "p1" and call == 0:
            return
        sctot = [(SC[0], SC[0].n), (SC[1], SC[1].n),
                 (GX[0], GX[0].n), (GX[1], GX[1].n)]
        p1s.close()
        with contextlib.ExitStack() as p2:
            E2 = p2.enter_context
            NXB = max(PSUB, 4)
            xe_sb = [E2(nc.sbuf_tensor(f"c{call}_xe{i}", [128, DM], BF16))
                     for i in range(NXB)]
            xeT_sb2 = [[E2(nc.sbuf_tensor(f"c{call}_xeT{p}_{d}", [128, ppad], BF16))
                        for d in range(DCN)] for p in range(2)]
            hT_sb = [E2(nc.sbuf_tensor(f"c{call}_hT{f}", [128, ppad], BF16))
                     for f in range(FCN)]
            w1b_sb = [E2(nc.sbuf_tensor(f"c{call}_w1b{i}", [128, DCN * 256], BF16))
                      for i in range(2)]
            w2a_sb = E2(nc.sbuf_tensor(f"c{call}_w2a", [128, FCN, DM], BF16))
            yeb_sb = [E2(nc.sbuf_tensor(f"c{call}_yeb{i}", [128, DM], BF16))
                      for i in range(2)]
            tr2_ps = E2(nc.psum_tensor(f"c{call}_tr2", [128, ppad], BF16))
            h_ps = [E2(nc.psum_tensor(f"c{call}_h{i}", [128, ppad], F32)) for i in range(2)]
            ye_ps = E2(nc.psum_tensor(f"c{call}_ye", [128, DM], F32))

            ngrp = FCN // 2
            ldXE = {}
            teXT = {}
            veXT = {}
            teH = {}
            evH = {}
            gdW1 = {}
            gdW2 = {}
            teYE = {}
            veYE = {}
            sdYE = {}
            sch = _splits(ppad, 512)

            for ex in range(NEXP):
                @blk.sync
                def _(e, ex=ex):
                    for sv, vv in sctot:
                        e.wait_ge(sv.h, vv)
                    if ex >= 1:
                        e.wait_ge(TE.h, teXT[(ex - 1, DCN - 1)])
                    for s in range(PSUB):
                        k = ex * PSUB + s
                        bb = k % NXB
                        ldXE[(ex, s)] = XE[ex % 2].dma(e.dma_start(
                            out=xe_sb[bb][:],
                            in_=sort_b[(ex * ppad + s * 128):(ex * ppad + (s + 1) * 128), :]))

                for dc in range(DCN):
                    @blk.tensor
                    def _(e, ex=ex, dc=dc):
                        if dc == 0:
                            e.wait_ge(XE[ex % 2].h, ldXE[(ex, PSUB - 1)])
                            if ex >= 2:
                                e.wait_ge(TE.h, teH[(ex - 2, 2 * ngrp - 1)])
                        if (ex, dc - 1) in veXT:
                            e.wait_ge(VE.h, veXT[(ex, dc - 1)])
                        elif ex >= 1:
                            e.wait_ge(VE.h, veXT[(ex - 1, DCN - 1)])
                        for s in range(PSUB):
                            bb = (ex * PSUB + s) % NXB
                            ins = e.transpose(tr2_ps[:, s * 128:(s + 1) * 128],
                                              xe_sb[bb][:, dc * 128:(dc + 1) * 128],
                                              idb_sb[:])
                        teXT[(ex, dc)] = TE.one(ins)

                    @BV
                    def _(e, ex=ex, dc=dc):
                        e.wait_ge(TE.h, teXT[(ex, dc)])
                        veXT[(ex, dc)] = VE.one(e.tensor_copy(
                            xeT_sb2[ex % 2][dc][:], tr2_ps[:]))


                for g in range(ngrp):
                    @blk.gpsimd
                    def _(e, ex=ex, g=g):
                        k = ex * ngrp + g
                        if k >= 2:
                            pk = k - 2
                            e.wait_ge(TE.h, teH[(pk // ngrp, 2 * (pk % ngrp) + 1)])
                        else:
                            for sv, vv in sctot:
                                e.wait_ge(sv.h, vv)
                        gdW1[(ex, g)] = W1D[k % 2].dma(e.dma_start(
                            out=w1b_sb[k % 2][:], in_=w1_d[ex, g]))
                        if g % 2 == 0 and g // 2 < FCN // 4:
                            f4 = (g // 2) * 4
                            if g == 0 and ex >= 1:
                                e.wait_ge(TE.h, teYE[(ex - 1, PSUB - 1)])
                            gdW2[(ex, f4)] = W2D.dma(e.dma_start(
                                out=w2a_sb[:, f4:f4 + 4, :],
                                in_=w2_d[ex, f4 * 128:(f4 + 4) * 128, :].rearrange(
                                    "(i p) c -> p i c", p=128)))

                    for fi in range(2):
                        fc = 2 * g + fi

                        @blk.tensor
                        def _(e, ex=ex, g=g, fi=fi, fc=fc):
                            if g == 0 and fi == 0:
                                e.wait_ge(VE.h, veXT[(ex, DCN - 1)])
                            prev = (ex, fc - 2)
                            if prev in evH:
                                eng, cnt = evH[prev]
                                e.wait_ge(eng.h, cnt)
                            elif ex >= 1:
                                eng, cnt = evH[(ex - 1, 2 * ngrp - 2 + fi)]
                                e.wait_ge(eng.h, cnt)
                            kb = (ex * ngrp + g) % 2
                            if fi == 0:
                                e.wait_ge(W1D[kb].h, gdW1[(ex, g)])
                            for dc in range(DCN):
                                for (so, sw) in sch:
                                    ins = e.matmul(h_ps[fc % 2][:, so:so + sw],
                                                   w1b_sb[kb][:, dc * 256 + fi * 128:dc * 256 + (fi + 1) * 128],
                                                   xeT_sb2[ex % 2][dc][:, so:so + sw],
                                                   start=(dc == 0), stop=(dc == DCN - 1))
                            teH[(ex, fc)] = TE.one(ins)

                        @BV
                        def _(e, ex=ex, fc=fc):
                            e.wait_ge(TE.h, teH[(ex, fc)])
                            evH[(ex, fc)] = (VE, VE.one(
                                e.tensor_scalar_max(hT_sb[fc][:], h_ps[fc % 2][:], 0.0)))

                for s in range(PSUB):
                    @blk.tensor
                    def _(e, ex=ex, s=s):
                        if s == 0:
                            eng, cnt = evH[(ex, 2 * ngrp - 1)]
                            e.wait_ge(eng.h, cnt)
                            e.wait_ge(W2D.h, gdW2[(ex, FCN - 4)])
                            e.wait_ge(W1D[0].h, W1D[0].n if False else gdW1[(ex, 0)])
                        if (ex, s - 1) in veYE:
                            e.wait_ge(VE.h, veYE[(ex, s - 1)])
                        elif ex >= 1:
                            e.wait_ge(VE.h, veYE[(ex - 1, PSUB - 1)])
                        for fc in range(FCN):
                            for (do, dw) in _splits(DM, 512):
                                ins = e.matmul(ye_ps[:, do:do + dw],
                                               hT_sb[fc][:, s * 128:(s + 1) * 128],
                                               w2a_sb[:, fc, do:do + dw],
                                               start=(fc == 0), stop=(fc == FCN - 1))
                        teYE[(ex, s)] = TE.one(ins)

                    @BV
                    def _(e, ex=ex, s=s):
                        e.wait_ge(TE.h, teYE[(ex, s)])
                        k = ex * PSUB + s
                        if k >= 2:
                            e.wait_ge(SDY[k % 2].h, sdYE[k - 2])
                        veYE[(ex, s)] = VE.one(e.tensor_copy(yeb_sb[k % 2][:], ye_ps[:]))

                    @BA
                    def _(e, ex=ex, s=s):
                        e.wait_ge(VE.h, veYE[(ex, s)])
                        k = ex * PSUB + s
                        sdYE[k] = SDY[k % 2].dma(e.dma_start(
                            out=yeout_b[(ex * ppad + s * 128):(ex * ppad + (s + 1) * 128), :],
                            in_=yeb_sb[k % 2][:]))

            yetot = [(SDY[0], SDY[0].n), (SDY[1], SDY[1].n)]

        if G.get("stop_after") == "p2" and call == 0:
            return
        veMS = [0] * nt
        gdGA = [0] * nt
        ldXR = [0] * nt
        veAD = [0] * nt
        sdST = [0] * nt
        for j in range(nt):
            b = j % 2

            @BV
            def _(e, j=j, b=b):
                e.wait_ge(VE.h, vePCall)
                if j >= 2:
                    e.wait_ge(VE.h, veAD[j - 2])
                veMS[j] = VE.one(e.memset(ybuf_sb[b][:], 0.0))

            @blk.gpsimd
            def _(e, j=j, b=b):
                e.wait_ge(VE.h, veMS[j])
                for sv, vv in yetot:
                    e.wait_ge(sv.h, vv)
                gdGA[j] = GA[b].dma(e.indirect_dma_start(
                    out=ybuf_sb[b][:], out_offset=None, in_=yeout_b,
                    in_offset=IndirectOffsetOnAxis(gidx_v[:, j:j + 1], 0),
                    bounds_check=bnd_reg, oob_is_err=False))

            @blk.sync
            def _(e, j=j, b=b):
                if j >= 2:
                    e.wait_ge(VE.h, veAD[j - 2])
                if call == 0:
                    e.wait_ge(GX[0].h, GX[0].n)
                    e.wait_ge(GX[1].h, GX[1].n)
                ldXR[j] = LDX[b].dma(e.dma_start(out=xr_sb[b][:],
                                                 in_=xres_b[j * 128:(j + 1) * 128, :]))

            @BV
            def _(e, j=j, b=b):
                e.wait_ge(GA[b].h, gdGA[j])
                e.wait_ge(LDX[b].h, ldXR[j])
                if j >= 2:
                    e.wait_ge(SDS[b].h, sdST[j - 2])
                veAD[j] = VE.one(e.scalar_tensor_tensor(
                    out_sb[b][:], ybuf_sb[b][:], pscale_v[:, j:j + 1], xr_sb[b][:],
                    ALU.mult, ALU.add))

            @BA
            def _(e, j=j, b=b):
                e.wait_ge(VE.h, veAD[j])
                dst = o_d if final else xres_b
                sdST[j] = SDS[b].dma(e.dma_start(out=dst[j * 128:(j + 1) * 128, :],
                                                 in_=out_sb[b][:]))

        if not final:
            G["sd_x1_done"] = [(SDS[0], SDS[0].n), (SDS[1], SDS[1].n)]
            if debug:
                emit_dump(G, xres_b, dbg["x1"], T, G["sd_x1_done"])


def emit_attention(G):
    nc, blk = G["nc"], G["blk"]
    nt, T, S, ST, QCH, DCN = G["nt"], G["T"], G["S"], G["ST"], G["QCH"], G["DCN"]
    LD, GD, SD, TE, VE, AE = (G[k] for k in ["LD", "GD", "SD", "TE", "VE", "AE"])
    LDX, SDS = G["LDX"], G["SDS"]
    BV, BA = G["BV"], G["BA"]
    xres_b, zsum_b, rr_d = G["xres_b"], G["zsum_b"], G["rr_d"]
    idb_sb, ones_sb, onesb_sb, wqkv_sb, bout_sb = (
        G[k] for k in ["idb_sb", "ones_sb", "onesb_sb", "wqkv_sb", "bout_sb"])
    woutT_d = G["woutT_d"]
    scr = G["scr"]
    dbg, debug = G["dbg"], G["debug"]
    x1_done = G["sd_x1_done"]

    with contextlib.ExitStack() as ph:
        E = ph.enter_context
        xlnT_sb = [E(nc.sbuf_tensor(f"a_xlnT{d}", [128, S], BF16)) for d in range(DCN)]
        attnT_sb = [E(nc.sbuf_tensor(f"a_attnT{d}", [128, S], BF16)) for d in range(DCN)]
        wout_sb = [E(nc.sbuf_tensor(f"a_wout{d}", [128, DM], BF16)) for d in range(DCN)]
        qt_sb = E(nc.sbuf_tensor("a_qt", [128, S], BF16))
        kt_sb = E(nc.sbuf_tensor("a_kt", [128, S], BF16))
        vx_sb = E(nc.sbuf_tensor("a_vx", [128, ST, 65], BF16))
        pb_sb = [E(nc.sbuf_tensor(f"a_pb{i}", [128, 1024], BF16)) for i in range(6)]
        xt_sb = [E(nc.sbuf_tensor(f"a_xt{i}", [128, DM], BF16)) for i in range(2)]
        xlnb_sb = [E(nc.sbuf_tensor(f"a_xlnb{i}", [128, DM], BF16)) for i in range(2)]
        zrow_sb = E(nc.sbuf_tensor("a_zrow", [1, (8 if NH % 8 == 0 else NH) * S], BF16))
        rrow_sb = E(nc.sbuf_tensor("a_rrow", [1, (8 if NH % 8 == 0 else NH) * S], BF16))
        zT_sb = E(nc.sbuf_tensor("a_zT", [128, (8 if NH % 8 == 0 else NH) * ST], BF16))
        rT_sb = E(nc.sbuf_tensor("a_rT", [128, (8 if NH % 8 == 0 else NH) * ST], F32))
        rTb_sb = E(nc.sbuf_tensor("a_rTb", [128, (8 if NH % 8 == 0 else NH) * ST], BF16))
        rb128_sb = E(nc.sbuf_tensor("a_rb128", [128, 128], BF16))
        xo_sb = [E(nc.sbuf_tensor(f"a_xo{i}", [128, DM], BF16)) for i in range(2)]

        gd_prior = GD.n

        @blk.gpsimd
        def _(e):
            e.wait_ge(GD.h, gd_prior)
            for sv, vv in x1_done:
                e.wait_ge(sv.h, vv)
            for d in range(DCN):
                GD.dma(e.dma_start(out=wout_sb[d][:], in_=woutT_d[d * 128:(d + 1) * 128, :]))
        gdWo = GD.n

        prev_batch_ve = G["init_ve"]
        sdXO_prev = {0: 0, 1: 0}
        for b in range(2):
            base = b * ST

            with contextlib.ExitStack() as pa:
                trb_ps = pa.enter_context(nc.psum_tensor(f"a{b}_trb", [128, 1024], BF16))
                ldX = [0] * ST
                aeS2 = [0] * ST
                veL2 = [0] * ST
                teT2 = [0] * ST
                veE2 = [0] * ST
                for t in range(ST):
                    j = base + t
                    bb = t % 2
                    sc = 32 * bb

                    @blk.sync
                    def _(e, t=t, j=j, bb=bb):
                        for sv, vv in x1_done:
                            e.wait_ge(sv.h, vv)
                        e.wait_ge(VE.h, prev_batch_ve)
                        if t >= 2:
                            e.wait_ge(VE.h, veL2[t - 2])
                        ldX[t] = LDX[bb].dma(e.dma_start(out=xt_sb[bb][:],
                                                         in_=xres_b[j * 128:(j + 1) * 128, :]))

                    @BV
                    def _(e, t=t, bb=bb, sc=sc):
                        e.wait_ge(LDX[bb].h, ldX[t])
                        e.bn_stats(scr[:, sc + 0:sc + 6], xt_sb[bb][:, 0:512])
                        e.bn_stats(scr[:, sc + 6:sc + 12], xt_sb[bb][:, 512:1024])
                        e.bn_aggr(scr[:, sc + 12:sc + 14], scr[:, sc + 0:sc + 12])
                        VE.one(e.tensor_scalar_add(scr[:, sc + 14:sc + 15],
                                                   scr[:, sc + 13:sc + 14], 1e-5))
                    vS = VE.n

                    @BA
                    def _(e, t=t, sc=sc, vS=vS):
                        e.wait_ge(VE.h, vS)
                        aeS2[t] = AE.one(e.activation(scr[:, sc + 15:sc + 16],
                                                      scr[:, sc + 14:sc + 15], AF.Sqrt))

                    @BV
                    def _(e, t=t, bb=bb, sc=sc):
                        e.wait_ge(AE.h, aeS2[t])
                        if t >= 2:
                            e.wait_ge(TE.h, teT2[t - 2])
                        e.reciprocal(scr[:, sc + 16:sc + 17], scr[:, sc + 15:sc + 16])
                        veL2[t] = VE.one(e.tensor_scalar(
                            xlnb_sb[bb][:], xt_sb[bb][:], scr[:, sc + 12:sc + 13],
                            scr[:, sc + 16:sc + 17], ALU.subtract, ALU.mult))

                    @blk.tensor
                    def _(e, t=t, bb=bb):
                        e.wait_ge(VE.h, veL2[t])
                        if t >= 1:
                            e.wait_ge(VE.h, veE2[t - 1])
                        for dc in range(DCN):
                            ins = e.transpose(trb_ps[:, dc * 128:(dc + 1) * 128],
                                              xlnb_sb[bb][:, dc * 128:(dc + 1) * 128],
                                              idb_sb[:])
                        teT2[t] = TE.one(ins)

                    @BV
                    def _(e, t=t):
                        e.wait_ge(TE.h, teT2[t])
                        for dc in range(DCN):
                            ins = e.tensor_copy(xlnT_sb[dc][:, t * 128:(t + 1) * 128],
                                                trb_ps[:, dc * 128:(dc + 1) * 128])
                        veE2[t] = VE.one(ins)
                xlnT_done = VE.n
            if G.get("stop_after") == "aln":
                return

            HPG = 8 if NH % 8 == 0 else NH
            NHG = NH // HPG
            pvEv = {}
            nmLast = xlnT_done
            veOnes_done = False
            for hg in range(NHG):
                with contextlib.ExitStack() as pb_:
                    Eb = pb_.enter_context
                    if not veOnes_done:
                        @BV
                        def _(e):
                            e.wait_ge(VE.h, xlnT_done)
                            e.wait_ge(GD.h, gdWo)
                            e.memset(vx_sb[:, :, 64:65], 1.0)
                            e.memset(qt_sb[64:128, :], 0.0)
                            VE.one(e.memset(kt_sb[64:128, :], 0.0))
                        veOnes = VE.n
                        veOnes_done = True

                    for h in range(hg * HPG, (hg + 1) * HPG):
                        p64 = (h % 2) * 64
                        ct = h // 2
                        hh = h % HPG

                        with contextlib.ExitStack() as pq:
                            qv_ps = [pq.enter_context(
                                nc.psum_tensor(f"a{b}h{h}_qv{i}", [128, 512], F32))
                                for i in range(2)]
                            teQ = []
                            veQ = []
                            qsteps = []
                            for (qo, qw) in _splits(S, 512):
                                qsteps.append(("q", qo, qw))
                                qsteps.append(("k", qo, qw))
                            for t in range(ST):
                                qsteps.append(("v", t, 0))
                            for qi, (kind, qo, qw) in enumerate(qsteps):
                                @blk.tensor
                                def _(e, h=h, p64=p64, ct=ct, kind=kind, qo=qo,
                                      qw=qw, qi=qi, nmLast=nmLast):
                                    if qi == 0:
                                        if h % HPG == 0:
                                            e.wait_ge(VE.h, nmLast)
                                        if h == 0:
                                            e.wait_ge(VE.h, veOnes)
                                        if h > 0:
                                            e.wait_ge(TE.h, G["a_pvdone"][(b, h - 1)])
                                            e.wait_ge(VE.h, G["a_pvev_last"][(b, h - 1)])
                                    if qi >= 2:
                                        e.wait_ge(VE.h, veQ[qi - 2])
                                    buf = qv_ps[qi % 2]
                                    zb = 192 * ((h % 2))
                                    if kind == "q":
                                        ins = e.matmul(buf[0:64, 0:qw],
                                                       wqkv_sb[:, zb + 0:zb + 64],
                                                       xlnT_sb[ct][:, qo:qo + qw],
                                                       start=True, stop=True)
                                    elif kind == "k":
                                        ins = e.matmul(buf[0:64, 0:qw],
                                                       wqkv_sb[:, zb + 64:zb + 128],
                                                       xlnT_sb[ct][:, qo:qo + qw],
                                                       start=True, stop=True)
                                    else:
                                        t = qo
                                        ins = e.matmul(buf[:, 0:64],
                                                       xlnT_sb[ct][:, t * 128:(t + 1) * 128],
                                                       wqkv_sb[:, zb + 128:zb + 192],
                                                       start=True, stop=True)
                                    teQ.append(TE.one(ins))

                                @BV
                                def _(e, kind=kind, qo=qo, qw=qw, qi=qi):
                                    e.wait_ge(TE.h, teQ[qi])
                                    buf = qv_ps[qi % 2]
                                    if kind == "q":
                                        ins = e.tensor_scalar_mul(
                                            qt_sb[0:64, qo:qo + qw], buf[0:64, 0:qw], 0.03125)
                                    elif kind == "k":
                                        ins = e.tensor_copy(kt_sb[0:64, qo:qo + qw],
                                                            buf[0:64, 0:qw])
                                    else:
                                        ins = e.tensor_copy(vx_sb[:, qo, 0:64],
                                                            buf[:, 0:64])
                                    veQ.append(VE.one(ins))
                            qkvDone = veQ[-1]

                        with contextlib.ExitStack() as psl:
                            s_ps = [psl.enter_context(
                                nc.psum_tensor(f"a{b}h{h}_s{i}", [128, 1024], F32))
                                for i in range(3)]
                            pv_ps = psl.enter_context(
                                nc.psum_tensor(f"a{b}h{h}_pv", [128, 1024], F32))

                            for ci, (qo, qw) in enumerate(QCH):
                                teS = [0] * ST
                                tePV = [0] * ST
                                aeP = [AE.n + kt + 1 for kt in range(ST)]

                                def emit_pv(ktp, qw=qw):
                                    @blk.tensor
                                    def _(e, ktp=ktp, qw=qw):
                                        e.wait_ge(AE.h, aeP[ktp])
                                        for (o2, w2) in _splits(qw, 512):
                                            ins = e.matmul(pv_ps[0:65, o2:o2 + w2],
                                                           vx_sb[:, ktp, :],
                                                           pb_sb[ktp % 6][:, o2:o2 + w2],
                                                           start=(ktp == 0),
                                                           stop=(ktp == ST - 1))
                                        tePV[ktp] = TE.one(ins)

                                for kt in range(ST):
                                    @blk.tensor
                                    def _(e, kt=kt, qo=qo, qw=qw, ci=ci, h=h,
                                          qkvDone=qkvDone):
                                        if kt == 0:
                                            e.wait_ge(VE.h, qkvDone)
                                            if ci > 0:
                                                e.wait_ge(VE.h, pvEv[(b, h, ci - 1)])
                                        if kt >= 3:
                                            e.wait_ge(AE.h, aeP[kt - 3])
                                        for (o2, w2) in _splits(qw, 512):
                                            ins = e.matmul(s_ps[kt % 3][:, o2:o2 + w2],
                                                           kt_sb[:, kt * 128:(kt + 1) * 128],
                                                           qt_sb[:, qo + o2:qo + o2 + w2],
                                                           start=True, stop=True)
                                        teS[kt] = TE.one(ins)

                                    if kt >= 2:
                                        emit_pv(kt - 2)
                                emit_pv(ST - 2)
                                emit_pv(ST - 1)

                                @BA
                                def _(e, qw=qw, teS=teS, tePV=tePV):
                                    for kt in range(ST):
                                        e.wait_ge(TE.h, teS[kt])
                                        if kt >= 6:
                                            e.wait_ge(TE.h, tePV[kt - 6])
                                        e.activation(pb_sb[kt % 6][:, 0:qw],
                                                     s_ps[kt % 3][:, 0:qw], AF.Exp)
                                assert AE.n == aeP[ST - 1], (AE.n, aeP[ST - 1])
                                G.setdefault("a_pvdone", {})[(b, h)] = tePV[ST - 1]

                                @BV
                                def _(e, h=h, ct=ct, p64=p64, qo=qo, qw=qw, ci=ci,
                                      hh=hh, last=tePV[ST - 1]):
                                    e.wait_ge(TE.h, last)
                                    e.tensor_copy(attnT_sb[ct][p64:p64 + 64, qo:qo + qw],
                                                  pv_ps[0:64, 0:qw])
                                    pvEv[(b, h, ci)] = VE.one(e.tensor_copy(
                                        zrow_sb[:, hh * S + qo:hh * S + qo + qw],
                                        pv_ps[64:65, 0:qw]))
                            G.setdefault("a_pvev_last", {})[(b, h)] = pvEv[(b, h, len(QCH) - 1)]
                    zEvAll = VE.n

                with contextlib.ExitStack() as pn:
                    En = pn.enter_context
                    tr2_ps = En(nc.psum_tensor(f"a{b}g{hg}_tr2", [128, 128], F32))
                    bc_ps = [En(nc.psum_tensor(f"a{b}g{hg}_bc{i}", [64, 1024], F32))
                             for i in range(2)]
                    zreg = zsum_b[:, hg * HPG * S:(hg + 1) * HPG * S]
                    NCOL = HPG * ST

                    @blk.gpsimd
                    def _(e, zreg=zreg):
                        e.wait_ge(VE.h, zEvAll)
                        e.wait_ge(LD.h, LD.n)
                        GD.dma(e.dma_start(out=zreg, in_=zrow_sb[:]))
                    gdZ = GD.n

                    @blk.sync
                    def _(e, zreg=zreg):
                        e.wait_ge(GD.h, gdZ)
                        LD.dma(e.dma_start(
                            out=zT_sb[:],
                            in_=zreg.rearrange("one (r c) -> (one r) c", c=128),
                            transpose=True))
                    ldZ = LD.n

                    @BV
                    def _(e):
                        e.wait_ge(LD.h, ldZ)
                        e.reciprocal(rT_sb[:], zT_sb[:])
                        VE.one(e.tensor_copy(rTb_sb[:], rT_sb[:]))
                    veR = VE.n

                    teT2r = []

                    @blk.tensor
                    def _(e, NCOL=NCOL):
                        e.wait_ge(VE.h, veR)
                        teT2r.append(TE.one(e.matmul(
                            tr2_ps[0:NCOL, 0:128], rTb_sb[:, 0:NCOL], idb_sb[:],
                            start=True, stop=True)))

                    @BV
                    def _(e, NCOL=NCOL, gprior=GD.n):
                        e.wait_ge(TE.h, teT2r[0])
                        e.wait_ge(GD.h, gprior)
                        VE.one(e.tensor_copy(rb128_sb[0:NCOL, :],
                                             tr2_ps[0:NCOL, 0:128]))
                    veT2r = VE.n

                    @blk.gpsimd
                    def _(e, NCOL=NCOL, lprior=LD.n):
                        e.wait_ge(VE.h, veT2r)
                        e.wait_ge(LD.h, lprior)
                        GD.dma(e.dma_start(
                            out=rr_d.rearrange("one p q -> (one p) q")[0:NCOL, :],
                            in_=rb128_sb[0:NCOL, :]))
                    gdRR = GD.n

                    @blk.sync
                    def _(e, NCOL=NCOL):
                        e.wait_ge(GD.h, gdRR)
                        LD.dma(e.dma_start(
                            out=rrow_sb[0:1, 0:NCOL * 128],
                            in_=rr_d.rearrange("one p q -> one (p q)")[0:1, 0:NCOL * 128]))
                    ldRR = LD.n

                    teBCn = []
                    veNM = []
                    for h in range(hg * HPG, (hg + 1) * HPG):
                        p64 = (h % 2) * 64
                        ct = h // 2
                        hh = h % HPG
                        for (qo, qw) in _splits(S, 1024):
                            @blk.tensor
                            def _(e, hh=hh, qo=qo, qw=qw, idx=len(teBCn)):
                                e.wait_ge(LD.h, ldRR)
                                if idx >= 2:
                                    e.wait_ge(VE.h, veNM[idx - 2])
                                for (o2, w2) in _splits(qw, 512):
                                    ins = e.matmul(
                                        bc_ps[idx % 2][:, o2:o2 + w2], onesb_sb[0:1, 0:64],
                                        rrow_sb[:, hh * S + qo + o2:hh * S + qo + o2 + w2],
                                        start=True, stop=True)
                                teBCn.append(TE.one(ins))

                            @BV
                            def _(e, ct=ct, p64=p64, qo=qo, qw=qw, idx=len(veNM)):
                                e.wait_ge(TE.h, teBCn[idx])
                                veNM.append(VE.one(e.tensor_mul(
                                    attnT_sb[ct][p64:p64 + 64, qo:qo + qw],
                                    attnT_sb[ct][p64:p64 + 64, qo:qo + qw],
                                    bc_ps[idx % 2][:, 0:qw])))
                    nmLast = veNM[-1]
            nmDone = nmLast

            with contextlib.ExitStack() as po:
                op_ps = [po.enter_context(
                    nc.psum_tensor(f"a{b}_op{i}", [128, 1024], F32)) for i in range(2)]
                ldXO = [0] * ST
                teOP = [0] * ST
                veAO = [0] * ST
                sdXO = [0] * ST
                for t in range(ST):
                    j = base + t
                    bb = t % 2

                    @blk.sync
                    def _(e, t=t, j=j, bb=bb):
                        e.wait_ge(VE.h, nmDone)
                        if t >= 2:
                            e.wait_ge(VE.h, veAO[t - 2])
                        ldXO[t] = LDX[bb].dma(e.dma_start(out=xt_sb[bb][:],
                                                          in_=xres_b[j * 128:(j + 1) * 128, :]))

                    @blk.tensor
                    def _(e, t=t, bb=bb):
                        e.wait_ge(VE.h, nmDone)
                        e.wait_ge(GD.h, gdWo)
                        if t >= 2:
                            e.wait_ge(VE.h, veAO[t - 2])
                        for (do, dw) in _splits(DM, 512):
                            for dc in range(DCN):
                                e.matmul(op_ps[bb][:, do:do + dw],
                                         attnT_sb[dc][:, t * 128:(t + 1) * 128],
                                         wout_sb[dc][:, do:do + dw],
                                         start=(dc == 0), stop=False)
                            ins = e.matmul(op_ps[bb][:, do:do + dw], onesb_sb[:],
                                           bout_sb[:, do:do + dw],
                                           start=False, stop=True)
                        teOP[t] = TE.one(ins)

                    @BV
                    def _(e, t=t, bb=bb):
                        e.wait_ge(TE.h, teOP[t])
                        e.wait_ge(LDX[bb].h, ldXO[t])
                        e.wait_ge(SDS[bb].h, sdXO[t - 2] if t >= 2 else sdXO_prev[bb])
                        veAO[t] = VE.one(e.tensor_add(xo_sb[bb][:], xt_sb[bb][:],
                                                      op_ps[bb][:]))

                    @BA
                    def _(e, t=t, j=j, bb=bb):
                        e.wait_ge(VE.h, veAO[t])
                        sdXO[t] = SDS[bb].dma(e.dma_start(out=xres_b[j * 128:(j + 1) * 128, :],
                                                          in_=xo_sb[bb][:]))

            prev_batch_ve = VE.n
            for bb2 in range(2):
                last = [sdXO[t2] for t2 in range(ST) if t2 % 2 == bb2]
                if last:
                    sdXO_prev[bb2] = last[-1]
        G["sd_xres_done"] = [(SDS[0], SDS[0].n), (SDS[1], SDS[1].n)]
        if debug:
            emit_dump(G, xres_b, dbg["x2"], T, G["sd_xres_done"])


def kernel(**inputs):
    x = np.asarray(inputs["L0_attn_x"], np.float32)
    Wq = np.asarray(inputs["L0_attn_Wq"], np.float32)
    Wk = np.asarray(inputs["L0_attn_Wk"], np.float32)
    Wv = np.asarray(inputs["L0_attn_Wv"], np.float32)
    Wout = np.asarray(inputs["L0_attn_W_out"], np.float32)
    bout = np.asarray(inputs["L0_attn_b_out"], np.float32)
    Wr = np.asarray(inputs["L0_ffn_Wr"], np.float32)
    W1 = np.asarray(inputs["L0_ffn_W1"], np.float32)
    W2 = np.asarray(inputs["L0_ffn_W2"], np.float32)

    nt = NT
    T = 128 * nt
    nc = build(nt=nt, ppad=PPAD, debug=False)
    in_maps = make_inputs(x, Wq, Wk, Wv, Wout, bout, Wr, W1, W2, nt)
    res = run_bass_kernel_spmd(nc, in_maps, core_ids=list(range(NCORE)))
    out = np.stack([r["o"].reshape(2, T // 2, DM) for r in res.results], axis=0)
    return np.ascontiguousarray(out.reshape(16, T // 2, DM)).astype(np.float32)


def make_inputs(x, Wq, Wk, Wv, Wout, bout, Wr, W1, W2, nt):
    T = 128 * nt
    w3 = np.concatenate([Wq.T, Wk.T, Wv.T], axis=1)
    z = np.zeros_like(w3)
    wqkv = np.stack([np.concatenate([w3, z], axis=0),
                     np.concatenate([z, w3], axis=0)])
    ngrp = FF // 256
    w1p = np.ascontiguousarray(
        W1.reshape(NEXP, DM // 128, 128, ngrp, 256).transpose(0, 3, 2, 1, 4)
    ).reshape(NEXP, ngrp, 128, (DM // 128) * 256)
    in_common = dict(
        wr=np.ascontiguousarray(Wr),
        w1=w1p,
        w2=np.ascontiguousarray(W2),
        wqkv=np.ascontiguousarray(wqkv),
        woutT=np.ascontiguousarray(Wout.T),
        bout=np.ascontiguousarray(bout.reshape(1, DM)),
        iota8=np.tile(np.arange(NEXP, dtype=np.float32), (128, 1)),
        tri=(np.arange(128)[:, None] < np.arange(128)[None, :]).astype(np.float32),
        idf=np.eye(128, dtype=np.float32),
        idb=np.eye(128).astype(ml_dtypes.bfloat16),
        onesin=np.ones((128, 128), np.float32),
    )
    in_maps = []
    for c in range(NCORE):
        m = dict(in_common)
        m["x"] = np.ascontiguousarray(x[2 * c:2 * c + 2].reshape(T, DM))
        m["pmask"] = (np.arange(NCORE) < c).astype(np.float32).reshape(NCORE, 1)
        in_maps.append(m)
    return in_maps


# version 10
# speedup vs baseline: 2.0751x; 1.0494x over previous
import contextlib
import numpy as np
import ml_dtypes
import concourse.bass as bass
import concourse.mybir as mybir
from concourse.bass import IndirectOffsetOnAxis
from concourse.bass_utils import run_bass_kernel_spmd

F32 = mybir.dt.float32
BF16 = mybir.dt.bfloat16
U32 = mybir.dt.uint32
AF = mybir.ActivationFunctionType
ALU = mybir.AluOpType
AX = mybir.AxisListType

NCORE = 8
DM = 1024
NH = 16
DH = 64
NEXP = 8
FF = 4096
BIGF = 1.0e7

NT = 32
PPAD = 768
DEBUG = False

RG = [list(range(NCORE))]


def _splits(total, step=512):
    out, o = [], 0
    while o < total:
        out.append((o, min(step, total - o)))
        o += step
    return out


class Sem:
    def __init__(self, nc, es, name):
        self.h = es.enter_context(nc.semaphore(name))
        self.n = 0
        self.last_chained = None

    def dma(self, instr):
        instr.then_inc(self.h, 16)
        self.n += 16
        return self.n

    def one(self, instr):
        if instr is self.last_chained:
            return self.n
        instr.then_inc(self.h, 1)
        self.n += 1
        return self.n


class Chain:
    PASSTHRU = ("wait_ge", "dma_start", "indirect_dma_start",
                "collective_compute", "engine_nop")

    def __init__(self, e, sem):
        self._e = e
        self._s = sem

    def __getattr__(self, name):
        fn = getattr(self._e, name)
        if name in Chain.PASSTHRU or not callable(fn):
            return fn
        s = self._s
        raw = self._e

        def wrapped(*a, **k):
            raw.wait_ge(s.h, s.n)
            ins = fn(*a, **k)
            ins.then_inc(s.h, 1)
            s.n += 1
            s.last_chained = ins
            return ins
        return wrapped


def build(nt=None, ppad=None, debug=None, stop_after=None):
    nt = NT if nt is None else nt
    ppad = PPAD if ppad is None else ppad
    debug = DEBUG if debug is None else debug

    T = 128 * nt
    S = T // 2
    ST = S // 128
    CAP = T
    PSUB = ppad // 128
    NSORT = NEXP * ppad
    FCN = FF // 128
    DCN = DM // 128
    QCH = _splits(S, 1024)

    nc = bass.Bass("TRN2", target_bir_lowering=False, debug=False)

    x_d = nc.dram_tensor("x", [T, DM], F32, kind="ExternalInput").ap()
    wr_d = nc.dram_tensor("wr", [DM, NEXP], F32, kind="ExternalInput").ap()
    w1_d = nc.dram_tensor("w1", [NEXP, FF // 256, 128, (DM // 128) * 256], F32, kind="ExternalInput").ap()
    w2_d = nc.dram_tensor("w2", [NEXP, FF, DM], F32, kind="ExternalInput").ap()
    wqkv_d = nc.dram_tensor("wqkv", [2, 128, 3 * DH], F32, kind="ExternalInput").ap()
    woutT_d = nc.dram_tensor("woutT", [DM, DM], F32, kind="ExternalInput").ap()
    bout_d = nc.dram_tensor("bout", [1, DM], F32, kind="ExternalInput").ap()
    iota8_d = nc.dram_tensor("iota8", [128, NEXP], F32, kind="ExternalInput").ap()
    tri_d = nc.dram_tensor("tri", [128, 128], F32, kind="ExternalInput").ap()
    idf_d = nc.dram_tensor("idf", [128, 128], F32, kind="ExternalInput").ap()
    idb_d = nc.dram_tensor("idb", [128, 128], BF16, kind="ExternalInput").ap()
    pmask_d = nc.dram_tensor("pmask", [NCORE, 1], F32, kind="ExternalInput").ap()
    ones_d = nc.dram_tensor("onesin", [128, 128], F32, kind="ExternalInput").ap()

    o_d = nc.dram_tensor("o", [T, DM], F32, kind="ExternalOutput").ap()
    dbg = {}
    if debug:
        for nm, shp, dt in [("eidx", [128, nt], F32), ("p", [128, nt], F32),
                            ("lrank", [128, nt], F32), ("pscale", [128, nt], F32),
                            ("gidx", [128, nt], U32), ("sidx", [128, nt], U32),
                            ("hist", [NCORE, NEXP], F32),
                            ("x1", [T, DM], BF16), ("x2", [T, DM], BF16)]:
            dbg[nm] = nc.dram_tensor("dbg_" + nm, shp, dt, kind="ExternalOutput").ap()

    xres_b = nc.dram_tensor("xres_b", [T, DM], BF16).ap()
    sort_b = nc.dram_tensor("sort_b", [NSORT, DM], BF16).ap()
    yeout_b = nc.dram_tensor("yeout_b", [NSORT, DM], BF16).ap()
    zsum_b = nc.dram_tensor("zsum_b", [1, NH * S], BF16).ap()
    rr_d = nc.dram_tensor("rr_d", [1, 128, 128], BF16).ap()
    ag_in = [nc.dram_tensor(f"ag_in{k}", [1, NEXP], F32).ap() for k in range(2)]
    ag_out = [nc.dram_tensor(f"ag_out{k}", [NCORE, NEXP], F32).ap() for k in range(2)]

    es = contextlib.ExitStack()
    with es:
        E = es.enter_context
        cpool = E(nc.sbuf_tensor("cpool", [128, 8], F32))
        iota8_sb = E(nc.sbuf_tensor("iota8_sb", [128, NEXP], F32))
        tri_sb = E(nc.sbuf_tensor("tri_sb", [128, 128], F32))
        idf_sb = E(nc.sbuf_tensor("idf_sb", [128, 128], F32))
        idb_sb = E(nc.sbuf_tensor("idb_sb", [128, 128], BF16))
        ones_sb = E(nc.sbuf_tensor("ones_sb", [128, 128], F32))
        onesb_sb = E(nc.sbuf_tensor("onesb_sb", [1, 128], BF16))
        pmask_sb = E(nc.sbuf_tensor("pmask_sb", [NCORE, 1], F32))
        wr_sb = E(nc.sbuf_tensor("wr_sb", [128, DCN * NEXP], F32))
        wqkv_sb = E(nc.sbuf_tensor("wqkv_sb", [128, 2 * 3 * DH], BF16))
        bout_sb = E(nc.sbuf_tensor("bout_sb", [1, DM], BF16))
        agout_sb = [E(nc.sbuf_tensor(f"agout_sb{k}", [NCORE, NEXP], F32)) for k in range(2)]
        p_v = E(nc.sbuf_tensor("p_v", [128, nt], F32))
        eidx_v = E(nc.sbuf_tensor("eidx_v", [128, nt], F32))
        lrank_v = E(nc.sbuf_tensor("lrank_v", [128, nt], F32))
        pscale_v = E(nc.sbuf_tensor("pscale_v", [128, nt], F32))
        sidx_v = E(nc.sbuf_tensor("sidx_v", [128, nt], U32))
        gidx_v = E(nc.sbuf_tensor("gidx_v", [128, nt], U32))
        oh_v = E(nc.sbuf_tensor("oh_v", [128, NEXP * nt], F32))
        lgall = E(nc.sbuf_tensor("lgall", [128, NEXP * nt], F32))
        exall = E(nc.sbuf_tensor("exall", [128, NEXP * nt], F32))
        mx8all = E(nc.sbuf_tensor("mx8all", [128, NEXP * nt], F32))
        pref_v = E(nc.sbuf_tensor("pref_v", [128, NEXP * nt], F32))
        totrow = E(nc.sbuf_tensor("totrow", [1, NEXP * nt], F32))
        runrow = E(nc.sbuf_tensor("runrow", [1, NEXP * (nt + 1)], F32))
        combrow = E(nc.sbuf_tensor("combrow", [1, NEXP * nt], F32))
        offs_row = E(nc.sbuf_tensor("offs_row", [1, NEXP], F32))
        hist_row = E(nc.sbuf_tensor("hist_row", [1, NEXP], F32))
        scr = E(nc.sbuf_tensor("scr", [128, 128], F32))
        scru = E(nc.sbuf_tensor("scru", [128, 16], U32))
        dump_sb = E(nc.sbuf_tensor("dump_sb", [128, DM], BF16)) if debug else None

        LD = Sem(nc, es, "LD")
        GD = Sem(nc, es, "GD")
        SD = Sem(nc, es, "SD")
        CC = Sem(nc, es, "CC")
        TE = Sem(nc, es, "TE")
        VE = Sem(nc, es, "VE")
        AE = Sem(nc, es, "AE")
        LDX = [Sem(nc, es, f"LDX{i}") for i in range(4)]
        XE = [Sem(nc, es, f"XE{i}") for i in range(2)]
        SC = [Sem(nc, es, f"SC{i}") for i in range(2)]
        GX = [Sem(nc, es, f"GX{i}") for i in range(2)]
        GA = [Sem(nc, es, f"GA{i}") for i in range(2)]
        SDY = [Sem(nc, es, f"SDY{i}") for i in range(2)]
        SDS = [Sem(nc, es, f"SDS{i}") for i in range(2)]
        W1D = [Sem(nc, es, f"W1D{i}") for i in range(2)]
        W2D = Sem(nc, es, "W2D")

        for cv, ci in [(0.0, 0), (1.0, 1), (0.5, 2), (0.03125, 3), (-0.5, 4)]:
            nc.const_aps.register(F32, cv, cpool[:, ci:ci + 1])

        blk = E(nc.Block())

        def BV(f):
            blk.vector(lambda e: f(Chain(e, VE)))

        def BA(f):
            blk.scalar(lambda e: f(Chain(e, AE)))

        @blk.sync
        def _(e):
            for dst, src in [(iota8_sb, iota8_d), (tri_sb, tri_d), (idf_sb, idf_d),
                             (idb_sb, idb_d), (ones_sb, ones_d), (pmask_sb, pmask_d)]:
                LD.dma(e.dma_start(out=dst[:], in_=src[:]))
            for dc in range(DCN):
                LD.dma(e.dma_start(out=wr_sb[:, dc * NEXP:(dc + 1) * NEXP],
                                   in_=wr_d[dc * 128:(dc + 1) * 128, :]))
        init_ld = LD.n

        bnd_reg = nc.gpsimd.alloc_register("bnd")

        @blk.gpsimd
        def _(e):
            e.reg_mov(bnd_reg, NSORT - 1)
            GD.dma(e.dma_start(out=wqkv_sb[:, 0:192], in_=wqkv_d[0]))
            GD.dma(e.dma_start(out=wqkv_sb[:, 192:384], in_=wqkv_d[1]))
            GD.dma(e.dma_start(out=bout_sb[:], in_=bout_d[:]))
        init_gd = GD.n

        @BV
        def _(e):
            for cv, ci in [(0.0, 0), (1.0, 1), (0.5, 2), (0.03125, 3), (-0.5, 4)]:
                e.memset(cpool[:, ci:ci + 1], cv)
            e.memset(runrow[:, 0:NEXP], 0.0)
            e.wait_ge(LD.h, init_ld)
            VE.one(e.tensor_copy(onesb_sb[:], ones_sb[0:1, :]))
        init_ve = VE.n

        G = dict(nc=nc, blk=blk, nt=nt, ppad=ppad, debug=debug, dbg=dbg,
                 T=T, S=S, ST=ST, CAP=CAP, PSUB=PSUB, NSORT=NSORT, FCN=FCN,
                 DCN=DCN, QCH=QCH,
                 x_d=x_d, o_d=o_d, w1_d=w1_d, w2_d=w2_d, woutT_d=woutT_d,
                 xres_b=xres_b, sort_b=sort_b, yeout_b=yeout_b, zsum_b=zsum_b,
                 rr_d=rr_d,
                 ag_in=ag_in, ag_out=ag_out, agout_sb=agout_sb,
                 cpool=cpool, iota8_sb=iota8_sb, tri_sb=tri_sb, idf_sb=idf_sb,
                 idb_sb=idb_sb, ones_sb=ones_sb, onesb_sb=onesb_sb,
                 pmask_sb=pmask_sb, wr_sb=wr_sb, wqkv_sb=wqkv_sb, bout_sb=bout_sb,
                 p_v=p_v, eidx_v=eidx_v, lrank_v=lrank_v, pscale_v=pscale_v,
                 sidx_v=sidx_v, gidx_v=gidx_v, oh_v=oh_v, pref_v=pref_v,
                 totrow=totrow, runrow=runrow, combrow=combrow, offs_row=offs_row,
                 hist_row=hist_row, scr=scr, scru=scru,
                 lgall=lgall, exall=exall, mx8all=mx8all,
                 dump_sb=dump_sb,
                 LD=LD, GD=GD, SD=SD, CC=CC, TE=TE, VE=VE, AE=AE,
                 LDX=LDX, XE=XE, SC=SC, GX=GX, GA=GA, SDY=SDY, SDS=SDS,
                 W1D=W1D, W2D=W2D, BV=BV, BA=BA, bnd_reg=bnd_reg,
                 init_ld=init_ld, init_gd=init_gd, init_ve=init_ve)

        G["stop_after"] = stop_after
        emit_ffn(G, 0)
        if stop_after not in ("p1", "p2", "ffn0"):
            emit_attention(G)
            if stop_after not in ("attn", "aln", "ahead", "anorm"):
                emit_ffn(G, 1)

        @blk.gpsimd
        def _(e):
            for s in [SD, GD, LD, W2D] + W1D + LDX + XE + SC + GX + GA + SDY + SDS:
                e.wait_ge(s.h, s.n)
    return nc


def emit_dump(G, src_dram, dst_dram, nrows, waits):
    nc, blk, LD, SD = G["nc"], G["blk"], G["LD"], G["SD"]
    BA = G["BA"]
    bsb = G["dump_sb"]
    sd_pre = SD.n
    lds = []
    sds = []
    for r in range(nrows // 128):
        @blk.sync
        def _(e, r=r):
            for sv, vv in waits:
                e.wait_ge(sv.h, vv)
            e.wait_ge(SD.h, sds[r - 1] if r >= 1 else sd_pre)
            lds.append(LD.dma(e.dma_start(out=bsb[:],
                                          in_=src_dram[r * 128:(r + 1) * 128, :])))

        @BA
        def _(e, r=r):
            e.wait_ge(LD.h, lds[r])
            sds.append(SD.dma(e.dma_start(out=dst_dram[r * 128:(r + 1) * 128, :],
                                          in_=bsb[:])))


def emit_ffn(G, call):
    nc, blk = G["nc"], G["blk"]
    nt, ppad, PSUB, NSORT = G["nt"], G["ppad"], G["PSUB"], G["NSORT"]
    T, CAP, FCN, DCN = G["T"], G["CAP"], G["FCN"], G["DCN"]
    LD, GD, SD, CC, TE, VE, AE = (G[k] for k in ["LD", "GD", "SD", "CC", "TE", "VE", "AE"])
    LDX, XE, SC, GX, GA, SDY, SDS = (G[k] for k in ["LDX", "XE", "SC", "GX", "GA", "SDY", "SDS"])
    W1D, W2D = G["W1D"], G["W2D"]
    BV, BA = G["BV"], G["BA"]
    (x_d, xres_b, sort_b, yeout_b, o_d, w1_d, w2_d, ag_in, ag_out, agout_sb) = (
        G[k] for k in ["x_d", "xres_b", "sort_b", "yeout_b", "o_d", "w1_d", "w2_d",
                       "ag_in", "ag_out", "agout_sb"])
    (iota8_sb, tri_sb, idf_sb, idb_sb, ones_sb, pmask_sb, wr_sb) = (
        G[k] for k in ["iota8_sb", "tri_sb", "idf_sb", "idb_sb", "ones_sb",
                       "pmask_sb", "wr_sb"])
    (p_v, eidx_v, lrank_v, pscale_v, sidx_v, gidx_v, oh_v, pref_v) = (
        G[k] for k in ["p_v", "eidx_v", "lrank_v", "pscale_v", "sidx_v", "gidx_v",
                       "oh_v", "pref_v"])
    (totrow, runrow, combrow, offs_row, hist_row, scr, scru) = (
        G[k] for k in ["totrow", "runrow", "combrow", "offs_row", "hist_row",
                       "scr", "scru"])
    dbg, debug = G["dbg"], G["debug"]
    bnd_reg = G["bnd_reg"]
    final = (call == 1)
    src_fp32 = (call == 0)
    xsrc = x_d if src_fp32 else xres_b

    with contextlib.ExitStack() as ph:
        E = ph.enter_context
        ybuf_sb = [E(nc.sbuf_tensor(f"c{call}_yb{i}", [128, DM], BF16)) for i in range(2)]
        xr_sb = [E(nc.sbuf_tensor(f"c{call}_xr{i}", [128, DM], BF16)) for i in range(2)]
        out_sb = [E(nc.sbuf_tensor(f"c{call}_out{i}", [128, DM], F32 if final else BF16))
                  for i in range(2)]
        p1s = ph.enter_context(contextlib.ExitStack())
        E1 = p1s.enter_context
        if src_fp32:
            xf_sb = [E1(nc.sbuf_tensor(f"c{call}_xf{i}", [128, DM], F32)) for i in range(4)]
        else:
            xf_sb = None
        xb_sb = [E1(nc.sbuf_tensor(f"c{call}_xb{i}", [128, DM], BF16)) for i in range(4)]
        xln_sb = [E1(nc.sbuf_tensor(f"c{call}_xln{i}", [128, DM], F32)) for i in range(4)]
        xlnb_sb = [E1(nc.sbuf_tensor(f"c{call}_xlnb{i}", [128, DM], BF16)) for i in range(nt)]
        xlnT_sb = [E1(nc.sbuf_tensor(f"c{call}_xlnT{i}", [128, DM], F32)) for i in range(2)]
        zero_sb = E1(nc.sbuf_tensor(f"c{call}_zero", [128, DM], BF16))
        tr_ps = [E1(nc.psum_tensor(f"c{call}_tr{i}", [128, DM], F32)) for i in range(2)]
        lg_ps = E1(nc.psum_tensor(f"c{call}_lg", [128, NEXP], F32))

        ldA = [0] * nt
        aeSQ = [0] * nt
        veLN = [0] * nt
        teTR = [0] * nt
        aeTR = [0] * nt
        teLG = [0] * nt
        aeEX = [0] * nt
        veOH = [0] * nt
        tePR = [0] * nt
        veSX = [0] * nt
        gdSC = [0] * nt
        gdXR = [0] * nt

        sd_prefill = SD.n
        if call == 0:
            @BV
            def _(e):
                VE.one(e.memset(zero_sb[:], 0.0))
            zv = VE.n

            @BA
            def _(e):
                e.wait_ge(VE.h, zv)
                for r in range(NSORT // 128):
                    SD.dma(e.dma_start(out=sort_b[r * 128:(r + 1) * 128, :], in_=zero_sb[:]))
            sd_prefill = SD.n

        veTR = [0] * nt
        veLG = [0] * nt
        lgall = G["lgall"]
        exall = G["exall"]
        mx8all = G["mx8all"]
        preB_ps = E1(nc.psum_tensor(f"c{call}_preB", [128, 32], F32))
        totB_ps = E1(nc.psum_tensor(f"c{call}_totB", [1, 32], F32))
        bc_ps = E1(nc.psum_tensor(f"c{call}_bc", [128, 32], F32))

        def stage_a(j):
            b4 = j % 4
            xin = xf_sb[b4] if src_fp32 else xb_sb[b4]
            sc = 32 * b4

            @blk.sync
            def _(e, j=j, b4=b4, xin=xin):
                e.wait_ge(VE.h, G["init_ve"])
                if j >= 4:
                    e.wait_ge(VE.h, veLN[j - 4])
                    if src_fp32:
                        e.wait_ge(GX[j % 2].h, gdXR[j - 4])
                if not src_fp32:
                    for sv, vv in G["sd_xres_done"]:
                        e.wait_ge(sv.h, vv)
                ldA[j] = LDX[j % 4].dma(e.dma_start(out=xin[:],
                                                    in_=xsrc[j * 128:(j + 1) * 128, :]))

            @BV
            def _(e, j=j, xin=xin, sc=sc):
                e.wait_ge(LDX[j % 4].h, ldA[j])
                e.bn_stats(scr[:, sc + 0:sc + 6], xin[:, 0:512])
                e.bn_stats(scr[:, sc + 6:sc + 12], xin[:, 512:1024])
                e.bn_aggr(scr[:, sc + 12:sc + 14], scr[:, sc + 0:sc + 12])
                VE.one(e.tensor_scalar_add(scr[:, sc + 14:sc + 15],
                                           scr[:, sc + 13:sc + 14], 1e-5))
            vStat = VE.n

            @BA
            def _(e, j=j, sc=sc, vStat=vStat):
                e.wait_ge(VE.h, vStat)
                e.activation(scr[:, sc + 15:sc + 16], scr[:, sc + 14:sc + 15], AF.Ln)
                aeSQ[j] = AE.one(e.activation(scr[:, sc + 16:sc + 17],
                                              scr[:, sc + 15:sc + 16], AF.Exp,
                                              scale=-0.5))

            @BV
            def _(e, j=j, b4=b4, xin=xin, sc=sc):
                e.wait_ge(AE.h, aeSQ[j])
                if j >= 4:
                    e.wait_ge(TE.h, teTR[j - 4])
                e.tensor_scalar(xln_sb[b4][:], xin[:], scr[:, sc + 12:sc + 13],
                                scr[:, sc + 16:sc + 17], ALU.subtract, ALU.mult)
                veLN[j] = VE.one(e.tensor_copy(xlnb_sb[j][:], xln_sb[b4][:]))

        def stage_b(j):
            b4 = j % 4
            b2 = j % 2

            @blk.tensor
            def _(e, j=j, b4=b4, b2=b2):
                e.wait_ge(VE.h, veLN[j])
                if j >= 2:
                    e.wait_ge(VE.h, veTR[j - 2])
                for dc in range(DCN):
                    ins = e.transpose(tr_ps[b2][:, dc * 128:(dc + 1) * 128],
                                      xln_sb[b4][:, dc * 128:(dc + 1) * 128], idf_sb[:])
                teTR[j] = TE.one(ins)

            @BV
            def _(e, j=j, b2=b2):
                e.wait_ge(TE.h, teTR[j])
                veTR[j] = VE.one(e.tensor_copy(xlnT_sb[b2][:], tr_ps[b2][:]))

        def stage_c(j):
            b2 = j % 2

            @blk.tensor
            def _(e, j=j, b2=b2):
                e.wait_ge(VE.h, veTR[j])
                if j >= 1:
                    e.wait_ge(VE.h, veLG[j - 1])
                for dc in range(DCN):
                    ins = e.matmul(lg_ps[:], xlnT_sb[b2][:, dc * 128:(dc + 1) * 128],
                                   wr_sb[:, dc * NEXP:(dc + 1) * NEXP],
                                   start=(dc == 0), stop=(dc == DCN - 1))
                teLG[j] = TE.one(ins)

            @BV
            def _(e, j=j):
                e.wait_ge(TE.h, teLG[j])
                veLG[j] = VE.one(e.tensor_copy(lgall[:, j * NEXP:(j + 1) * NEXP],
                                               lg_ps[:]))

            if src_fp32:
                @blk.gpsimd
                def _(e, j=j):
                    e.wait_ge(VE.h, veLN[j])
                    if j >= 2:
                        e.wait_ge(GX[j % 2].h, gdXR[j - 2])
                    gdXR[j] = GX[j % 2].dma(e.dma_start(
                        out=xres_b[j * 128:(j + 1) * 128, :], in_=xf_sb[j % 4][:]))

        for step in range(nt + 3):
            if step < nt:
                stage_a(step)
            if 0 <= step - 2 < nt:
                stage_b(step - 2)
            if 0 <= step - 3 < nt:
                stage_c(step - 3)

        @BA
        def _(e):
            e.wait_ge(VE.h, veLG[nt - 1])
            AE.one(e.activation(exall[:], lgall[:], AF.Exp))
        aeEXall = AE.n

        nblk = nt // 4
        veOH4 = [0] * nblk
        veSX4 = [0] * nblk
        tePR4 = [0] * nblk
        teBC4 = [0] * nblk
        veRR4 = [0] * nblk
        for B in range(nblk):
            @BV
            def _(e, B=B):
                if B == 0:
                    e.wait_ge(AE.h, aeEXall)
                for j in range(4 * B, 4 * B + 4):
                    bb8 = 8 * (j % 2)
                    e.max(mx8all[:, j * NEXP:(j + 1) * NEXP],
                          exall[:, j * NEXP:(j + 1) * NEXP])
                    e.max_index(scru[:, bb8:bb8 + 8], mx8all[:, j * NEXP:(j + 1) * NEXP],
                                exall[:, j * NEXP:(j + 1) * NEXP])
                    e.tensor_copy(eidx_v[:, j:j + 1], scru[:, bb8:bb8 + 1])
                    e.tensor_single_scalar(oh_v[:, j * NEXP:(j + 1) * NEXP],
                                           iota8_sb[:], eidx_v[:, j:j + 1], ALU.is_equal)
                s0 = 32 * B
                e.tensor_reduce(scr[:, 0:4].rearrange("p (g one) -> p g one", one=1),
                                exall[:, s0:s0 + 32].rearrange("p (g x) -> p g x", x=8),
                                axis=AX.X, op=ALU.add)
                e.reciprocal(scr[:, 4:8], scr[:, 0:4])
                VE.one(e.tensor_tensor(
                    p_v[:, 4 * B:4 * B + 4],
                    mx8all[:, s0:s0 + 32].rearrange("p (g x) -> p g x", x=8)[:, :, 0],
                    scr[:, 4:8], ALU.mult))
            veOH4[B] = VE.n

            @blk.tensor
            def _(e, B=B):
                e.wait_ge(VE.h, veOH4[B])
                if B >= 1:
                    e.wait_ge(VE.h, veRR4[B - 1])
                e.matmul(preB_ps[:], tri_sb[:], oh_v[:, 32 * B:32 * B + 32],
                         start=True, stop=True)
                tePR4[B] = TE.one(e.matmul(totB_ps[:], ones_sb[:, 0:1],
                                           oh_v[:, 32 * B:32 * B + 32],
                                           start=True, stop=True))

            @BV
            def _(e, B=B):
                e.wait_ge(TE.h, tePR4[B])
                e.tensor_copy(pref_v[:, 32 * B:32 * B + 32], preB_ps[:])
                e.tensor_copy(totrow[:, 32 * B:32 * B + 32], totB_ps[:])
                last = None
                for j in range(4 * B, 4 * B + 4):
                    last = e.tensor_add(runrow[:, (j + 1) * NEXP:(j + 2) * NEXP],
                                        runrow[:, j * NEXP:(j + 1) * NEXP],
                                        totrow[:, j * NEXP:(j + 1) * NEXP])
                VE.one(last)
            veRR4[B] = VE.n

            @blk.tensor
            def _(e, B=B):
                e.wait_ge(VE.h, veRR4[B])
                if B >= 1:
                    e.wait_ge(VE.h, veSX4[B - 1])
                teBC4[B] = TE.one(e.matmul(bc_ps[:], ones_sb[0:1, :],
                                           runrow[:, 32 * B:32 * B + 32],
                                           start=True, stop=True))

            @BV
            def _(e, B=B):
                e.wait_ge(TE.h, teBC4[B])
                s0 = 32 * B
                e.tensor_add(pref_v[:, s0:s0 + 32], pref_v[:, s0:s0 + 32], bc_ps[:])
                e.tensor_mul(scr[:, 32:64], oh_v[:, s0:s0 + 32], pref_v[:, s0:s0 + 32])
                e.tensor_reduce(lrank_v[:, 4 * B:4 * B + 4].rearrange(
                                    "p (g one) -> p g one", one=1),
                                scr[:, 32:64].rearrange("p (g x) -> p g x", x=8),
                                axis=AX.X, op=ALU.add)
                e.tensor_single_scalar(scr[:, 0:4], lrank_v[:, 4 * B:4 * B + 4],
                                       float(ppad), ALU.is_lt)
                e.tensor_scalar_mul(scr[:, 4:8], eidx_v[:, 4 * B:4 * B + 4], float(ppad))
                e.tensor_add(scr[:, 4:8], scr[:, 4:8], lrank_v[:, 4 * B:4 * B + 4])
                e.tensor_scalar(scr[:, 8:12], scr[:, 0:4], -BIGF, BIGF,
                                ALU.mult, ALU.add)
                e.tensor_add(scr[:, 4:8], scr[:, 4:8], scr[:, 8:12])
                VE.one(e.tensor_copy(sidx_v[:, 4 * B:4 * B + 4], scr[:, 4:8]))
            veSX4[B] = VE.n

            for j in range(4 * B, 4 * B + 4):
                @blk.gpsimd
                def _(e, j=j, B=B):
                    e.wait_ge(VE.h, veSX4[B])
                    if call == 0:
                        e.wait_ge(SD.h, sd_prefill)
                    if j >= 1:
                        e.wait_ge(SC[1 - j % 2].h, gdSC[j - 1])
                    gdSC[j] = SC[j % 2].dma(e.indirect_dma_start(
                        out=sort_b, out_offset=IndirectOffsetOnAxis(sidx_v[:, j:j + 1], 0),
                        in_=xlnb_sb[j][:], in_offset=None,
                        bounds_check=bnd_reg, oob_is_err=False))
        veSXall = VE.n

        @BV
        def _(e):
            VE.one(e.tensor_copy(hist_row[:], runrow[:, nt * NEXP:(nt + 1) * NEXP]))
        veHist = VE.n

        @blk.gpsimd
        def _(e):
            e.wait_ge(VE.h, veHist)
            GD.dma(e.dma_start(out=ag_in[call], in_=hist_row[:]))
            e.wait_ge(GD.h, GD.n)
            CC.one(e.collective_compute("AllGather", ALU.bypass, replica_groups=RG,
                                        ins=[ag_in[call]], outs=[ag_out[call]]))
            e.wait_ge(CC.h, CC.n)
            GD.dma(e.dma_start(out=agout_sb[call][:], in_=ag_out[call]))
        gdAg = GD.n

        @blk.tensor
        def _(e):
            e.wait_ge(GD.h, gdAg)
            e.wait_ge(VE.h, veSXall)
            TE.one(e.matmul(totB_ps[0:1, 0:NEXP], pmask_sb[:], agout_sb[call][:],
                            start=True, stop=True))
        teOffs = TE.n

        @BV
        def _(e):
            e.wait_ge(TE.h, teOffs)
            last = None
            for r in range(4):
                last = e.tensor_copy(offs_row[:] if False else
                                     combrow[:, r * NEXP:(r + 1) * NEXP],
                                     totB_ps[0:1, 0:NEXP])
            VE.one(last)
        veOffsC = VE.n

        @blk.tensor
        def _(e):
            e.wait_ge(VE.h, veOffsC)
            TE.one(e.matmul(bc_ps[:], ones_sb[0:1, :], combrow[:, 0:32],
                            start=True, stop=True))
        teBCoffs = TE.n

        vePC4 = [0] * nblk
        for B in range(nblk):
            @BV
            def _(e, B=B):
                e.wait_ge(TE.h, teBCoffs)
                s0 = 32 * B
                e.tensor_mul(scr[:, 32:64], oh_v[:, s0:s0 + 32], bc_ps[:])
                e.tensor_reduce(scr[:, 0:4].rearrange("p (g one) -> p g one", one=1),
                                scr[:, 32:64].rearrange("p (g x) -> p g x", x=8),
                                axis=AX.X, op=ALU.add)
                e.tensor_add(scr[:, 0:4], scr[:, 0:4], lrank_v[:, 4 * B:4 * B + 4])
                e.tensor_single_scalar(scr[:, 4:8], scr[:, 0:4], float(CAP), ALU.is_lt)
                e.tensor_single_scalar(scr[:, 8:12], lrank_v[:, 4 * B:4 * B + 4],
                                       float(ppad), ALU.is_lt)
                e.tensor_mul(scr[:, 4:8], scr[:, 4:8], scr[:, 8:12])
                e.tensor_mul(scr[:, 8:12], scr[:, 4:8], p_v[:, 4 * B:4 * B + 4])
                e.tensor_scalar_mul(pscale_v[:, 4 * B:4 * B + 4], scr[:, 8:12], 0.5)
                e.tensor_scalar_mul(scr[:, 8:12], eidx_v[:, 4 * B:4 * B + 4], float(ppad))
                e.tensor_add(scr[:, 8:12], scr[:, 8:12], lrank_v[:, 4 * B:4 * B + 4])
                e.tensor_scalar(scr[:, 12:16], scr[:, 4:8], -BIGF, BIGF,
                                ALU.mult, ALU.add)
                e.tensor_add(scr[:, 8:12], scr[:, 8:12], scr[:, 12:16])
                vePC4[B] = VE.one(e.tensor_copy(gidx_v[:, 4 * B:4 * B + 4], scr[:, 8:12]))
        vePCall = VE.n

        if debug and call == 0:
            @BA
            def _(e):
                e.wait_ge(VE.h, vePCall)
                for nm, src in [("p", p_v), ("eidx", eidx_v), ("lrank", lrank_v),
                                ("pscale", pscale_v), ("gidx", gidx_v), ("sidx", sidx_v)]:
                    SD.dma(e.dma_start(out=dbg[nm], in_=src[:]))
                SD.dma(e.dma_start(out=dbg["hist"], in_=agout_sb[call][:]))

        if G.get("stop_after") == "p1" and call == 0:
            return
        sctot = [(SC[0], SC[0].n), (SC[1], SC[1].n),
                 (GX[0], GX[0].n), (GX[1], GX[1].n)]
        p1s.close()
        with contextlib.ExitStack() as p2:
            E2 = p2.enter_context
            NXB = max(PSUB, 4)
            xe_sb = [E2(nc.sbuf_tensor(f"c{call}_xe{i}", [128, DM], BF16))
                     for i in range(NXB)]
            xeT_sb2 = [[E2(nc.sbuf_tensor(f"c{call}_xeT{p}_{d}", [128, ppad], BF16))
                        for d in range(DCN)] for p in range(2)]
            hT_sb = [E2(nc.sbuf_tensor(f"c{call}_hT{f}", [128, ppad], BF16))
                     for f in range(FCN)]
            w1b_sb = [E2(nc.sbuf_tensor(f"c{call}_w1b{i}", [128, DCN * 256], BF16))
                      for i in range(2)]
            w2a_sb = E2(nc.sbuf_tensor(f"c{call}_w2a", [128, FCN, DM], BF16))
            yeb_sb = [E2(nc.sbuf_tensor(f"c{call}_yeb{i}", [128, DM], BF16))
                      for i in range(2)]
            tr2_ps = E2(nc.psum_tensor(f"c{call}_tr2", [128, ppad], BF16))
            h_ps = [E2(nc.psum_tensor(f"c{call}_h{i}", [128, ppad], F32)) for i in range(2)]
            ye_ps = E2(nc.psum_tensor(f"c{call}_ye", [128, DM], F32))

            ngrp = FCN // 2
            ldXE = {}
            teXT = {}
            veXT = {}
            teH = {}
            evH = {}
            gdW1 = {}
            gdW2 = {}
            teYE = {}
            veYE = {}
            sdYE = {}
            sch = _splits(ppad, 512)

            for ex in range(NEXP):
                @blk.sync
                def _(e, ex=ex):
                    for sv, vv in sctot:
                        e.wait_ge(sv.h, vv)
                    if ex >= 1:
                        e.wait_ge(TE.h, teXT[(ex - 1, DCN - 1)])
                    for s in range(PSUB):
                        k = ex * PSUB + s
                        bb = k % NXB
                        ldXE[(ex, s)] = XE[ex % 2].dma(e.dma_start(
                            out=xe_sb[bb][:],
                            in_=sort_b[(ex * ppad + s * 128):(ex * ppad + (s + 1) * 128), :]))

                for dc in range(DCN):
                    @blk.tensor
                    def _(e, ex=ex, dc=dc):
                        if dc == 0:
                            e.wait_ge(XE[ex % 2].h, ldXE[(ex, PSUB - 1)])
                            if ex >= 2:
                                e.wait_ge(TE.h, teH[(ex - 2, 2 * ngrp - 1)])
                        if (ex, dc - 1) in veXT:
                            e.wait_ge(VE.h, veXT[(ex, dc - 1)])
                        elif ex >= 1:
                            e.wait_ge(VE.h, veXT[(ex - 1, DCN - 1)])
                        for s in range(PSUB):
                            bb = (ex * PSUB + s) % NXB
                            ins = e.transpose(tr2_ps[:, s * 128:(s + 1) * 128],
                                              xe_sb[bb][:, dc * 128:(dc + 1) * 128],
                                              idb_sb[:])
                        teXT[(ex, dc)] = TE.one(ins)

                    @BV
                    def _(e, ex=ex, dc=dc):
                        e.wait_ge(TE.h, teXT[(ex, dc)])
                        veXT[(ex, dc)] = VE.one(e.tensor_copy(
                            xeT_sb2[ex % 2][dc][:], tr2_ps[:]))


                for g in range(ngrp):
                    @blk.gpsimd
                    def _(e, ex=ex, g=g):
                        k = ex * ngrp + g
                        if k >= 2:
                            pk = k - 2
                            e.wait_ge(TE.h, teH[(pk // ngrp, 2 * (pk % ngrp) + 1)])
                        else:
                            for sv, vv in sctot:
                                e.wait_ge(sv.h, vv)
                        gdW1[(ex, g)] = W1D[k % 2].dma(e.dma_start(
                            out=w1b_sb[k % 2][:], in_=w1_d[ex, g]))
                        if g % 2 == 0 and g // 2 < FCN // 4:
                            f4 = (g // 2) * 4
                            if g == 0 and ex >= 1:
                                e.wait_ge(TE.h, teYE[(ex - 1, PSUB - 1)])
                            gdW2[(ex, f4)] = W2D.dma(e.dma_start(
                                out=w2a_sb[:, f4:f4 + 4, :],
                                in_=w2_d[ex, f4 * 128:(f4 + 4) * 128, :].rearrange(
                                    "(i p) c -> p i c", p=128)))

                    for fi in range(2):
                        fc = 2 * g + fi

                        @blk.tensor
                        def _(e, ex=ex, g=g, fi=fi, fc=fc):
                            if g == 0 and fi == 0:
                                e.wait_ge(VE.h, veXT[(ex, DCN - 1)])
                            prev = (ex, fc - 2)
                            if prev in evH:
                                eng, cnt = evH[prev]
                                e.wait_ge(eng.h, cnt)
                            elif ex >= 1:
                                eng, cnt = evH[(ex - 1, 2 * ngrp - 2 + fi)]
                                e.wait_ge(eng.h, cnt)
                            kb = (ex * ngrp + g) % 2
                            if fi == 0:
                                e.wait_ge(W1D[kb].h, gdW1[(ex, g)])
                            for dc in range(DCN):
                                for (so, sw) in sch:
                                    ins = e.matmul(h_ps[fc % 2][:, so:so + sw],
                                                   w1b_sb[kb][:, dc * 256 + fi * 128:dc * 256 + (fi + 1) * 128],
                                                   xeT_sb2[ex % 2][dc][:, so:so + sw],
                                                   start=(dc == 0), stop=(dc == DCN - 1))
                            teH[(ex, fc)] = TE.one(ins)

                        @BV
                        def _(e, ex=ex, fc=fc):
                            e.wait_ge(TE.h, teH[(ex, fc)])
                            evH[(ex, fc)] = (VE, VE.one(
                                e.tensor_scalar_max(hT_sb[fc][:], h_ps[fc % 2][:], 0.0)))

                for s in range(PSUB):
                    @blk.tensor
                    def _(e, ex=ex, s=s):
                        if s == 0:
                            eng, cnt = evH[(ex, 2 * ngrp - 1)]
                            e.wait_ge(eng.h, cnt)
                            e.wait_ge(W2D.h, gdW2[(ex, FCN - 4)])
                            e.wait_ge(W1D[0].h, W1D[0].n if False else gdW1[(ex, 0)])
                        if (ex, s - 1) in veYE:
                            e.wait_ge(VE.h, veYE[(ex, s - 1)])
                        elif ex >= 1:
                            e.wait_ge(VE.h, veYE[(ex - 1, PSUB - 1)])
                        for fc in range(FCN):
                            for (do, dw) in _splits(DM, 512):
                                ins = e.matmul(ye_ps[:, do:do + dw],
                                               hT_sb[fc][:, s * 128:(s + 1) * 128],
                                               w2a_sb[:, fc, do:do + dw],
                                               start=(fc == 0), stop=(fc == FCN - 1))
                        teYE[(ex, s)] = TE.one(ins)

                    @BV
                    def _(e, ex=ex, s=s):
                        e.wait_ge(TE.h, teYE[(ex, s)])
                        k = ex * PSUB + s
                        if k >= 2:
                            e.wait_ge(SDY[k % 2].h, sdYE[k - 2])
                        veYE[(ex, s)] = VE.one(e.tensor_copy(yeb_sb[k % 2][:], ye_ps[:]))

                    @BA
                    def _(e, ex=ex, s=s):
                        e.wait_ge(VE.h, veYE[(ex, s)])
                        k = ex * PSUB + s
                        sdYE[k] = SDY[k % 2].dma(e.dma_start(
                            out=yeout_b[(ex * ppad + s * 128):(ex * ppad + (s + 1) * 128), :],
                            in_=yeb_sb[k % 2][:]))

            yetot = [(SDY[0], SDY[0].n), (SDY[1], SDY[1].n)]

        if G.get("stop_after") == "p2" and call == 0:
            return
        veMS = [0] * nt
        gdGA = [0] * nt
        ldXR = [0] * nt
        veAD = [0] * nt
        sdST = [0] * nt
        for j in range(nt):
            b = j % 2

            @BV
            def _(e, j=j, b=b):
                e.wait_ge(VE.h, vePCall)
                if j >= 2:
                    e.wait_ge(VE.h, veAD[j - 2])
                veMS[j] = VE.one(e.memset(ybuf_sb[b][:], 0.0))

            @blk.gpsimd
            def _(e, j=j, b=b):
                e.wait_ge(VE.h, veMS[j])
                for sv, vv in yetot:
                    e.wait_ge(sv.h, vv)
                gdGA[j] = GA[b].dma(e.indirect_dma_start(
                    out=ybuf_sb[b][:], out_offset=None, in_=yeout_b,
                    in_offset=IndirectOffsetOnAxis(gidx_v[:, j:j + 1], 0),
                    bounds_check=bnd_reg, oob_is_err=False))

            @blk.sync
            def _(e, j=j, b=b):
                if j >= 2:
                    e.wait_ge(VE.h, veAD[j - 2])
                if call == 0:
                    e.wait_ge(GX[0].h, GX[0].n)
                    e.wait_ge(GX[1].h, GX[1].n)
                ldXR[j] = LDX[b].dma(e.dma_start(out=xr_sb[b][:],
                                                 in_=xres_b[j * 128:(j + 1) * 128, :]))

            @BV
            def _(e, j=j, b=b):
                e.wait_ge(GA[b].h, gdGA[j])
                e.wait_ge(LDX[b].h, ldXR[j])
                if j >= 2:
                    e.wait_ge(SDS[b].h, sdST[j - 2])
                veAD[j] = VE.one(e.scalar_tensor_tensor(
                    out_sb[b][:], ybuf_sb[b][:], pscale_v[:, j:j + 1], xr_sb[b][:],
                    ALU.mult, ALU.add))

            @BA
            def _(e, j=j, b=b):
                e.wait_ge(VE.h, veAD[j])
                dst = o_d if final else xres_b
                sdST[j] = SDS[b].dma(e.dma_start(out=dst[j * 128:(j + 1) * 128, :],
                                                 in_=out_sb[b][:]))

        if not final:
            G["sd_x1_done"] = [(SDS[0], SDS[0].n), (SDS[1], SDS[1].n)]
            if debug:
                emit_dump(G, xres_b, dbg["x1"], T, G["sd_x1_done"])


def emit_attention(G):
    nc, blk = G["nc"], G["blk"]
    nt, T, S, ST, QCH, DCN = G["nt"], G["T"], G["S"], G["ST"], G["QCH"], G["DCN"]
    LD, GD, SD, TE, VE, AE = (G[k] for k in ["LD", "GD", "SD", "TE", "VE", "AE"])
    LDX, SDS = G["LDX"], G["SDS"]
    BV, BA = G["BV"], G["BA"]
    xres_b, zsum_b, rr_d = G["xres_b"], G["zsum_b"], G["rr_d"]
    idb_sb, ones_sb, onesb_sb, wqkv_sb, bout_sb = (
        G[k] for k in ["idb_sb", "ones_sb", "onesb_sb", "wqkv_sb", "bout_sb"])
    woutT_d = G["woutT_d"]
    scr = G["scr"]
    dbg, debug = G["dbg"], G["debug"]
    x1_done = G["sd_x1_done"]

    with contextlib.ExitStack() as ph:
        E = ph.enter_context
        xlnT_sb = [E(nc.sbuf_tensor(f"a_xlnT{d}", [128, S], BF16)) for d in range(DCN)]
        attnT_sb = [E(nc.sbuf_tensor(f"a_attnT{d}", [128, S], BF16)) for d in range(DCN)]
        wout_sb = [E(nc.sbuf_tensor(f"a_wout{d}", [128, DM], BF16)) for d in range(DCN)]
        qt_sb = [E(nc.sbuf_tensor(f"a_qt{i}", [128, S], BF16)) for i in range(2)]
        kt_sb = [E(nc.sbuf_tensor(f"a_kt{i}", [128, S], BF16)) for i in range(2)]
        vx_sb = [E(nc.sbuf_tensor(f"a_vx{i}", [128, ST, 65], BF16)) for i in range(2)]
        qv_ps = [E(nc.psum_tensor(f"a_qv{i}", [128, 512], F32)) for i in range(2)]
        pb_sb = [E(nc.sbuf_tensor(f"a_pb{i}", [128, 1024], BF16)) for i in range(6)]
        xt_sb = [E(nc.sbuf_tensor(f"a_xt{i}", [128, DM], BF16)) for i in range(2)]
        xlnb_sb = [E(nc.sbuf_tensor(f"a_xlnb{i}", [128, DM], BF16)) for i in range(2)]
        zrow_sb = E(nc.sbuf_tensor("a_zrow", [1, (8 if NH % 8 == 0 else NH) * S], BF16))
        rrow_sb = E(nc.sbuf_tensor("a_rrow", [1, (8 if NH % 8 == 0 else NH) * S], BF16))
        zT_sb = E(nc.sbuf_tensor("a_zT", [128, (8 if NH % 8 == 0 else NH) * ST], BF16))
        rT_sb = E(nc.sbuf_tensor("a_rT", [128, (8 if NH % 8 == 0 else NH) * ST], F32))
        rTb_sb = E(nc.sbuf_tensor("a_rTb", [128, (8 if NH % 8 == 0 else NH) * ST], BF16))
        rb128_sb = E(nc.sbuf_tensor("a_rb128", [128, 128], BF16))
        xo_sb = [E(nc.sbuf_tensor(f"a_xo{i}", [128, DM], BF16)) for i in range(2)]

        teQall, veQall = [], []

        def qkv_steps(h):
            steps = []
            for (qo, qw) in _splits(S, 512):
                steps.append((h, "q", qo, qw))
                steps.append((h, "k", qo, qw))
            for t in range(ST):
                steps.append((h, "v", t, 0))
            return steps

        def emit_qkv_step(step, first_wait=None, first_ve_wait=None):
            h, kind, qo, qw = step
            qg = len(teQall)
            teQall.append(None)
            par = h % 2
            ct = h // 2
            zb = 192 * par

            @blk.tensor
            def _(e, qg=qg, kind=kind, qo=qo, qw=qw, ct=ct, zb=zb):
                if first_ve_wait is not None:
                    e.wait_ge(VE.h, first_ve_wait)
                if qg >= 2:
                    e.wait_ge(VE.h, veQall[qg - 2])
                buf = qv_ps[qg % 2]
                if kind == "q":
                    ins = e.matmul(buf[0:64, 0:qw], wqkv_sb[:, zb:zb + 64],
                                   xlnT_sb[ct][:, qo:qo + qw], start=True, stop=True)
                elif kind == "k":
                    ins = e.matmul(buf[0:64, 0:qw], wqkv_sb[:, zb + 64:zb + 128],
                                   xlnT_sb[ct][:, qo:qo + qw], start=True, stop=True)
                else:
                    t = qo
                    ins = e.matmul(buf[:, 0:64], xlnT_sb[ct][:, t * 128:(t + 1) * 128],
                                   wqkv_sb[:, zb + 128:zb + 192], start=True, stop=True)
                teQall[qg] = TE.one(ins)

            @BV
            def _(e, qg=qg, kind=kind, qo=qo, qw=qw, par=par, first_wait=first_wait):
                e.wait_ge(TE.h, teQall[qg])
                if first_wait is not None:
                    e.wait_ge(TE.h, first_wait)
                buf = qv_ps[qg % 2]
                if kind == "q":
                    ins = e.tensor_scalar_mul(qt_sb[par][0:64, qo:qo + qw],
                                              buf[0:64, 0:qw], 0.03125)
                elif kind == "k":
                    ins = e.tensor_copy(kt_sb[par][0:64, qo:qo + qw], buf[0:64, 0:qw])
                else:
                    ins = e.tensor_copy(vx_sb[par][:, qo, 0:64], buf[:, 0:64])
                veQall.append(VE.one(ins))

        gd_prior = GD.n

        @blk.gpsimd
        def _(e):
            e.wait_ge(GD.h, gd_prior)
            for sv, vv in x1_done:
                e.wait_ge(sv.h, vv)
            for d in range(DCN):
                GD.dma(e.dma_start(out=wout_sb[d][:], in_=woutT_d[d * 128:(d + 1) * 128, :]))
        gdWo = GD.n

        prev_batch_ve = G["init_ve"]
        sdXO_prev = {0: 0, 1: 0}
        for b in range(2):
            base = b * ST

            with contextlib.ExitStack() as pa:
                trb_ps = pa.enter_context(nc.psum_tensor(f"a{b}_trb", [128, 1024], BF16))
                ldX = [0] * ST
                aeS2 = [0] * ST
                veL2 = [0] * ST
                teT2 = [0] * ST
                veE2 = [0] * ST
                for t in range(ST):
                    j = base + t
                    bb = t % 2
                    sc = 32 * bb

                    @blk.sync
                    def _(e, t=t, j=j, bb=bb):
                        for sv, vv in x1_done:
                            e.wait_ge(sv.h, vv)
                        e.wait_ge(VE.h, prev_batch_ve)
                        if t >= 2:
                            e.wait_ge(VE.h, veL2[t - 2])
                        ldX[t] = LDX[bb].dma(e.dma_start(out=xt_sb[bb][:],
                                                         in_=xres_b[j * 128:(j + 1) * 128, :]))

                    @BV
                    def _(e, t=t, bb=bb, sc=sc):
                        e.wait_ge(LDX[bb].h, ldX[t])
                        e.bn_stats(scr[:, sc + 0:sc + 6], xt_sb[bb][:, 0:512])
                        e.bn_stats(scr[:, sc + 6:sc + 12], xt_sb[bb][:, 512:1024])
                        e.bn_aggr(scr[:, sc + 12:sc + 14], scr[:, sc + 0:sc + 12])
                        VE.one(e.tensor_scalar_add(scr[:, sc + 14:sc + 15],
                                                   scr[:, sc + 13:sc + 14], 1e-5))
                    vS = VE.n

                    @BA
                    def _(e, t=t, sc=sc, vS=vS):
                        e.wait_ge(VE.h, vS)
                        aeS2[t] = AE.one(e.activation(scr[:, sc + 15:sc + 16],
                                                      scr[:, sc + 14:sc + 15], AF.Sqrt))

                    @BV
                    def _(e, t=t, bb=bb, sc=sc):
                        e.wait_ge(AE.h, aeS2[t])
                        if t >= 2:
                            e.wait_ge(TE.h, teT2[t - 2])
                        e.reciprocal(scr[:, sc + 16:sc + 17], scr[:, sc + 15:sc + 16])
                        veL2[t] = VE.one(e.tensor_scalar(
                            xlnb_sb[bb][:], xt_sb[bb][:], scr[:, sc + 12:sc + 13],
                            scr[:, sc + 16:sc + 17], ALU.subtract, ALU.mult))

                    @blk.tensor
                    def _(e, t=t, bb=bb):
                        e.wait_ge(VE.h, veL2[t])
                        if t >= 1:
                            e.wait_ge(VE.h, veE2[t - 1])
                        for dc in range(DCN):
                            ins = e.transpose(trb_ps[:, dc * 128:(dc + 1) * 128],
                                              xlnb_sb[bb][:, dc * 128:(dc + 1) * 128],
                                              idb_sb[:])
                        teT2[t] = TE.one(ins)

                    @BV
                    def _(e, t=t):
                        e.wait_ge(TE.h, teT2[t])
                        for dc in range(DCN):
                            ins = e.tensor_copy(xlnT_sb[dc][:, t * 128:(t + 1) * 128],
                                                trb_ps[:, dc * 128:(dc + 1) * 128])
                        veE2[t] = VE.one(ins)
                xlnT_done = VE.n
            if G.get("stop_after") == "aln":
                return

            HPG = 8 if NH % 8 == 0 else NH
            NHG = NH // HPG
            pvEv = {}
            qkvDoneIdx = {}
            nmLast = xlnT_done
            veOnes_done = False
            for hg in range(NHG):
                with contextlib.ExitStack() as pb_:
                    Eb = pb_.enter_context
                    if not veOnes_done:
                        @BV
                        def _(e):
                            e.wait_ge(VE.h, xlnT_done)
                            e.wait_ge(GD.h, gdWo)
                            for i in range(2):
                                e.memset(vx_sb[i][:, :, 64:65], 1.0)
                                e.memset(qt_sb[i][64:128, :], 0.0)
                                ins = e.memset(kt_sb[i][64:128, :], 0.0)
                            VE.one(ins)
                        veOnes_done = True

                    for h in range(hg * HPG, (hg + 1) * HPG):
                        p64 = (h % 2) * 64
                        ct = h // 2
                        hh = h % HPG
                        par = h % 2

                        if (b, h) not in qkvDoneIdx:
                            fw = TE.n
                            for si, stp in enumerate(qkv_steps(h)):
                                emit_qkv_step(
                                    stp,
                                    first_wait=(fw if si == 0 else None),
                                    first_ve_wait=(xlnT_done if si == 0 else None))
                            qkvDoneIdx[(b, h)] = len(veQall) - 1

                        nxt = h + 1 if (h + 1) < NH else None
                        nsteps = qkv_steps(nxt) if nxt is not None else []
                        fw_key = (b, h - 1) if h >= 1 else (b - 1, NH - 1)
                        fw0 = G.get("a_pvdone", {}).get(fw_key)
                        tot_slots = len(QCH) * ST
                        slot = 0
                        nemit = 0

                        with contextlib.ExitStack() as psl:
                            s_ps = [psl.enter_context(
                                nc.psum_tensor(f"a{b}h{h}_s{i}", [128, 1024], F32))
                                for i in range(2)]
                            pv_ps = psl.enter_context(
                                nc.psum_tensor(f"a{b}h{h}_pv", [128, 1024], F32))

                            qkvDone = veQall[qkvDoneIdx[(b, h)]]
                            aplv = G.get("a_pvev_last", {}).get(fw_key)
                            for ci, (qo, qw) in enumerate(QCH):
                                teS = [0] * ST
                                tePV = [0] * ST
                                aeP = [AE.n + kt + 1 for kt in range(ST)]

                                def emit_pv(ktp, qw=qw):
                                    @blk.tensor
                                    def _(e, ktp=ktp, qw=qw):
                                        e.wait_ge(AE.h, aeP[ktp])
                                        for (o2, w2) in _splits(qw, 512):
                                            ins = e.matmul(pv_ps[0:65, o2:o2 + w2],
                                                           vx_sb[par][:, ktp, :],
                                                           pb_sb[ktp % 6][:, o2:o2 + w2],
                                                           start=(ktp == 0),
                                                           stop=(ktp == ST - 1))
                                        tePV[ktp] = TE.one(ins)

                                for kt in range(ST):
                                    @blk.tensor
                                    def _(e, kt=kt, qo=qo, qw=qw, ci=ci, h=h,
                                          qkvDone=qkvDone, par=par, hh=hh,
                                          nmLast=nmLast):
                                        if kt == 0:
                                            e.wait_ge(VE.h, qkvDone)
                                            if ci == 0 and hh == 0:
                                                e.wait_ge(VE.h, nmLast)
                                            if ci == 0 and aplv is not None:
                                                e.wait_ge(VE.h, aplv)
                                            if ci > 0:
                                                e.wait_ge(VE.h, pvEv[(b, h, ci - 1)])
                                        if kt >= 2:
                                            e.wait_ge(AE.h, aeP[kt - 2])
                                        for (o2, w2) in _splits(qw, 512):
                                            ins = e.matmul(s_ps[kt % 2][:, o2:o2 + w2],
                                                           kt_sb[par][:, kt * 128:(kt + 1) * 128],
                                                           qt_sb[par][:, qo + o2:qo + o2 + w2],
                                                           start=True, stop=True)
                                        teS[kt] = TE.one(ins)

                                    if kt >= 2:
                                        emit_pv(kt - 2)
                                    slot += 1
                                    tgt = (slot * len(nsteps)) // tot_slots
                                    while nemit < tgt:
                                        emit_qkv_step(
                                            nsteps[nemit],
                                            first_wait=(fw0 if nemit == 0 else None))
                                        nemit += 1
                                emit_pv(ST - 2)
                                emit_pv(ST - 1)

                                @BA
                                def _(e, qw=qw, teS=teS, tePV=tePV):
                                    for kt in range(ST):
                                        e.wait_ge(TE.h, teS[kt])
                                        if kt >= 6:
                                            e.wait_ge(TE.h, tePV[kt - 6])
                                        e.activation(pb_sb[kt % 6][:, 0:qw],
                                                     s_ps[kt % 2][:, 0:qw], AF.Exp)
                                assert AE.n == aeP[ST - 1], (AE.n, aeP[ST - 1])
                                G.setdefault("a_pvdone", {})[(b, h)] = tePV[ST - 1]

                                @BV
                                def _(e, h=h, ct=ct, p64=p64, qo=qo, qw=qw, ci=ci,
                                      hh=hh, last=tePV[ST - 1]):
                                    e.wait_ge(TE.h, last)
                                    e.tensor_copy(attnT_sb[ct][p64:p64 + 64, qo:qo + qw],
                                                  pv_ps[0:64, 0:qw])
                                    pvEv[(b, h, ci)] = VE.one(e.tensor_copy(
                                        zrow_sb[:, hh * S + qo:hh * S + qo + qw],
                                        pv_ps[64:65, 0:qw]))
                            G.setdefault("a_pvev_last", {})[(b, h)] = pvEv[(b, h, len(QCH) - 1)]
                        if nxt is not None:
                            assert nemit == len(nsteps)
                            qkvDoneIdx[(b, nxt)] = len(veQall) - 1
                    zEvAll = VE.n

                with contextlib.ExitStack() as pn:
                    En = pn.enter_context
                    tr2_ps = En(nc.psum_tensor(f"a{b}g{hg}_tr2", [128, 128], F32))
                    bc_ps = [En(nc.psum_tensor(f"a{b}g{hg}_bc{i}", [64, 1024], F32))
                             for i in range(2)]
                    zreg = zsum_b[:, hg * HPG * S:(hg + 1) * HPG * S]
                    NCOL = HPG * ST

                    @blk.gpsimd
                    def _(e, zreg=zreg):
                        e.wait_ge(VE.h, zEvAll)
                        e.wait_ge(LD.h, LD.n)
                        GD.dma(e.dma_start(out=zreg, in_=zrow_sb[:]))
                    gdZ = GD.n

                    @blk.sync
                    def _(e, zreg=zreg):
                        e.wait_ge(GD.h, gdZ)
                        LD.dma(e.dma_start(
                            out=zT_sb[:],
                            in_=zreg.rearrange("one (r c) -> (one r) c", c=128),
                            transpose=True))
                    ldZ = LD.n

                    @BV
                    def _(e):
                        e.wait_ge(LD.h, ldZ)
                        e.reciprocal(rT_sb[:], zT_sb[:])
                        VE.one(e.tensor_copy(rTb_sb[:], rT_sb[:]))
                    veR = VE.n

                    teT2r = []

                    @blk.tensor
                    def _(e, NCOL=NCOL):
                        e.wait_ge(VE.h, veR)
                        teT2r.append(TE.one(e.matmul(
                            tr2_ps[0:NCOL, 0:128], rTb_sb[:, 0:NCOL], idb_sb[:],
                            start=True, stop=True)))

                    @BV
                    def _(e, NCOL=NCOL, gprior=GD.n):
                        e.wait_ge(TE.h, teT2r[0])
                        e.wait_ge(GD.h, gprior)
                        VE.one(e.tensor_copy(rb128_sb[0:NCOL, :],
                                             tr2_ps[0:NCOL, 0:128]))
                    veT2r = VE.n

                    @blk.gpsimd
                    def _(e, NCOL=NCOL, lprior=LD.n):
                        e.wait_ge(VE.h, veT2r)
                        e.wait_ge(LD.h, lprior)
                        GD.dma(e.dma_start(
                            out=rr_d.rearrange("one p q -> (one p) q")[0:NCOL, :],
                            in_=rb128_sb[0:NCOL, :]))
                    gdRR = GD.n

                    @blk.sync
                    def _(e, NCOL=NCOL):
                        e.wait_ge(GD.h, gdRR)
                        LD.dma(e.dma_start(
                            out=rrow_sb[0:1, 0:NCOL * 128],
                            in_=rr_d.rearrange("one p q -> one (p q)")[0:1, 0:NCOL * 128]))
                    ldRR = LD.n

                    teBCn = []
                    veNM = []
                    for h in range(hg * HPG, (hg + 1) * HPG):
                        p64 = (h % 2) * 64
                        ct = h // 2
                        hh = h % HPG
                        for (qo, qw) in _splits(S, 1024):
                            @blk.tensor
                            def _(e, hh=hh, qo=qo, qw=qw, idx=len(teBCn)):
                                e.wait_ge(LD.h, ldRR)
                                if idx >= 2:
                                    e.wait_ge(VE.h, veNM[idx - 2])
                                for (o2, w2) in _splits(qw, 512):
                                    ins = e.matmul(
                                        bc_ps[idx % 2][:, o2:o2 + w2], onesb_sb[0:1, 0:64],
                                        rrow_sb[:, hh * S + qo + o2:hh * S + qo + o2 + w2],
                                        start=True, stop=True)
                                teBCn.append(TE.one(ins))

                            @BV
                            def _(e, ct=ct, p64=p64, qo=qo, qw=qw, idx=len(veNM)):
                                e.wait_ge(TE.h, teBCn[idx])
                                veNM.append(VE.one(e.tensor_mul(
                                    attnT_sb[ct][p64:p64 + 64, qo:qo + qw],
                                    attnT_sb[ct][p64:p64 + 64, qo:qo + qw],
                                    bc_ps[idx % 2][:, 0:qw])))
                    nmLast = veNM[-1]
            nmDone = nmLast

            with contextlib.ExitStack() as po:
                op_ps = [po.enter_context(
                    nc.psum_tensor(f"a{b}_op{i}", [128, 1024], F32)) for i in range(2)]
                ldXO = [0] * ST
                teOP = [0] * ST
                veAO = [0] * ST
                sdXO = [0] * ST
                for t in range(ST):
                    j = base + t
                    bb = t % 2

                    @blk.sync
                    def _(e, t=t, j=j, bb=bb):
                        e.wait_ge(VE.h, nmDone)
                        if t >= 2:
                            e.wait_ge(VE.h, veAO[t - 2])
                        ldXO[t] = LDX[bb].dma(e.dma_start(out=xt_sb[bb][:],
                                                          in_=xres_b[j * 128:(j + 1) * 128, :]))

                    @blk.tensor
                    def _(e, t=t, bb=bb):
                        e.wait_ge(VE.h, nmDone)
                        e.wait_ge(GD.h, gdWo)
                        if t >= 2:
                            e.wait_ge(VE.h, veAO[t - 2])
                        for (do, dw) in _splits(DM, 512):
                            for dc in range(DCN):
                                e.matmul(op_ps[bb][:, do:do + dw],
                                         attnT_sb[dc][:, t * 128:(t + 1) * 128],
                                         wout_sb[dc][:, do:do + dw],
                                         start=(dc == 0), stop=False)
                            ins = e.matmul(op_ps[bb][:, do:do + dw], onesb_sb[:],
                                           bout_sb[:, do:do + dw],
                                           start=False, stop=True)
                        teOP[t] = TE.one(ins)

                    @BV
                    def _(e, t=t, bb=bb):
                        e.wait_ge(TE.h, teOP[t])
                        e.wait_ge(LDX[bb].h, ldXO[t])
                        e.wait_ge(SDS[bb].h, sdXO[t - 2] if t >= 2 else sdXO_prev[bb])
                        veAO[t] = VE.one(e.tensor_add(xo_sb[bb][:], xt_sb[bb][:],
                                                      op_ps[bb][:]))

                    @BA
                    def _(e, t=t, j=j, bb=bb):
                        e.wait_ge(VE.h, veAO[t])
                        sdXO[t] = SDS[bb].dma(e.dma_start(out=xres_b[j * 128:(j + 1) * 128, :],
                                                          in_=xo_sb[bb][:]))

            prev_batch_ve = VE.n
            for bb2 in range(2):
                last = [sdXO[t2] for t2 in range(ST) if t2 % 2 == bb2]
                if last:
                    sdXO_prev[bb2] = last[-1]
        G["sd_xres_done"] = [(SDS[0], SDS[0].n), (SDS[1], SDS[1].n)]
        if debug:
            emit_dump(G, xres_b, dbg["x2"], T, G["sd_xres_done"])


def kernel(**inputs):
    x = np.asarray(inputs["L0_attn_x"], np.float32)
    Wq = np.asarray(inputs["L0_attn_Wq"], np.float32)
    Wk = np.asarray(inputs["L0_attn_Wk"], np.float32)
    Wv = np.asarray(inputs["L0_attn_Wv"], np.float32)
    Wout = np.asarray(inputs["L0_attn_W_out"], np.float32)
    bout = np.asarray(inputs["L0_attn_b_out"], np.float32)
    Wr = np.asarray(inputs["L0_ffn_Wr"], np.float32)
    W1 = np.asarray(inputs["L0_ffn_W1"], np.float32)
    W2 = np.asarray(inputs["L0_ffn_W2"], np.float32)

    nt = NT
    T = 128 * nt
    nc = build(nt=nt, ppad=PPAD, debug=False)
    in_maps = make_inputs(x, Wq, Wk, Wv, Wout, bout, Wr, W1, W2, nt)
    res = run_bass_kernel_spmd(nc, in_maps, core_ids=list(range(NCORE)))
    out = np.stack([r["o"].reshape(2, T // 2, DM) for r in res.results], axis=0)
    return np.ascontiguousarray(out.reshape(16, T // 2, DM)).astype(np.float32)


def make_inputs(x, Wq, Wk, Wv, Wout, bout, Wr, W1, W2, nt):
    T = 128 * nt
    w3 = np.concatenate([Wq.T, Wk.T, Wv.T], axis=1)
    z = np.zeros_like(w3)
    wqkv = np.stack([np.concatenate([w3, z], axis=0),
                     np.concatenate([z, w3], axis=0)])
    ngrp = FF // 256
    w1p = np.ascontiguousarray(
        W1.reshape(NEXP, DM // 128, 128, ngrp, 256).transpose(0, 3, 2, 1, 4)
    ).reshape(NEXP, ngrp, 128, (DM // 128) * 256)
    in_common = dict(
        wr=np.ascontiguousarray(Wr),
        w1=w1p,
        w2=np.ascontiguousarray(W2),
        wqkv=np.ascontiguousarray(wqkv),
        woutT=np.ascontiguousarray(Wout.T),
        bout=np.ascontiguousarray(bout.reshape(1, DM)),
        iota8=np.tile(np.arange(NEXP, dtype=np.float32), (128, 1)),
        tri=(np.arange(128)[:, None] < np.arange(128)[None, :]).astype(np.float32),
        idf=np.eye(128, dtype=np.float32),
        idb=np.eye(128).astype(ml_dtypes.bfloat16),
        onesin=np.ones((128, 128), np.float32),
    )
    in_maps = []
    for c in range(NCORE):
        m = dict(in_common)
        m["x"] = np.ascontiguousarray(x[2 * c:2 * c + 2].reshape(T, DM))
        m["pmask"] = (np.arange(NCORE) < c).astype(np.float32).reshape(NCORE, 1)
        in_maps.append(m)
    return in_maps


# version 11
# speedup vs baseline: 2.1322x; 1.0275x over previous
import contextlib
import numpy as np
import ml_dtypes
import concourse.bass as bass
import concourse.mybir as mybir
from concourse.bass import IndirectOffsetOnAxis
from concourse.bass_utils import run_bass_kernel_spmd

F32 = mybir.dt.float32
BF16 = mybir.dt.bfloat16
U32 = mybir.dt.uint32
AF = mybir.ActivationFunctionType
ALU = mybir.AluOpType
AX = mybir.AxisListType

NCORE = 8
DM = 1024
NH = 16
DH = 64
NEXP = 8
FF = 4096
BIGF = 1.0e7

NT = 32
PPAD = 768
DEBUG = False

RG = [list(range(NCORE))]


def _splits(total, step=512):
    out, o = [], 0
    while o < total:
        out.append((o, min(step, total - o)))
        o += step
    return out


class Sem:
    def __init__(self, nc, es, name):
        self.h = es.enter_context(nc.semaphore(name))
        self.n = 0
        self.last_chained = None

    def dma(self, instr):
        instr.then_inc(self.h, 16)
        self.n += 16
        return self.n

    def one(self, instr):
        if instr is self.last_chained:
            return self.n
        instr.then_inc(self.h, 1)
        self.n += 1
        return self.n


class Chain:
    PASSTHRU = ("wait_ge", "dma_start", "indirect_dma_start",
                "collective_compute", "engine_nop")

    def __init__(self, e, sem):
        self._e = e
        self._s = sem

    def __getattr__(self, name):
        fn = getattr(self._e, name)
        if name in Chain.PASSTHRU or not callable(fn):
            return fn
        s = self._s
        raw = self._e

        def wrapped(*a, **k):
            raw.wait_ge(s.h, s.n)
            ins = fn(*a, **k)
            ins.then_inc(s.h, 1)
            s.n += 1
            s.last_chained = ins
            return ins
        return wrapped


def build(nt=None, ppad=None, debug=None, stop_after=None):
    nt = NT if nt is None else nt
    ppad = PPAD if ppad is None else ppad
    debug = DEBUG if debug is None else debug

    T = 128 * nt
    S = T // 2
    ST = S // 128
    CAP = T
    PSUB = ppad // 128
    NSORT = NEXP * ppad
    FCN = FF // 128
    DCN = DM // 128
    QCH = _splits(S, 1024)

    nc = bass.Bass("TRN2", target_bir_lowering=False, debug=False)

    x_d = nc.dram_tensor("x", [T, DM], F32, kind="ExternalInput").ap()
    wr_d = nc.dram_tensor("wr", [DM, NEXP], F32, kind="ExternalInput").ap()
    w1_d = nc.dram_tensor("w1", [NEXP, FF // 256, 128, (DM // 128) * 256], F32, kind="ExternalInput").ap()
    w2_d = nc.dram_tensor("w2", [NEXP, FF, DM], F32, kind="ExternalInput").ap()
    wqkv_d = nc.dram_tensor("wqkv", [2, 128, 3 * DH], F32, kind="ExternalInput").ap()
    woutT_d = nc.dram_tensor("woutT", [DM, DM], F32, kind="ExternalInput").ap()
    bout_d = nc.dram_tensor("bout", [1, DM], F32, kind="ExternalInput").ap()
    iota8_d = nc.dram_tensor("iota8", [128, NEXP], F32, kind="ExternalInput").ap()
    tri_d = nc.dram_tensor("tri", [128, 128], F32, kind="ExternalInput").ap()
    idf_d = nc.dram_tensor("idf", [128, 128], F32, kind="ExternalInput").ap()
    idb_d = nc.dram_tensor("idb", [128, 128], BF16, kind="ExternalInput").ap()
    pmask_d = nc.dram_tensor("pmask", [NCORE, 1], F32, kind="ExternalInput").ap()
    ones_d = nc.dram_tensor("onesin", [128, 128], F32, kind="ExternalInput").ap()

    o_d = nc.dram_tensor("o", [T, DM], F32, kind="ExternalOutput").ap()
    dbg = {}
    if debug:
        for nm, shp, dt in [("eidx", [128, nt], F32), ("p", [128, nt], F32),
                            ("lrank", [128, nt], F32), ("pscale", [128, nt], F32),
                            ("gidx", [128, nt], U32), ("sidx", [128, nt], U32),
                            ("hist", [NCORE, NEXP], F32),
                            ("x1", [T, DM], BF16), ("x2", [T, DM], BF16)]:
            dbg[nm] = nc.dram_tensor("dbg_" + nm, shp, dt, kind="ExternalOutput").ap()

    xres_b = nc.dram_tensor("xres_b", [T, DM], BF16).ap()
    sort_b = nc.dram_tensor("sort_b", [NSORT, DM], BF16).ap()
    yeout_b = nc.dram_tensor("yeout_b", [NSORT, DM], BF16).ap()
    zsum_b = nc.dram_tensor("zsum_b", [1, NH * S], BF16).ap()
    rr_d = nc.dram_tensor("rr_d", [1, 128, 128], BF16).ap()
    ag_in = [nc.dram_tensor(f"ag_in{k}", [1, NEXP], F32).ap() for k in range(2)]
    ag_out = [nc.dram_tensor(f"ag_out{k}", [NCORE, NEXP], F32).ap() for k in range(2)]

    es = contextlib.ExitStack()
    with es:
        E = es.enter_context
        cpool = E(nc.sbuf_tensor("cpool", [128, 8], F32))
        iota8_sb = E(nc.sbuf_tensor("iota8_sb", [128, NEXP], F32))
        tri_sb = E(nc.sbuf_tensor("tri_sb", [128, 128], F32))
        idf_sb = E(nc.sbuf_tensor("idf_sb", [128, 128], F32))
        idb_sb = E(nc.sbuf_tensor("idb_sb", [128, 128], BF16))
        ones_sb = E(nc.sbuf_tensor("ones_sb", [128, 128], F32))
        onesb_sb = E(nc.sbuf_tensor("onesb_sb", [1, 128], BF16))
        pmask_sb = E(nc.sbuf_tensor("pmask_sb", [NCORE, 1], F32))
        wr_sb = E(nc.sbuf_tensor("wr_sb", [128, DCN * NEXP], F32))
        wqkv_sb = E(nc.sbuf_tensor("wqkv_sb", [128, 2 * 3 * DH], BF16))
        bout_sb = E(nc.sbuf_tensor("bout_sb", [1, DM], BF16))
        agout_sb = [E(nc.sbuf_tensor(f"agout_sb{k}", [NCORE, NEXP], F32)) for k in range(2)]
        p_v = E(nc.sbuf_tensor("p_v", [128, nt], F32))
        eidx_v = E(nc.sbuf_tensor("eidx_v", [128, nt], F32))
        lrank_v = E(nc.sbuf_tensor("lrank_v", [128, nt], F32))
        pscale_v = E(nc.sbuf_tensor("pscale_v", [128, nt], F32))
        sidx_v = E(nc.sbuf_tensor("sidx_v", [128, nt], U32))
        gidx_v = E(nc.sbuf_tensor("gidx_v", [128, nt], U32))
        oh_v = E(nc.sbuf_tensor("oh_v", [128, NEXP * nt], F32))
        lgall = E(nc.sbuf_tensor("lgall", [128, NEXP * nt], F32))
        exall = E(nc.sbuf_tensor("exall", [128, NEXP * nt], F32))
        mx8all = E(nc.sbuf_tensor("mx8all", [128, NEXP * nt], F32))
        pref_v = E(nc.sbuf_tensor("pref_v", [128, NEXP * nt], F32))
        totrow = E(nc.sbuf_tensor("totrow", [1, NEXP * nt], F32))
        runrow = E(nc.sbuf_tensor("runrow", [1, NEXP * (nt + 1)], F32))
        combrow = E(nc.sbuf_tensor("combrow", [1, NEXP * nt], F32))
        offs_row = E(nc.sbuf_tensor("offs_row", [1, NEXP], F32))
        hist_row = E(nc.sbuf_tensor("hist_row", [1, NEXP], F32))
        scr = E(nc.sbuf_tensor("scr", [128, 128], F32))
        scru = E(nc.sbuf_tensor("scru", [128, 16], U32))
        dump_sb = E(nc.sbuf_tensor("dump_sb", [128, DM], BF16)) if debug else None

        LD = Sem(nc, es, "LD")
        GD = Sem(nc, es, "GD")
        SD = Sem(nc, es, "SD")
        CC = Sem(nc, es, "CC")
        TE = Sem(nc, es, "TE")
        VE = Sem(nc, es, "VE")
        AE = Sem(nc, es, "AE")
        LDX = [Sem(nc, es, f"LDX{i}") for i in range(4)]
        XE = [Sem(nc, es, f"XE{i}") for i in range(2)]
        SC = [Sem(nc, es, f"SC{i}") for i in range(2)]
        GX = [Sem(nc, es, f"GX{i}") for i in range(2)]
        GA = [Sem(nc, es, f"GA{i}") for i in range(2)]
        SDY = [Sem(nc, es, f"SDY{i}") for i in range(2)]
        SDS = [Sem(nc, es, f"SDS{i}") for i in range(2)]
        W1D = [Sem(nc, es, f"W1D{i}") for i in range(2)]
        W2D = Sem(nc, es, "W2D")

        for cv, ci in [(0.0, 0), (1.0, 1), (0.5, 2), (0.03125, 3), (-0.5, 4)]:
            nc.const_aps.register(F32, cv, cpool[:, ci:ci + 1])

        blk = E(nc.Block())

        def BV(f):
            blk.vector(lambda e: f(Chain(e, VE)))

        def BA(f):
            blk.scalar(lambda e: f(Chain(e, AE)))

        @blk.sync
        def _(e):
            for dst, src in [(iota8_sb, iota8_d), (tri_sb, tri_d), (idf_sb, idf_d),
                             (idb_sb, idb_d), (ones_sb, ones_d), (pmask_sb, pmask_d)]:
                LD.dma(e.dma_start(out=dst[:], in_=src[:]))
            for dc in range(DCN):
                LD.dma(e.dma_start(out=wr_sb[:, dc * NEXP:(dc + 1) * NEXP],
                                   in_=wr_d[dc * 128:(dc + 1) * 128, :]))
        init_ld = LD.n

        bnd_reg = nc.gpsimd.alloc_register("bnd")

        @blk.gpsimd
        def _(e):
            e.reg_mov(bnd_reg, NSORT - 1)
            GD.dma(e.dma_start(out=wqkv_sb[:, 0:192], in_=wqkv_d[0]))
            GD.dma(e.dma_start(out=wqkv_sb[:, 192:384], in_=wqkv_d[1]))
            GD.dma(e.dma_start(out=bout_sb[:], in_=bout_d[:]))
        init_gd = GD.n

        @BV
        def _(e):
            for cv, ci in [(0.0, 0), (1.0, 1), (0.5, 2), (0.03125, 3), (-0.5, 4)]:
                e.memset(cpool[:, ci:ci + 1], cv)
            e.memset(runrow[:, 0:NEXP], 0.0)
            e.wait_ge(LD.h, init_ld)
            VE.one(e.tensor_copy(onesb_sb[:], ones_sb[0:1, :]))
        init_ve = VE.n

        G = dict(nc=nc, blk=blk, nt=nt, ppad=ppad, debug=debug, dbg=dbg,
                 T=T, S=S, ST=ST, CAP=CAP, PSUB=PSUB, NSORT=NSORT, FCN=FCN,
                 DCN=DCN, QCH=QCH,
                 x_d=x_d, o_d=o_d, w1_d=w1_d, w2_d=w2_d, woutT_d=woutT_d,
                 xres_b=xres_b, sort_b=sort_b, yeout_b=yeout_b, zsum_b=zsum_b,
                 rr_d=rr_d,
                 ag_in=ag_in, ag_out=ag_out, agout_sb=agout_sb,
                 cpool=cpool, iota8_sb=iota8_sb, tri_sb=tri_sb, idf_sb=idf_sb,
                 idb_sb=idb_sb, ones_sb=ones_sb, onesb_sb=onesb_sb,
                 pmask_sb=pmask_sb, wr_sb=wr_sb, wqkv_sb=wqkv_sb, bout_sb=bout_sb,
                 p_v=p_v, eidx_v=eidx_v, lrank_v=lrank_v, pscale_v=pscale_v,
                 sidx_v=sidx_v, gidx_v=gidx_v, oh_v=oh_v, pref_v=pref_v,
                 totrow=totrow, runrow=runrow, combrow=combrow, offs_row=offs_row,
                 hist_row=hist_row, scr=scr, scru=scru,
                 lgall=lgall, exall=exall, mx8all=mx8all,
                 dump_sb=dump_sb,
                 LD=LD, GD=GD, SD=SD, CC=CC, TE=TE, VE=VE, AE=AE,
                 LDX=LDX, XE=XE, SC=SC, GX=GX, GA=GA, SDY=SDY, SDS=SDS,
                 W1D=W1D, W2D=W2D, BV=BV, BA=BA, bnd_reg=bnd_reg,
                 init_ld=init_ld, init_gd=init_gd, init_ve=init_ve)

        G["stop_after"] = stop_after
        emit_ffn(G, 0)
        if stop_after not in ("p1", "p2", "ffn0"):
            emit_attention(G)
            if stop_after not in ("attn", "aln", "ahead", "anorm"):
                emit_ffn(G, 1)

        @blk.gpsimd
        def _(e):
            for s in [SD, GD, LD, W2D] + W1D + LDX + XE + SC + GX + GA + SDY + SDS:
                e.wait_ge(s.h, s.n)
    return nc


def emit_dump(G, src_dram, dst_dram, nrows, waits):
    nc, blk, LD, SD = G["nc"], G["blk"], G["LD"], G["SD"]
    BA = G["BA"]
    bsb = G["dump_sb"]
    sd_pre = SD.n
    lds = []
    sds = []
    for r in range(nrows // 128):
        @blk.sync
        def _(e, r=r):
            for sv, vv in waits:
                e.wait_ge(sv.h, vv)
            e.wait_ge(SD.h, sds[r - 1] if r >= 1 else sd_pre)
            lds.append(LD.dma(e.dma_start(out=bsb[:],
                                          in_=src_dram[r * 128:(r + 1) * 128, :])))

        @BA
        def _(e, r=r):
            e.wait_ge(LD.h, lds[r])
            sds.append(SD.dma(e.dma_start(out=dst_dram[r * 128:(r + 1) * 128, :],
                                          in_=bsb[:])))


def emit_ffn(G, call):
    nc, blk = G["nc"], G["blk"]
    nt, ppad, PSUB, NSORT = G["nt"], G["ppad"], G["PSUB"], G["NSORT"]
    T, CAP, FCN, DCN = G["T"], G["CAP"], G["FCN"], G["DCN"]
    LD, GD, SD, CC, TE, VE, AE = (G[k] for k in ["LD", "GD", "SD", "CC", "TE", "VE", "AE"])
    LDX, XE, SC, GX, GA, SDY, SDS = (G[k] for k in ["LDX", "XE", "SC", "GX", "GA", "SDY", "SDS"])
    W1D, W2D = G["W1D"], G["W2D"]
    BV, BA = G["BV"], G["BA"]
    (x_d, xres_b, sort_b, yeout_b, o_d, w1_d, w2_d, ag_in, ag_out, agout_sb) = (
        G[k] for k in ["x_d", "xres_b", "sort_b", "yeout_b", "o_d", "w1_d", "w2_d",
                       "ag_in", "ag_out", "agout_sb"])
    (iota8_sb, tri_sb, idf_sb, idb_sb, ones_sb, pmask_sb, wr_sb) = (
        G[k] for k in ["iota8_sb", "tri_sb", "idf_sb", "idb_sb", "ones_sb",
                       "pmask_sb", "wr_sb"])
    (p_v, eidx_v, lrank_v, pscale_v, sidx_v, gidx_v, oh_v, pref_v) = (
        G[k] for k in ["p_v", "eidx_v", "lrank_v", "pscale_v", "sidx_v", "gidx_v",
                       "oh_v", "pref_v"])
    (totrow, runrow, combrow, offs_row, hist_row, scr, scru) = (
        G[k] for k in ["totrow", "runrow", "combrow", "offs_row", "hist_row",
                       "scr", "scru"])
    dbg, debug = G["dbg"], G["debug"]
    bnd_reg = G["bnd_reg"]
    final = (call == 1)
    src_fp32 = (call == 0)
    xsrc = x_d if src_fp32 else xres_b

    with contextlib.ExitStack() as ph:
        E = ph.enter_context
        ybuf_sb = [E(nc.sbuf_tensor(f"c{call}_yb{i}", [128, DM], BF16)) for i in range(2)]
        xr_sb = [E(nc.sbuf_tensor(f"c{call}_xr{i}", [128, DM], BF16)) for i in range(2)]
        out_sb = [E(nc.sbuf_tensor(f"c{call}_out{i}", [128, DM], F32 if final else BF16))
                  for i in range(2)]
        p1s = ph.enter_context(contextlib.ExitStack())
        E1 = p1s.enter_context
        if src_fp32:
            xf_sb = [E1(nc.sbuf_tensor(f"c{call}_xf{i}", [128, DM], F32)) for i in range(4)]
        else:
            xf_sb = None
        xb_sb = [E1(nc.sbuf_tensor(f"c{call}_xb{i}", [128, DM], BF16)) for i in range(4)]
        xln_sb = [E1(nc.sbuf_tensor(f"c{call}_xln{i}", [128, DM], F32)) for i in range(4)]
        xlnb_sb = [E1(nc.sbuf_tensor(f"c{call}_xlnb{i}", [128, DM], BF16)) for i in range(nt)]
        xlnT_sb = [E1(nc.sbuf_tensor(f"c{call}_xlnT{i}", [128, DM], F32)) for i in range(2)]
        zero_sb = E1(nc.sbuf_tensor(f"c{call}_zero", [128, DM], BF16))
        tr_ps = [E1(nc.psum_tensor(f"c{call}_tr{i}", [128, DM], F32)) for i in range(2)]
        lg_ps = E1(nc.psum_tensor(f"c{call}_lg", [128, NEXP], F32))

        ldA = [0] * nt
        aeSQ = [0] * nt
        veLN = [0] * nt
        teTR = [0] * nt
        aeTR = [0] * nt
        teLG = [0] * nt
        aeEX = [0] * nt
        veOH = [0] * nt
        tePR = [0] * nt
        veSX = [0] * nt
        gdSC = [0] * nt
        gdXR = [0] * nt

        sd_prefill = SD.n
        if call == 0:
            @BV
            def _(e):
                VE.one(e.memset(zero_sb[:], 0.0))
            zv = VE.n

            @BA
            def _(e):
                e.wait_ge(VE.h, zv)
                for r in range(NSORT // 128):
                    SD.dma(e.dma_start(out=sort_b[r * 128:(r + 1) * 128, :], in_=zero_sb[:]))
            sd_prefill = SD.n

        veTR = [0] * nt
        veLG = [0] * nt
        lgall = G["lgall"]
        exall = G["exall"]
        mx8all = G["mx8all"]
        preB_ps = E1(nc.psum_tensor(f"c{call}_preB", [128, 32], F32))
        totB_ps = E1(nc.psum_tensor(f"c{call}_totB", [1, 32], F32))
        bc_ps = E1(nc.psum_tensor(f"c{call}_bc", [128, 32], F32))

        def stage_a(j):
            b4 = j % 4
            xin = xf_sb[b4] if src_fp32 else xb_sb[b4]
            sc = 32 * b4

            @blk.sync
            def _(e, j=j, b4=b4, xin=xin):
                e.wait_ge(VE.h, G["init_ve"])
                if j >= 4:
                    e.wait_ge(VE.h, veLN[j - 4])
                    if src_fp32:
                        e.wait_ge(GX[j % 2].h, gdXR[j - 4])
                if not src_fp32:
                    for sv, vv in G["sd_xres_done"]:
                        e.wait_ge(sv.h, vv)
                ldA[j] = LDX[j % 4].dma(e.dma_start(out=xin[:],
                                                    in_=xsrc[j * 128:(j + 1) * 128, :]))

            @BV
            def _(e, j=j, xin=xin, sc=sc):
                e.wait_ge(LDX[j % 4].h, ldA[j])
                e.bn_stats(scr[:, sc + 0:sc + 6], xin[:, 0:512])
                e.bn_stats(scr[:, sc + 6:sc + 12], xin[:, 512:1024])
                e.bn_aggr(scr[:, sc + 12:sc + 14], scr[:, sc + 0:sc + 12])
                VE.one(e.tensor_scalar_add(scr[:, sc + 14:sc + 15],
                                           scr[:, sc + 13:sc + 14], 1e-5))
            vStat = VE.n

            @BA
            def _(e, j=j, sc=sc, vStat=vStat):
                e.wait_ge(VE.h, vStat)
                e.activation(scr[:, sc + 15:sc + 16], scr[:, sc + 14:sc + 15], AF.Ln)
                aeSQ[j] = AE.one(e.activation(scr[:, sc + 16:sc + 17],
                                              scr[:, sc + 15:sc + 16], AF.Exp,
                                              scale=-0.5))

            @BV
            def _(e, j=j, b4=b4, xin=xin, sc=sc):
                e.wait_ge(AE.h, aeSQ[j])
                if j >= 4:
                    e.wait_ge(TE.h, teTR[j - 4])
                e.tensor_scalar(xln_sb[b4][:], xin[:], scr[:, sc + 12:sc + 13],
                                scr[:, sc + 16:sc + 17], ALU.subtract, ALU.mult)
                veLN[j] = VE.one(e.tensor_copy(xlnb_sb[j][:], xln_sb[b4][:]))

        def stage_b(j):
            b4 = j % 4
            b2 = j % 2

            @blk.tensor
            def _(e, j=j, b4=b4, b2=b2):
                e.wait_ge(VE.h, veLN[j])
                if j >= 2:
                    e.wait_ge(VE.h, veTR[j - 2])
                for dc in range(DCN):
                    ins = e.transpose(tr_ps[b2][:, dc * 128:(dc + 1) * 128],
                                      xln_sb[b4][:, dc * 128:(dc + 1) * 128], idf_sb[:])
                teTR[j] = TE.one(ins)

            @BV
            def _(e, j=j, b2=b2):
                e.wait_ge(TE.h, teTR[j])
                veTR[j] = VE.one(e.tensor_copy(xlnT_sb[b2][:], tr_ps[b2][:]))

        def stage_c(j):
            b2 = j % 2

            @blk.tensor
            def _(e, j=j, b2=b2):
                e.wait_ge(VE.h, veTR[j])
                if j >= 1:
                    e.wait_ge(VE.h, veLG[j - 1])
                for dc in range(DCN):
                    ins = e.matmul(lg_ps[:], xlnT_sb[b2][:, dc * 128:(dc + 1) * 128],
                                   wr_sb[:, dc * NEXP:(dc + 1) * NEXP],
                                   start=(dc == 0), stop=(dc == DCN - 1))
                teLG[j] = TE.one(ins)

            @BV
            def _(e, j=j):
                e.wait_ge(TE.h, teLG[j])
                veLG[j] = VE.one(e.tensor_copy(lgall[:, j * NEXP:(j + 1) * NEXP],
                                               lg_ps[:]))

            if src_fp32:
                @blk.gpsimd
                def _(e, j=j):
                    e.wait_ge(VE.h, veLN[j])
                    if j >= 2:
                        e.wait_ge(GX[j % 2].h, gdXR[j - 2])
                    gdXR[j] = GX[j % 2].dma(e.dma_start(
                        out=xres_b[j * 128:(j + 1) * 128, :], in_=xf_sb[j % 4][:]))

        for step in range(nt + 3):
            if step < nt:
                stage_a(step)
            if 0 <= step - 2 < nt:
                stage_b(step - 2)
            if 0 <= step - 3 < nt:
                stage_c(step - 3)

        @BA
        def _(e):
            e.wait_ge(VE.h, veLG[nt - 1])
            AE.one(e.activation(exall[:], lgall[:], AF.Exp))
        aeEXall = AE.n

        nblk = nt // 4
        veOH4 = [0] * nblk
        veSX4 = [0] * nblk
        tePR4 = [0] * nblk
        teBC4 = [0] * nblk
        veRR4 = [0] * nblk
        for B in range(nblk):
            @BV
            def _(e, B=B):
                if B == 0:
                    e.wait_ge(AE.h, aeEXall)
                for j in range(4 * B, 4 * B + 4):
                    bb8 = 8 * (j % 2)
                    e.max(mx8all[:, j * NEXP:(j + 1) * NEXP],
                          exall[:, j * NEXP:(j + 1) * NEXP])
                    e.max_index(scru[:, bb8:bb8 + 8], mx8all[:, j * NEXP:(j + 1) * NEXP],
                                exall[:, j * NEXP:(j + 1) * NEXP])
                    e.tensor_copy(eidx_v[:, j:j + 1], scru[:, bb8:bb8 + 1])
                    e.tensor_single_scalar(oh_v[:, j * NEXP:(j + 1) * NEXP],
                                           iota8_sb[:], eidx_v[:, j:j + 1], ALU.is_equal)
                s0 = 32 * B
                e.tensor_reduce(scr[:, 0:4].rearrange("p (g one) -> p g one", one=1),
                                exall[:, s0:s0 + 32].rearrange("p (g x) -> p g x", x=8),
                                axis=AX.X, op=ALU.add)
                e.reciprocal(scr[:, 4:8], scr[:, 0:4])
                VE.one(e.tensor_tensor(
                    p_v[:, 4 * B:4 * B + 4],
                    mx8all[:, s0:s0 + 32].rearrange("p (g x) -> p g x", x=8)[:, :, 0],
                    scr[:, 4:8], ALU.mult))
            veOH4[B] = VE.n

            @blk.tensor
            def _(e, B=B):
                e.wait_ge(VE.h, veOH4[B])
                if B >= 1:
                    e.wait_ge(VE.h, veRR4[B - 1])
                e.matmul(preB_ps[:], tri_sb[:], oh_v[:, 32 * B:32 * B + 32],
                         start=True, stop=True)
                tePR4[B] = TE.one(e.matmul(totB_ps[:], ones_sb[:, 0:1],
                                           oh_v[:, 32 * B:32 * B + 32],
                                           start=True, stop=True))

            @BV
            def _(e, B=B):
                e.wait_ge(TE.h, tePR4[B])
                e.tensor_copy(pref_v[:, 32 * B:32 * B + 32], preB_ps[:])
                e.tensor_copy(totrow[:, 32 * B:32 * B + 32], totB_ps[:])
                last = None
                for j in range(4 * B, 4 * B + 4):
                    last = e.tensor_add(runrow[:, (j + 1) * NEXP:(j + 2) * NEXP],
                                        runrow[:, j * NEXP:(j + 1) * NEXP],
                                        totrow[:, j * NEXP:(j + 1) * NEXP])
                VE.one(last)
            veRR4[B] = VE.n

            @blk.tensor
            def _(e, B=B):
                e.wait_ge(VE.h, veRR4[B])
                if B >= 1:
                    e.wait_ge(VE.h, veSX4[B - 1])
                teBC4[B] = TE.one(e.matmul(bc_ps[:], ones_sb[0:1, :],
                                           runrow[:, 32 * B:32 * B + 32],
                                           start=True, stop=True))

            @BV
            def _(e, B=B):
                e.wait_ge(TE.h, teBC4[B])
                s0 = 32 * B
                e.tensor_add(pref_v[:, s0:s0 + 32], pref_v[:, s0:s0 + 32], bc_ps[:])
                e.tensor_mul(scr[:, 32:64], oh_v[:, s0:s0 + 32], pref_v[:, s0:s0 + 32])
                e.tensor_reduce(lrank_v[:, 4 * B:4 * B + 4].rearrange(
                                    "p (g one) -> p g one", one=1),
                                scr[:, 32:64].rearrange("p (g x) -> p g x", x=8),
                                axis=AX.X, op=ALU.add)
                e.tensor_single_scalar(scr[:, 0:4], lrank_v[:, 4 * B:4 * B + 4],
                                       float(ppad), ALU.is_lt)
                e.tensor_scalar_mul(scr[:, 4:8], eidx_v[:, 4 * B:4 * B + 4], float(ppad))
                e.tensor_add(scr[:, 4:8], scr[:, 4:8], lrank_v[:, 4 * B:4 * B + 4])
                e.tensor_scalar(scr[:, 8:12], scr[:, 0:4], -BIGF, BIGF,
                                ALU.mult, ALU.add)
                e.tensor_add(scr[:, 4:8], scr[:, 4:8], scr[:, 8:12])
                VE.one(e.tensor_copy(sidx_v[:, 4 * B:4 * B + 4], scr[:, 4:8]))
            veSX4[B] = VE.n

            for j in range(4 * B, 4 * B + 4):
                @blk.gpsimd
                def _(e, j=j, B=B):
                    e.wait_ge(VE.h, veSX4[B])
                    if call == 0:
                        e.wait_ge(SD.h, sd_prefill)
                    if j >= 1:
                        e.wait_ge(SC[1 - j % 2].h, gdSC[j - 1])
                    gdSC[j] = SC[j % 2].dma(e.indirect_dma_start(
                        out=sort_b, out_offset=IndirectOffsetOnAxis(sidx_v[:, j:j + 1], 0),
                        in_=xlnb_sb[j][:], in_offset=None,
                        bounds_check=bnd_reg, oob_is_err=False))
        veSXall = VE.n

        @BV
        def _(e):
            VE.one(e.tensor_copy(hist_row[:], runrow[:, nt * NEXP:(nt + 1) * NEXP]))
        veHist = VE.n

        @blk.gpsimd
        def _(e):
            e.wait_ge(VE.h, veHist)
            GD.dma(e.dma_start(out=ag_in[call], in_=hist_row[:]))
            e.wait_ge(GD.h, GD.n)
            CC.one(e.collective_compute("AllGather", ALU.bypass, replica_groups=RG,
                                        ins=[ag_in[call]], outs=[ag_out[call]]))
            e.wait_ge(CC.h, CC.n)
            GD.dma(e.dma_start(out=agout_sb[call][:], in_=ag_out[call]))
        gdAg = GD.n

        @blk.tensor
        def _(e):
            e.wait_ge(GD.h, gdAg)
            e.wait_ge(VE.h, veSXall)
            TE.one(e.matmul(totB_ps[0:1, 0:NEXP], pmask_sb[:], agout_sb[call][:],
                            start=True, stop=True))
        teOffs = TE.n

        @BV
        def _(e):
            e.wait_ge(TE.h, teOffs)
            last = None
            for r in range(4):
                last = e.tensor_copy(offs_row[:] if False else
                                     combrow[:, r * NEXP:(r + 1) * NEXP],
                                     totB_ps[0:1, 0:NEXP])
            VE.one(last)
        veOffsC = VE.n

        @blk.tensor
        def _(e):
            e.wait_ge(VE.h, veOffsC)
            TE.one(e.matmul(bc_ps[:], ones_sb[0:1, :], combrow[:, 0:32],
                            start=True, stop=True))
        teBCoffs = TE.n

        vePC4 = [0] * nblk
        for B in range(nblk):
            @BV
            def _(e, B=B):
                e.wait_ge(TE.h, teBCoffs)
                s0 = 32 * B
                e.tensor_mul(scr[:, 32:64], oh_v[:, s0:s0 + 32], bc_ps[:])
                e.tensor_reduce(scr[:, 0:4].rearrange("p (g one) -> p g one", one=1),
                                scr[:, 32:64].rearrange("p (g x) -> p g x", x=8),
                                axis=AX.X, op=ALU.add)
                e.tensor_add(scr[:, 0:4], scr[:, 0:4], lrank_v[:, 4 * B:4 * B + 4])
                e.tensor_single_scalar(scr[:, 4:8], scr[:, 0:4], float(CAP), ALU.is_lt)
                e.tensor_single_scalar(scr[:, 8:12], lrank_v[:, 4 * B:4 * B + 4],
                                       float(ppad), ALU.is_lt)
                e.tensor_mul(scr[:, 4:8], scr[:, 4:8], scr[:, 8:12])
                e.tensor_mul(scr[:, 8:12], scr[:, 4:8], p_v[:, 4 * B:4 * B + 4])
                e.tensor_scalar_mul(pscale_v[:, 4 * B:4 * B + 4], scr[:, 8:12], 0.5)
                e.tensor_scalar_mul(scr[:, 8:12], eidx_v[:, 4 * B:4 * B + 4], float(ppad))
                e.tensor_add(scr[:, 8:12], scr[:, 8:12], lrank_v[:, 4 * B:4 * B + 4])
                e.tensor_scalar(scr[:, 12:16], scr[:, 4:8], -BIGF, BIGF,
                                ALU.mult, ALU.add)
                e.tensor_add(scr[:, 8:12], scr[:, 8:12], scr[:, 12:16])
                vePC4[B] = VE.one(e.tensor_copy(gidx_v[:, 4 * B:4 * B + 4], scr[:, 8:12]))
        vePCall = VE.n

        if debug and call == 0:
            @BA
            def _(e):
                e.wait_ge(VE.h, vePCall)
                for nm, src in [("p", p_v), ("eidx", eidx_v), ("lrank", lrank_v),
                                ("pscale", pscale_v), ("gidx", gidx_v), ("sidx", sidx_v)]:
                    SD.dma(e.dma_start(out=dbg[nm], in_=src[:]))
                SD.dma(e.dma_start(out=dbg["hist"], in_=agout_sb[call][:]))

        if G.get("stop_after") == "p1" and call == 0:
            return
        sctot = [(SC[0], SC[0].n), (SC[1], SC[1].n),
                 (GX[0], GX[0].n), (GX[1], GX[1].n)]
        p1s.close()
        with contextlib.ExitStack() as p2:
            E2 = p2.enter_context
            NXB = max(PSUB, 4)
            xe_sb = [E2(nc.sbuf_tensor(f"c{call}_xe{i}", [128, DM], BF16))
                     for i in range(NXB)]
            xeT_sb2 = [[E2(nc.sbuf_tensor(f"c{call}_xeT{p}_{d}", [128, ppad], BF16))
                        for d in range(DCN)] for p in range(2)]
            hT_sb = [E2(nc.sbuf_tensor(f"c{call}_hT{f}", [128, ppad], BF16))
                     for f in range(FCN)]
            w1b_sb = [E2(nc.sbuf_tensor(f"c{call}_w1b{i}", [128, DCN * 256], BF16))
                      for i in range(2)]
            w2a_sb = E2(nc.sbuf_tensor(f"c{call}_w2a", [128, FCN, DM], BF16))
            yeb_sb = [E2(nc.sbuf_tensor(f"c{call}_yeb{i}", [128, DM], BF16))
                      for i in range(2)]
            tr2_ps = E2(nc.psum_tensor(f"c{call}_tr2", [128, ppad], BF16))
            h_ps = [E2(nc.psum_tensor(f"c{call}_h{i}", [128, ppad], F32)) for i in range(2)]
            ye_ps = E2(nc.psum_tensor(f"c{call}_ye", [128, DM], F32))

            ngrp = FCN // 2
            ldXE = {}
            teXT = {}
            veXT = {}
            teH = {}
            evH = {}
            gdW1 = {}
            gdW2 = {}
            teYE = {}
            veYE = {}
            sdYE = {}
            sch = _splits(ppad, 512)

            for ex in range(NEXP):
                @blk.sync
                def _(e, ex=ex):
                    for sv, vv in sctot:
                        e.wait_ge(sv.h, vv)
                    if ex >= 1:
                        e.wait_ge(TE.h, teXT[(ex - 1, DCN - 1)])
                    for s in range(PSUB):
                        k = ex * PSUB + s
                        bb = k % NXB
                        ldXE[(ex, s)] = XE[ex % 2].dma(e.dma_start(
                            out=xe_sb[bb][:],
                            in_=sort_b[(ex * ppad + s * 128):(ex * ppad + (s + 1) * 128), :]))

                for dc in range(DCN):
                    @blk.tensor
                    def _(e, ex=ex, dc=dc):
                        if dc == 0:
                            e.wait_ge(XE[ex % 2].h, ldXE[(ex, PSUB - 1)])
                            if ex >= 2:
                                e.wait_ge(TE.h, teH[(ex - 2, 2 * ngrp - 1)])
                        if (ex, dc - 1) in veXT:
                            e.wait_ge(VE.h, veXT[(ex, dc - 1)])
                        elif ex >= 1:
                            e.wait_ge(VE.h, veXT[(ex - 1, DCN - 1)])
                        for s in range(PSUB):
                            bb = (ex * PSUB + s) % NXB
                            ins = e.transpose(tr2_ps[:, s * 128:(s + 1) * 128],
                                              xe_sb[bb][:, dc * 128:(dc + 1) * 128],
                                              idb_sb[:])
                        teXT[(ex, dc)] = TE.one(ins)

                    @BV
                    def _(e, ex=ex, dc=dc):
                        e.wait_ge(TE.h, teXT[(ex, dc)])
                        veXT[(ex, dc)] = VE.one(e.tensor_copy(
                            xeT_sb2[ex % 2][dc][:], tr2_ps[:]))


                for g in range(ngrp):
                    @blk.gpsimd
                    def _(e, ex=ex, g=g):
                        k = ex * ngrp + g
                        if k >= 2:
                            pk = k - 2
                            e.wait_ge(TE.h, teH[(pk // ngrp, 2 * (pk % ngrp) + 1)])
                        else:
                            for sv, vv in sctot:
                                e.wait_ge(sv.h, vv)
                        gdW1[(ex, g)] = W1D[k % 2].dma(e.dma_start(
                            out=w1b_sb[k % 2][:], in_=w1_d[ex, g]))
                        if g % 2 == 0 and g // 2 < FCN // 4:
                            f4 = (g // 2) * 4
                            if g == 0 and ex >= 1:
                                e.wait_ge(TE.h, teYE[(ex - 1, PSUB - 1)])
                            gdW2[(ex, f4)] = W2D.dma(e.dma_start(
                                out=w2a_sb[:, f4:f4 + 4, :],
                                in_=w2_d[ex, f4 * 128:(f4 + 4) * 128, :].rearrange(
                                    "(i p) c -> p i c", p=128)))

                    for fi in range(2):
                        fc = 2 * g + fi

                        @blk.tensor
                        def _(e, ex=ex, g=g, fi=fi, fc=fc):
                            if g == 0 and fi == 0:
                                e.wait_ge(VE.h, veXT[(ex, DCN - 1)])
                            prev = (ex, fc - 2)
                            if prev in evH:
                                eng, cnt = evH[prev]
                                e.wait_ge(eng.h, cnt)
                            elif ex >= 1:
                                eng, cnt = evH[(ex - 1, 2 * ngrp - 2 + fi)]
                                e.wait_ge(eng.h, cnt)
                            kb = (ex * ngrp + g) % 2
                            if fi == 0:
                                e.wait_ge(W1D[kb].h, gdW1[(ex, g)])
                            for dc in range(DCN):
                                for (so, sw) in sch:
                                    ins = e.matmul(h_ps[fc % 2][:, so:so + sw],
                                                   w1b_sb[kb][:, dc * 256 + fi * 128:dc * 256 + (fi + 1) * 128],
                                                   xeT_sb2[ex % 2][dc][:, so:so + sw],
                                                   start=(dc == 0), stop=(dc == DCN - 1))
                            teH[(ex, fc)] = TE.one(ins)

                        @BV
                        def _(e, ex=ex, fc=fc):
                            e.wait_ge(TE.h, teH[(ex, fc)])
                            evH[(ex, fc)] = (VE, VE.one(
                                e.tensor_scalar_max(hT_sb[fc][:], h_ps[fc % 2][:], 0.0)))

                for s in range(PSUB):
                    @blk.tensor
                    def _(e, ex=ex, s=s):
                        if s == 0:
                            eng, cnt = evH[(ex, 2 * ngrp - 1)]
                            e.wait_ge(eng.h, cnt)
                            e.wait_ge(W2D.h, gdW2[(ex, FCN - 4)])
                            e.wait_ge(W1D[0].h, W1D[0].n if False else gdW1[(ex, 0)])
                        if (ex, s - 1) in veYE:
                            e.wait_ge(VE.h, veYE[(ex, s - 1)])
                        elif ex >= 1:
                            e.wait_ge(VE.h, veYE[(ex - 1, PSUB - 1)])
                        for fc in range(FCN):
                            for (do, dw) in _splits(DM, 512):
                                ins = e.matmul(ye_ps[:, do:do + dw],
                                               hT_sb[fc][:, s * 128:(s + 1) * 128],
                                               w2a_sb[:, fc, do:do + dw],
                                               start=(fc == 0), stop=(fc == FCN - 1))
                        teYE[(ex, s)] = TE.one(ins)

                    @BV
                    def _(e, ex=ex, s=s):
                        e.wait_ge(TE.h, teYE[(ex, s)])
                        k = ex * PSUB + s
                        if k >= 2:
                            e.wait_ge(SDY[k % 2].h, sdYE[k - 2])
                        veYE[(ex, s)] = VE.one(e.tensor_copy(yeb_sb[k % 2][:], ye_ps[:]))

                    @BA
                    def _(e, ex=ex, s=s):
                        e.wait_ge(VE.h, veYE[(ex, s)])
                        k = ex * PSUB + s
                        sdYE[k] = SDY[k % 2].dma(e.dma_start(
                            out=yeout_b[(ex * ppad + s * 128):(ex * ppad + (s + 1) * 128), :],
                            in_=yeb_sb[k % 2][:]))

            yetot = [(SDY[0], SDY[0].n), (SDY[1], SDY[1].n)]

        if G.get("stop_after") == "p2" and call == 0:
            return
        veMS = [0] * nt
        gdGA = [0] * nt
        ldXR = [0] * nt
        veAD = [0] * nt
        sdST = [0] * nt
        for j in range(nt):
            b = j % 2

            @BV
            def _(e, j=j, b=b):
                e.wait_ge(VE.h, vePCall)
                if j >= 2:
                    e.wait_ge(VE.h, veAD[j - 2])
                veMS[j] = VE.one(e.memset(ybuf_sb[b][:], 0.0))

            @blk.gpsimd
            def _(e, j=j, b=b):
                e.wait_ge(VE.h, veMS[j])
                for sv, vv in yetot:
                    e.wait_ge(sv.h, vv)
                gdGA[j] = GA[b].dma(e.indirect_dma_start(
                    out=ybuf_sb[b][:], out_offset=None, in_=yeout_b,
                    in_offset=IndirectOffsetOnAxis(gidx_v[:, j:j + 1], 0),
                    bounds_check=bnd_reg, oob_is_err=False))

            @blk.sync
            def _(e, j=j, b=b):
                if j >= 2:
                    e.wait_ge(VE.h, veAD[j - 2])
                if call == 0:
                    e.wait_ge(GX[0].h, GX[0].n)
                    e.wait_ge(GX[1].h, GX[1].n)
                ldXR[j] = LDX[b].dma(e.dma_start(out=xr_sb[b][:],
                                                 in_=xres_b[j * 128:(j + 1) * 128, :]))

            @BV
            def _(e, j=j, b=b):
                e.wait_ge(GA[b].h, gdGA[j])
                e.wait_ge(LDX[b].h, ldXR[j])
                if j >= 2:
                    e.wait_ge(SDS[b].h, sdST[j - 2])
                veAD[j] = VE.one(e.scalar_tensor_tensor(
                    out_sb[b][:], ybuf_sb[b][:], pscale_v[:, j:j + 1], xr_sb[b][:],
                    ALU.mult, ALU.add))

            @BA
            def _(e, j=j, b=b):
                e.wait_ge(VE.h, veAD[j])
                dst = o_d if final else xres_b
                sdST[j] = SDS[b].dma(e.dma_start(out=dst[j * 128:(j + 1) * 128, :],
                                                 in_=out_sb[b][:]))

        if not final:
            G["sd_x1_done"] = [(SDS[0], SDS[0].n), (SDS[1], SDS[1].n)]
            if debug:
                emit_dump(G, xres_b, dbg["x1"], T, G["sd_x1_done"])


def emit_attention(G):
    nc, blk = G["nc"], G["blk"]
    nt, T, S, ST, QCH, DCN = G["nt"], G["T"], G["S"], G["ST"], G["QCH"], G["DCN"]
    LD, GD, SD, TE, VE, AE = (G[k] for k in ["LD", "GD", "SD", "TE", "VE", "AE"])
    LDX, SDS = G["LDX"], G["SDS"]
    BV, BA = G["BV"], G["BA"]
    xres_b, zsum_b, rr_d = G["xres_b"], G["zsum_b"], G["rr_d"]
    idb_sb, ones_sb, onesb_sb, wqkv_sb, bout_sb = (
        G[k] for k in ["idb_sb", "ones_sb", "onesb_sb", "wqkv_sb", "bout_sb"])
    woutT_d = G["woutT_d"]
    scr = G["scr"]
    dbg, debug = G["dbg"], G["debug"]
    x1_done = G["sd_x1_done"]

    with contextlib.ExitStack() as ph:
        E = ph.enter_context
        xlnT_sb = [E(nc.sbuf_tensor(f"a_xlnT{d}", [128, S], BF16)) for d in range(DCN)]
        attnT_sb = [E(nc.sbuf_tensor(f"a_attnT{d}", [128, S], BF16)) for d in range(DCN)]
        wout_sb = [E(nc.sbuf_tensor(f"a_wout{d}", [128, DM], BF16)) for d in range(DCN)]
        qt_sb = [E(nc.sbuf_tensor(f"a_qt{i}", [128, S], BF16)) for i in range(2)]
        kt_sb = [E(nc.sbuf_tensor(f"a_kt{i}", [128, S], BF16)) for i in range(2)]
        vx_sb = [E(nc.sbuf_tensor(f"a_vx{i}", [128, ST, 65], BF16)) for i in range(2)]
        qv_ps = [E(nc.psum_tensor(f"a_qv{i}", [128, 512], F32)) for i in range(2)]
        pb_sb = [E(nc.sbuf_tensor(f"a_pb{i}", [128, 1024], BF16)) for i in range(6)]
        xt_sb = [E(nc.sbuf_tensor(f"a_xt{i}", [128, DM], BF16)) for i in range(2)]
        xlnb_sb = [E(nc.sbuf_tensor(f"a_xlnb{i}", [128, DM], BF16)) for i in range(2)]
        zrow_sb = E(nc.sbuf_tensor("a_zrow", [1, (8 if NH % 8 == 0 else NH) * S], BF16))
        rrow_sb = E(nc.sbuf_tensor("a_rrow", [1, (8 if NH % 8 == 0 else NH) * S], BF16))
        zT_sb = E(nc.sbuf_tensor("a_zT", [128, (8 if NH % 8 == 0 else NH) * ST], BF16))
        rT_sb = E(nc.sbuf_tensor("a_rT", [128, (8 if NH % 8 == 0 else NH) * ST], F32))
        rTb_sb = E(nc.sbuf_tensor("a_rTb", [128, (8 if NH % 8 == 0 else NH) * ST], BF16))
        rb128_sb = E(nc.sbuf_tensor("a_rb128", [128, 128], BF16))
        xo_sb = [E(nc.sbuf_tensor(f"a_xo{i}", [128, DM], BF16)) for i in range(2)]

        teQall, veQall = [], []

        def qkv_steps(h):
            steps = []
            for (qo, qw) in _splits(S, 512):
                steps.append((h, "q", qo, qw))
                steps.append((h, "k", qo, qw))
            for t in range(ST):
                steps.append((h, "v", t, 0))
            return steps

        def emit_qkv_step(step, first_wait=None, first_ve_wait=None):
            h, kind, qo, qw = step
            qg = len(teQall)
            teQall.append(None)
            par = h % 2
            ct = h // 2
            zb = 192 * par

            @blk.tensor
            def _(e, qg=qg, kind=kind, qo=qo, qw=qw, ct=ct, zb=zb):
                if first_ve_wait is not None:
                    e.wait_ge(VE.h, first_ve_wait)
                if qg >= 2:
                    e.wait_ge(VE.h, veQall[qg - 2])
                buf = qv_ps[qg % 2]
                if kind == "q":
                    ins = e.matmul(buf[0:64, 0:qw], wqkv_sb[:, zb:zb + 64],
                                   xlnT_sb[ct][:, qo:qo + qw], start=True, stop=True)
                elif kind == "k":
                    ins = e.matmul(buf[0:64, 0:qw], wqkv_sb[:, zb + 64:zb + 128],
                                   xlnT_sb[ct][:, qo:qo + qw], start=True, stop=True)
                else:
                    t = qo
                    ins = e.matmul(buf[:, 0:64], xlnT_sb[ct][:, t * 128:(t + 1) * 128],
                                   wqkv_sb[:, zb + 128:zb + 192], start=True, stop=True)
                teQall[qg] = TE.one(ins)

            @BV
            def _(e, qg=qg, kind=kind, qo=qo, qw=qw, par=par, first_wait=first_wait):
                e.wait_ge(TE.h, teQall[qg])
                if first_wait is not None:
                    e.wait_ge(TE.h, first_wait)
                buf = qv_ps[qg % 2]
                if kind == "q":
                    ins = e.tensor_scalar_mul(qt_sb[par][0:64, qo:qo + qw],
                                              buf[0:64, 0:qw], 0.03125)
                elif kind == "k":
                    ins = e.tensor_copy(kt_sb[par][0:64, qo:qo + qw], buf[0:64, 0:qw])
                else:
                    ins = e.tensor_copy(vx_sb[par][:, qo, 0:64], buf[:, 0:64])
                veQall.append(VE.one(ins))

        gd_prior = GD.n

        @blk.gpsimd
        def _(e):
            e.wait_ge(GD.h, gd_prior)
            for sv, vv in x1_done:
                e.wait_ge(sv.h, vv)
            for d in range(DCN):
                GD.dma(e.dma_start(out=wout_sb[d][:], in_=woutT_d[d * 128:(d + 1) * 128, :]))
        gdWo = GD.n

        prev_batch_ve = G["init_ve"]
        sdXO_prev = {0: 0, 1: 0}
        for b in range(2):
            base = b * ST

            with contextlib.ExitStack() as pa:
                trb_ps = pa.enter_context(nc.psum_tensor(f"a{b}_trb", [128, 1024], BF16))
                ldX = [0] * ST
                aeS2 = [0] * ST
                veL2 = [0] * ST
                teT2 = [0] * ST
                veE2 = [0] * ST
                for t in range(ST):
                    j = base + t
                    bb = t % 2
                    sc = 32 * bb

                    @blk.sync
                    def _(e, t=t, j=j, bb=bb):
                        for sv, vv in x1_done:
                            e.wait_ge(sv.h, vv)
                        e.wait_ge(VE.h, prev_batch_ve)
                        if t >= 2:
                            e.wait_ge(VE.h, veL2[t - 2])
                        ldX[t] = LDX[bb].dma(e.dma_start(out=xt_sb[bb][:],
                                                         in_=xres_b[j * 128:(j + 1) * 128, :]))

                    @BV
                    def _(e, t=t, bb=bb, sc=sc):
                        e.wait_ge(LDX[bb].h, ldX[t])
                        e.bn_stats(scr[:, sc + 0:sc + 6], xt_sb[bb][:, 0:512])
                        e.bn_stats(scr[:, sc + 6:sc + 12], xt_sb[bb][:, 512:1024])
                        e.bn_aggr(scr[:, sc + 12:sc + 14], scr[:, sc + 0:sc + 12])
                        VE.one(e.tensor_scalar_add(scr[:, sc + 14:sc + 15],
                                                   scr[:, sc + 13:sc + 14], 1e-5))
                    vS = VE.n

                    @BA
                    def _(e, t=t, sc=sc, vS=vS):
                        e.wait_ge(VE.h, vS)
                        aeS2[t] = AE.one(e.activation(scr[:, sc + 15:sc + 16],
                                                      scr[:, sc + 14:sc + 15], AF.Sqrt))

                    @BV
                    def _(e, t=t, bb=bb, sc=sc):
                        e.wait_ge(AE.h, aeS2[t])
                        if t >= 2:
                            e.wait_ge(TE.h, teT2[t - 2])
                        e.reciprocal(scr[:, sc + 16:sc + 17], scr[:, sc + 15:sc + 16])
                        veL2[t] = VE.one(e.tensor_scalar(
                            xlnb_sb[bb][:], xt_sb[bb][:], scr[:, sc + 12:sc + 13],
                            scr[:, sc + 16:sc + 17], ALU.subtract, ALU.mult))

                    @blk.tensor
                    def _(e, t=t, bb=bb):
                        e.wait_ge(VE.h, veL2[t])
                        if t >= 1:
                            e.wait_ge(VE.h, veE2[t - 1])
                        for dc in range(DCN):
                            ins = e.transpose(trb_ps[:, dc * 128:(dc + 1) * 128],
                                              xlnb_sb[bb][:, dc * 128:(dc + 1) * 128],
                                              idb_sb[:])
                        teT2[t] = TE.one(ins)

                    @BV
                    def _(e, t=t):
                        e.wait_ge(TE.h, teT2[t])
                        for dc in range(DCN):
                            ins = e.tensor_copy(xlnT_sb[dc][:, t * 128:(t + 1) * 128],
                                                trb_ps[:, dc * 128:(dc + 1) * 128])
                        veE2[t] = VE.one(ins)
                xlnT_done = VE.n
            if G.get("stop_after") == "aln":
                return

            HPG = 8 if NH % 8 == 0 else NH
            NHG = NH // HPG
            pvEv = {}
            qkvDoneIdx = {}
            nmLast = xlnT_done
            veOnes_done = False
            for hg in range(NHG):
                with contextlib.ExitStack() as pb_:
                    Eb = pb_.enter_context
                    if not veOnes_done:
                        @BV
                        def _(e):
                            e.wait_ge(VE.h, xlnT_done)
                            e.wait_ge(GD.h, gdWo)
                            for i in range(2):
                                e.memset(vx_sb[i][:, :, 64:65], 1.0)
                                e.memset(qt_sb[i][64:128, :], 0.0)
                                ins = e.memset(kt_sb[i][64:128, :], 0.0)
                            VE.one(ins)
                        veOnes_done = True

                    for h in range(hg * HPG, (hg + 1) * HPG):
                        p64 = (h % 2) * 64
                        ct = h // 2
                        hh = h % HPG
                        par = h % 2

                        if (b, h) not in qkvDoneIdx:
                            fw = TE.n
                            for si, stp in enumerate(qkv_steps(h)):
                                emit_qkv_step(
                                    stp,
                                    first_wait=(fw if si == 0 else None),
                                    first_ve_wait=(xlnT_done if si == 0 else None))
                            qkvDoneIdx[(b, h)] = len(veQall) - 1

                        nxt = h + 1 if (h + 1) < NH else None
                        nsteps = qkv_steps(nxt) if nxt is not None else []
                        fw_key = (b, h - 1) if h >= 1 else (b - 1, NH - 1)
                        fw0 = G.get("a_pvdone", {}).get(fw_key)
                        tot_slots = len(QCH) * ST
                        slot = 0
                        nemit = 0

                        with contextlib.ExitStack() as psl:
                            s_ps = [psl.enter_context(
                                nc.psum_tensor(f"a{b}h{h}_s{i}", [128, 1024], F32))
                                for i in range(2)]
                            pv_ps = psl.enter_context(
                                nc.psum_tensor(f"a{b}h{h}_pv", [128, 1024], F32))

                            qkvDone = veQall[qkvDoneIdx[(b, h)]]
                            aplv = G.get("a_pvev_last", {}).get(fw_key)
                            NCH = len(QCH)
                            aePall = [AE.n + i + 1 for i in range(NCH * ST)]
                            teSall = []
                            tePVall = []
                            for ci, (qo, qw) in enumerate(QCH):
                                teS = [0] * ST
                                tePV = [0] * ST
                                aeP = aePall[ci * ST:(ci + 1) * ST]

                                def emit_pv(ktp, qw=qw, ci=ci):
                                    @blk.tensor
                                    def _(e, ktp=ktp, qw=qw, ci=ci):
                                        e.wait_ge(AE.h, aeP[ktp])
                                        if ktp == 0 and ci > 0:
                                            e.wait_ge(VE.h, pvEv[(b, h, ci - 1)])
                                        for (o2, w2) in _splits(qw, 512):
                                            ins = e.matmul(pv_ps[0:65, o2:o2 + w2],
                                                           vx_sb[par][:, ktp, :],
                                                           pb_sb[ktp % 6][:, o2:o2 + w2],
                                                           start=(ktp == 0),
                                                           stop=(ktp == ST - 1))
                                        tePV[ktp] = TE.one(ins)

                                for kt in range(ST):
                                    gi = ci * ST + kt

                                    @blk.tensor
                                    def _(e, kt=kt, gi=gi, qo=qo, qw=qw, ci=ci, h=h,
                                          qkvDone=qkvDone, par=par, hh=hh,
                                          nmLast=nmLast):
                                        if gi == 0:
                                            e.wait_ge(VE.h, qkvDone)
                                            if hh == 0:
                                                e.wait_ge(VE.h, nmLast)
                                            if aplv is not None:
                                                e.wait_ge(VE.h, aplv)
                                        if gi >= 2:
                                            e.wait_ge(AE.h, aePall[gi - 2])
                                        for (o2, w2) in _splits(qw, 512):
                                            ins = e.matmul(s_ps[kt % 2][:, o2:o2 + w2],
                                                           kt_sb[par][:, kt * 128:(kt + 1) * 128],
                                                           qt_sb[par][:, qo + o2:qo + o2 + w2],
                                                           start=True, stop=True)
                                        teS[kt] = TE.one(ins)

                                    if kt >= 2:
                                        emit_pv(kt - 2)
                                    slot += 1
                                    tgt = (slot * len(nsteps)) // tot_slots
                                    while nemit < tgt:
                                        emit_qkv_step(
                                            nsteps[nemit],
                                            first_wait=(fw0 if nemit == 0 else None))
                                        nemit += 1
                                emit_pv(ST - 2)
                                emit_pv(ST - 1)
                                teSall.extend(teS)
                                tePVall.extend(tePV)
                                G.setdefault("a_pvdone", {})[(b, h)] = tePV[ST - 1]

                                @BV
                                def _(e, h=h, ct=ct, p64=p64, qo=qo, qw=qw, ci=ci,
                                      hh=hh, last=tePV[ST - 1]):
                                    e.wait_ge(TE.h, last)
                                    e.tensor_copy(attnT_sb[ct][p64:p64 + 64, qo:qo + qw],
                                                  pv_ps[0:64, 0:qw])
                                    pvEv[(b, h, ci)] = VE.one(e.tensor_copy(
                                        zrow_sb[:, hh * S + qo:hh * S + qo + qw],
                                        pv_ps[64:65, 0:qw]))

                            @BA
                            def _(e, teSall=teSall, tePVall=tePVall):
                                for gi in range(NCH * ST):
                                    ci2, kt2 = divmod(gi, ST)
                                    qw2 = QCH[ci2][1]
                                    e.wait_ge(TE.h, teSall[gi])
                                    if gi >= 6:
                                        e.wait_ge(TE.h, tePVall[gi - 6])
                                    e.activation(pb_sb[kt2 % 6][:, 0:qw2],
                                                 s_ps[kt2 % 2][:, 0:qw2], AF.Exp)
                            assert AE.n == aePall[-1], (AE.n, aePall[-1])
                            G.setdefault("a_pvev_last", {})[(b, h)] = pvEv[(b, h, len(QCH) - 1)]
                        if nxt is not None:
                            assert nemit == len(nsteps)
                            qkvDoneIdx[(b, nxt)] = len(veQall) - 1
                    zEvAll = VE.n

                with contextlib.ExitStack() as pn:
                    En = pn.enter_context
                    tr2_ps = En(nc.psum_tensor(f"a{b}g{hg}_tr2", [128, 128], F32))
                    bc_ps = [En(nc.psum_tensor(f"a{b}g{hg}_bc{i}", [64, 1024], F32))
                             for i in range(2)]
                    zreg = zsum_b[:, hg * HPG * S:(hg + 1) * HPG * S]
                    NCOL = HPG * ST

                    @blk.gpsimd
                    def _(e, zreg=zreg):
                        e.wait_ge(VE.h, zEvAll)
                        e.wait_ge(LD.h, LD.n)
                        GD.dma(e.dma_start(out=zreg, in_=zrow_sb[:]))
                    gdZ = GD.n

                    @blk.sync
                    def _(e, zreg=zreg):
                        e.wait_ge(GD.h, gdZ)
                        LD.dma(e.dma_start(
                            out=zT_sb[:],
                            in_=zreg.rearrange("one (r c) -> (one r) c", c=128),
                            transpose=True))
                    ldZ = LD.n

                    @BV
                    def _(e):
                        e.wait_ge(LD.h, ldZ)
                        e.reciprocal(rT_sb[:], zT_sb[:])
                        VE.one(e.tensor_copy(rTb_sb[:], rT_sb[:]))
                    veR = VE.n

                    teT2r = []

                    @blk.tensor
                    def _(e, NCOL=NCOL):
                        e.wait_ge(VE.h, veR)
                        teT2r.append(TE.one(e.matmul(
                            tr2_ps[0:NCOL, 0:128], rTb_sb[:, 0:NCOL], idb_sb[:],
                            start=True, stop=True)))

                    @BV
                    def _(e, NCOL=NCOL, gprior=GD.n):
                        e.wait_ge(TE.h, teT2r[0])
                        e.wait_ge(GD.h, gprior)
                        VE.one(e.tensor_copy(rb128_sb[0:NCOL, :],
                                             tr2_ps[0:NCOL, 0:128]))
                    veT2r = VE.n

                    @blk.gpsimd
                    def _(e, NCOL=NCOL, lprior=LD.n):
                        e.wait_ge(VE.h, veT2r)
                        e.wait_ge(LD.h, lprior)
                        GD.dma(e.dma_start(
                            out=rr_d.rearrange("one p q -> (one p) q")[0:NCOL, :],
                            in_=rb128_sb[0:NCOL, :]))
                    gdRR = GD.n

                    @blk.sync
                    def _(e, NCOL=NCOL):
                        e.wait_ge(GD.h, gdRR)
                        LD.dma(e.dma_start(
                            out=rrow_sb[0:1, 0:NCOL * 128],
                            in_=rr_d.rearrange("one p q -> one (p q)")[0:1, 0:NCOL * 128]))
                    ldRR = LD.n

                    teBCn = []
                    veNM = []
                    for h in range(hg * HPG, (hg + 1) * HPG):
                        p64 = (h % 2) * 64
                        ct = h // 2
                        hh = h % HPG
                        for (qo, qw) in _splits(S, 1024):
                            @blk.tensor
                            def _(e, hh=hh, qo=qo, qw=qw, idx=len(teBCn)):
                                e.wait_ge(LD.h, ldRR)
                                if idx >= 2:
                                    e.wait_ge(VE.h, veNM[idx - 2])
                                for (o2, w2) in _splits(qw, 512):
                                    ins = e.matmul(
                                        bc_ps[idx % 2][:, o2:o2 + w2], onesb_sb[0:1, 0:64],
                                        rrow_sb[:, hh * S + qo + o2:hh * S + qo + o2 + w2],
                                        start=True, stop=True)
                                teBCn.append(TE.one(ins))

                            @BV
                            def _(e, ct=ct, p64=p64, qo=qo, qw=qw, idx=len(veNM)):
                                e.wait_ge(TE.h, teBCn[idx])
                                veNM.append(VE.one(e.tensor_mul(
                                    attnT_sb[ct][p64:p64 + 64, qo:qo + qw],
                                    attnT_sb[ct][p64:p64 + 64, qo:qo + qw],
                                    bc_ps[idx % 2][:, 0:qw])))
                    nmLast = veNM[-1]
            nmDone = nmLast

            with contextlib.ExitStack() as po:
                op_ps = [po.enter_context(
                    nc.psum_tensor(f"a{b}_op{i}", [128, 1024], F32)) for i in range(2)]
                ldXO = [0] * ST
                teOP = [0] * ST
                veAO = [0] * ST
                sdXO = [0] * ST
                for t in range(ST):
                    j = base + t
                    bb = t % 2

                    @blk.sync
                    def _(e, t=t, j=j, bb=bb):
                        e.wait_ge(VE.h, nmDone)
                        if t >= 2:
                            e.wait_ge(VE.h, veAO[t - 2])
                        ldXO[t] = LDX[bb].dma(e.dma_start(out=xt_sb[bb][:],
                                                          in_=xres_b[j * 128:(j + 1) * 128, :]))

                    @blk.tensor
                    def _(e, t=t, bb=bb):
                        e.wait_ge(VE.h, nmDone)
                        e.wait_ge(GD.h, gdWo)
                        if t >= 2:
                            e.wait_ge(VE.h, veAO[t - 2])
                        for (do, dw) in _splits(DM, 512):
                            for dc in range(DCN):
                                e.matmul(op_ps[bb][:, do:do + dw],
                                         attnT_sb[dc][:, t * 128:(t + 1) * 128],
                                         wout_sb[dc][:, do:do + dw],
                                         start=(dc == 0), stop=False)
                            ins = e.matmul(op_ps[bb][:, do:do + dw], onesb_sb[:],
                                           bout_sb[:, do:do + dw],
                                           start=False, stop=True)
                        teOP[t] = TE.one(ins)

                    @BV
                    def _(e, t=t, bb=bb):
                        e.wait_ge(TE.h, teOP[t])
                        e.wait_ge(LDX[bb].h, ldXO[t])
                        e.wait_ge(SDS[bb].h, sdXO[t - 2] if t >= 2 else sdXO_prev[bb])
                        veAO[t] = VE.one(e.tensor_add(xo_sb[bb][:], xt_sb[bb][:],
                                                      op_ps[bb][:]))

                    @BA
                    def _(e, t=t, j=j, bb=bb):
                        e.wait_ge(VE.h, veAO[t])
                        sdXO[t] = SDS[bb].dma(e.dma_start(out=xres_b[j * 128:(j + 1) * 128, :],
                                                          in_=xo_sb[bb][:]))

            prev_batch_ve = VE.n
            for bb2 in range(2):
                last = [sdXO[t2] for t2 in range(ST) if t2 % 2 == bb2]
                if last:
                    sdXO_prev[bb2] = last[-1]
        G["sd_xres_done"] = [(SDS[0], SDS[0].n), (SDS[1], SDS[1].n)]
        if debug:
            emit_dump(G, xres_b, dbg["x2"], T, G["sd_xres_done"])


def kernel(**inputs):
    x = np.asarray(inputs["L0_attn_x"], np.float32)
    Wq = np.asarray(inputs["L0_attn_Wq"], np.float32)
    Wk = np.asarray(inputs["L0_attn_Wk"], np.float32)
    Wv = np.asarray(inputs["L0_attn_Wv"], np.float32)
    Wout = np.asarray(inputs["L0_attn_W_out"], np.float32)
    bout = np.asarray(inputs["L0_attn_b_out"], np.float32)
    Wr = np.asarray(inputs["L0_ffn_Wr"], np.float32)
    W1 = np.asarray(inputs["L0_ffn_W1"], np.float32)
    W2 = np.asarray(inputs["L0_ffn_W2"], np.float32)

    nt = NT
    T = 128 * nt
    nc = build(nt=nt, ppad=PPAD, debug=False)
    in_maps = make_inputs(x, Wq, Wk, Wv, Wout, bout, Wr, W1, W2, nt)
    res = run_bass_kernel_spmd(nc, in_maps, core_ids=list(range(NCORE)))
    out = np.stack([r["o"].reshape(2, T // 2, DM) for r in res.results], axis=0)
    return np.ascontiguousarray(out.reshape(16, T // 2, DM)).astype(np.float32)


def make_inputs(x, Wq, Wk, Wv, Wout, bout, Wr, W1, W2, nt):
    T = 128 * nt
    w3 = np.concatenate([Wq.T, Wk.T, Wv.T], axis=1)
    z = np.zeros_like(w3)
    wqkv = np.stack([np.concatenate([w3, z], axis=0),
                     np.concatenate([z, w3], axis=0)])
    ngrp = FF // 256
    w1p = np.ascontiguousarray(
        W1.reshape(NEXP, DM // 128, 128, ngrp, 256).transpose(0, 3, 2, 1, 4)
    ).reshape(NEXP, ngrp, 128, (DM // 128) * 256)
    in_common = dict(
        wr=np.ascontiguousarray(Wr),
        w1=w1p,
        w2=np.ascontiguousarray(W2),
        wqkv=np.ascontiguousarray(wqkv),
        woutT=np.ascontiguousarray(Wout.T),
        bout=np.ascontiguousarray(bout.reshape(1, DM)),
        iota8=np.tile(np.arange(NEXP, dtype=np.float32), (128, 1)),
        tri=(np.arange(128)[:, None] < np.arange(128)[None, :]).astype(np.float32),
        idf=np.eye(128, dtype=np.float32),
        idb=np.eye(128).astype(ml_dtypes.bfloat16),
        onesin=np.ones((128, 128), np.float32),
    )
    in_maps = []
    for c in range(NCORE):
        m = dict(in_common)
        m["x"] = np.ascontiguousarray(x[2 * c:2 * c + 2].reshape(T, DM))
        m["pmask"] = (np.arange(NCORE) < c).astype(np.float32).reshape(NCORE, 1)
        in_maps.append(m)
    return in_maps


# version 13 (final)
# speedup vs baseline: 2.2537x; 1.0570x over previous
import contextlib
import numpy as np
import ml_dtypes
import concourse.bass as bass
import concourse.mybir as mybir
from concourse.bass import IndirectOffsetOnAxis
from concourse.bass_utils import run_bass_kernel_spmd

F32 = mybir.dt.float32
BF16 = mybir.dt.bfloat16
U32 = mybir.dt.uint32
AF = mybir.ActivationFunctionType
ALU = mybir.AluOpType
AX = mybir.AxisListType

NCORE = 8
DM = 1024
NH = 16
DH = 64
NEXP = 8
FF = 4096
BIGF = 1.0e7

NT = 32
PPAD = 768
DEBUG = False

RG = [list(range(NCORE))]


def _splits(total, step=512):
    out, o = [], 0
    while o < total:
        out.append((o, min(step, total - o)))
        o += step
    return out


class Sem:
    def __init__(self, nc, es, name):
        self.h = es.enter_context(nc.semaphore(name))
        self.n = 0
        self.last_chained = None

    def dma(self, instr):
        instr.then_inc(self.h, 16)
        self.n += 16
        return self.n

    def one(self, instr):
        if instr is self.last_chained:
            return self.n
        instr.then_inc(self.h, 1)
        self.n += 1
        return self.n


class Chain:
    PASSTHRU = ("wait_ge", "dma_start", "indirect_dma_start",
                "collective_compute", "engine_nop")

    def __init__(self, e, sem):
        self._e = e
        self._s = sem

    def __getattr__(self, name):
        fn = getattr(self._e, name)
        if name in Chain.PASSTHRU or not callable(fn):
            return fn
        s = self._s
        raw = self._e

        def wrapped(*a, **k):
            raw.wait_ge(s.h, s.n)
            ins = fn(*a, **k)
            ins.then_inc(s.h, 1)
            s.n += 1
            s.last_chained = ins
            return ins
        return wrapped


def build(nt=None, ppad=None, debug=None, stop_after=None):
    nt = NT if nt is None else nt
    ppad = PPAD if ppad is None else ppad
    debug = DEBUG if debug is None else debug

    T = 128 * nt
    S = T // 2
    ST = S // 128
    CAP = T
    PSUB = ppad // 128
    NSORT = NEXP * ppad
    FCN = FF // 128
    DCN = DM // 128
    QCH = _splits(S, 1024)

    nc = bass.Bass("TRN2", target_bir_lowering=False, debug=False)

    x_d = nc.dram_tensor("x", [T, DM], F32, kind="ExternalInput").ap()
    wr_d = nc.dram_tensor("wr", [DM, NEXP], F32, kind="ExternalInput").ap()
    w1_d = nc.dram_tensor("w1", [NEXP, FF // 256, 128, (DM // 128) * 256], F32, kind="ExternalInput").ap()
    w2_d = nc.dram_tensor("w2", [NEXP, FF, DM], F32, kind="ExternalInput").ap()
    wqkv_d = nc.dram_tensor("wqkv", [2, 128, 3 * DH], F32, kind="ExternalInput").ap()
    woutT_d = nc.dram_tensor("woutT", [DM, DM], F32, kind="ExternalInput").ap()
    bout_d = nc.dram_tensor("bout", [1, DM], F32, kind="ExternalInput").ap()
    iota8_d = nc.dram_tensor("iota8", [128, NEXP], F32, kind="ExternalInput").ap()
    tri_d = nc.dram_tensor("tri", [128, 128], F32, kind="ExternalInput").ap()
    idf_d = nc.dram_tensor("idf", [128, 128], F32, kind="ExternalInput").ap()
    idb_d = nc.dram_tensor("idb", [128, 128], BF16, kind="ExternalInput").ap()
    pmask_d = nc.dram_tensor("pmask", [NCORE, 1], F32, kind="ExternalInput").ap()
    ones_d = nc.dram_tensor("onesin", [128, 128], F32, kind="ExternalInput").ap()

    o_d = nc.dram_tensor("o", [T, DM], F32, kind="ExternalOutput").ap()
    dbg = {}
    if debug:
        for nm, shp, dt in [("eidx", [128, nt], F32), ("p", [128, nt], F32),
                            ("lrank", [128, nt], F32), ("pscale", [128, nt], F32),
                            ("gidx", [128, nt], U32), ("sidx", [128, nt], U32),
                            ("hist", [NCORE, NEXP], F32),
                            ("x1", [T, DM], BF16), ("x2", [T, DM], BF16)]:
            dbg[nm] = nc.dram_tensor("dbg_" + nm, shp, dt, kind="ExternalOutput").ap()

    xres_b = nc.dram_tensor("xres_b", [T, DM], BF16).ap()
    sort_b = nc.dram_tensor("sort_b", [NSORT, DM], BF16).ap()
    yeout_b = nc.dram_tensor("yeout_b", [NSORT, DM], BF16).ap()
    zsum_b = nc.dram_tensor("zsum_b", [1, NH * S], BF16).ap()
    rr_d = nc.dram_tensor("rr_d", [1, 128, 128], BF16).ap()
    ag_in = [nc.dram_tensor(f"ag_in{k}", [1, NEXP], F32).ap() for k in range(2)]
    ag_out = [nc.dram_tensor(f"ag_out{k}", [NCORE, NEXP], F32).ap() for k in range(2)]

    es = contextlib.ExitStack()
    with es:
        E = es.enter_context
        cpool = E(nc.sbuf_tensor("cpool", [128, 8], F32))
        iota8_sb = E(nc.sbuf_tensor("iota8_sb", [128, NEXP], F32))
        tri_sb = E(nc.sbuf_tensor("tri_sb", [128, 128], F32))
        idf_sb = E(nc.sbuf_tensor("idf_sb", [128, 128], F32))
        idb_sb = E(nc.sbuf_tensor("idb_sb", [128, 128], BF16))
        ones_sb = E(nc.sbuf_tensor("ones_sb", [128, 128], F32))
        onesb_sb = E(nc.sbuf_tensor("onesb_sb", [1, 128], BF16))
        pmask_sb = E(nc.sbuf_tensor("pmask_sb", [NCORE, 1], F32))
        wr_sb = E(nc.sbuf_tensor("wr_sb", [128, DCN * NEXP], F32))
        wqkv_sb = E(nc.sbuf_tensor("wqkv_sb", [128, 2 * 3 * DH], BF16))
        bout_sb = E(nc.sbuf_tensor("bout_sb", [1, DM], BF16))
        agout_sb = [E(nc.sbuf_tensor(f"agout_sb{k}", [NCORE, NEXP], F32)) for k in range(2)]
        p_v = E(nc.sbuf_tensor("p_v", [128, nt], F32))
        eidx_v = E(nc.sbuf_tensor("eidx_v", [128, nt], F32))
        lrank_v = E(nc.sbuf_tensor("lrank_v", [128, nt], F32))
        pscale_v = E(nc.sbuf_tensor("pscale_v", [128, nt], F32))
        sidx_v = E(nc.sbuf_tensor("sidx_v", [128, nt], U32))
        gidx_v = E(nc.sbuf_tensor("gidx_v", [128, nt], U32))
        oh_v = E(nc.sbuf_tensor("oh_v", [128, NEXP * nt], F32))
        lgall = E(nc.sbuf_tensor("lgall", [128, NEXP * nt], F32))
        exall = E(nc.sbuf_tensor("exall", [128, NEXP * nt], F32))
        mx8all = E(nc.sbuf_tensor("mx8all", [128, NEXP * nt], F32))
        pref_v = E(nc.sbuf_tensor("pref_v", [128, NEXP * nt], F32))
        totrow = E(nc.sbuf_tensor("totrow", [1, NEXP * nt], F32))
        runrow = E(nc.sbuf_tensor("runrow", [1, NEXP * (nt + 1)], F32))
        combrow = E(nc.sbuf_tensor("combrow", [1, NEXP * nt], F32))
        offs_row = E(nc.sbuf_tensor("offs_row", [1, NEXP], F32))
        hist_row = E(nc.sbuf_tensor("hist_row", [1, NEXP], F32))
        scr = E(nc.sbuf_tensor("scr", [128, 128], F32))
        scru = E(nc.sbuf_tensor("scru", [128, 16], U32))
        dump_sb = E(nc.sbuf_tensor("dump_sb", [128, DM], BF16)) if debug else None

        LD = Sem(nc, es, "LD")
        GD = Sem(nc, es, "GD")
        SD = Sem(nc, es, "SD")
        CC = Sem(nc, es, "CC")
        TE = Sem(nc, es, "TE")
        VE = Sem(nc, es, "VE")
        AE = Sem(nc, es, "AE")
        LDX = [Sem(nc, es, f"LDX{i}") for i in range(4)]
        XE = [Sem(nc, es, f"XE{i}") for i in range(2)]
        SC = [Sem(nc, es, f"SC{i}") for i in range(2)]
        GX = [Sem(nc, es, f"GX{i}") for i in range(2)]
        GA = [Sem(nc, es, f"GA{i}") for i in range(4)]
        SDY = [Sem(nc, es, f"SDY{i}") for i in range(2)]
        SDS = [Sem(nc, es, f"SDS{i}") for i in range(2)]
        W1D = [Sem(nc, es, f"W1D{i}") for i in range(2)]
        W2D = Sem(nc, es, "W2D")

        for cv, ci in [(0.0, 0), (1.0, 1), (0.5, 2), (0.03125, 3), (-0.5, 4)]:
            nc.const_aps.register(F32, cv, cpool[:, ci:ci + 1])

        blk = E(nc.Block())

        def BV(f):
            blk.vector(lambda e: f(Chain(e, VE)))

        def BA(f):
            blk.scalar(lambda e: f(Chain(e, AE)))

        @blk.sync
        def _(e):
            for dst, src in [(iota8_sb, iota8_d), (tri_sb, tri_d), (idf_sb, idf_d),
                             (idb_sb, idb_d), (ones_sb, ones_d), (pmask_sb, pmask_d)]:
                LD.dma(e.dma_start(out=dst[:], in_=src[:]))
            for dc in range(DCN):
                LD.dma(e.dma_start(out=wr_sb[:, dc * NEXP:(dc + 1) * NEXP],
                                   in_=wr_d[dc * 128:(dc + 1) * 128, :]))
        init_ld = LD.n

        bnd_reg = nc.gpsimd.alloc_register("bnd")

        @blk.gpsimd
        def _(e):
            e.reg_mov(bnd_reg, NSORT - 1)
            GD.dma(e.dma_start(out=wqkv_sb[:, 0:192], in_=wqkv_d[0]))
            GD.dma(e.dma_start(out=wqkv_sb[:, 192:384], in_=wqkv_d[1]))
            GD.dma(e.dma_start(out=bout_sb[:], in_=bout_d[:]))
        init_gd = GD.n

        @BV
        def _(e):
            for cv, ci in [(0.0, 0), (1.0, 1), (0.5, 2), (0.03125, 3), (-0.5, 4)]:
                e.memset(cpool[:, ci:ci + 1], cv)
            e.memset(runrow[:, 0:NEXP], 0.0)
            e.wait_ge(LD.h, init_ld)
            VE.one(e.tensor_copy(onesb_sb[:], ones_sb[0:1, :]))
        init_ve = VE.n

        G = dict(nc=nc, blk=blk, nt=nt, ppad=ppad, debug=debug, dbg=dbg,
                 T=T, S=S, ST=ST, CAP=CAP, PSUB=PSUB, NSORT=NSORT, FCN=FCN,
                 DCN=DCN, QCH=QCH,
                 x_d=x_d, o_d=o_d, w1_d=w1_d, w2_d=w2_d, woutT_d=woutT_d,
                 xres_b=xres_b, sort_b=sort_b, yeout_b=yeout_b, zsum_b=zsum_b,
                 rr_d=rr_d,
                 ag_in=ag_in, ag_out=ag_out, agout_sb=agout_sb,
                 cpool=cpool, iota8_sb=iota8_sb, tri_sb=tri_sb, idf_sb=idf_sb,
                 idb_sb=idb_sb, ones_sb=ones_sb, onesb_sb=onesb_sb,
                 pmask_sb=pmask_sb, wr_sb=wr_sb, wqkv_sb=wqkv_sb, bout_sb=bout_sb,
                 p_v=p_v, eidx_v=eidx_v, lrank_v=lrank_v, pscale_v=pscale_v,
                 sidx_v=sidx_v, gidx_v=gidx_v, oh_v=oh_v, pref_v=pref_v,
                 totrow=totrow, runrow=runrow, combrow=combrow, offs_row=offs_row,
                 hist_row=hist_row, scr=scr, scru=scru,
                 lgall=lgall, exall=exall, mx8all=mx8all,
                 dump_sb=dump_sb,
                 LD=LD, GD=GD, SD=SD, CC=CC, TE=TE, VE=VE, AE=AE,
                 LDX=LDX, XE=XE, SC=SC, GX=GX, GA=GA, SDY=SDY, SDS=SDS,
                 W1D=W1D, W2D=W2D, BV=BV, BA=BA, bnd_reg=bnd_reg,
                 init_ld=init_ld, init_gd=init_gd, init_ve=init_ve)

        G["stop_after"] = stop_after
        emit_ffn(G, 0)
        if stop_after not in ("p1", "p2", "ffn0"):
            emit_attention(G)
            if stop_after not in ("attn", "aln", "ahead", "anorm"):
                emit_ffn(G, 1)

        @blk.gpsimd
        def _(e):
            for s in [SD, GD, LD, W2D] + W1D + LDX + XE + SC + GX + GA + SDY + SDS:
                e.wait_ge(s.h, s.n)
    return nc


def emit_dump(G, src_dram, dst_dram, nrows, waits):
    nc, blk, LD, SD = G["nc"], G["blk"], G["LD"], G["SD"]
    BA = G["BA"]
    bsb = G["dump_sb"]
    sd_pre = SD.n
    lds = []
    sds = []
    for r in range(nrows // 128):
        @blk.sync
        def _(e, r=r):
            for sv, vv in waits:
                e.wait_ge(sv.h, vv)
            e.wait_ge(SD.h, sds[r - 1] if r >= 1 else sd_pre)
            lds.append(LD.dma(e.dma_start(out=bsb[:],
                                          in_=src_dram[r * 128:(r + 1) * 128, :])))

        @BA
        def _(e, r=r):
            e.wait_ge(LD.h, lds[r])
            sds.append(SD.dma(e.dma_start(out=dst_dram[r * 128:(r + 1) * 128, :],
                                          in_=bsb[:])))


def emit_ffn(G, call):
    nc, blk = G["nc"], G["blk"]
    nt, ppad, PSUB, NSORT = G["nt"], G["ppad"], G["PSUB"], G["NSORT"]
    T, CAP, FCN, DCN = G["T"], G["CAP"], G["FCN"], G["DCN"]
    LD, GD, SD, CC, TE, VE, AE = (G[k] for k in ["LD", "GD", "SD", "CC", "TE", "VE", "AE"])
    LDX, XE, SC, GX, GA, SDY, SDS = (G[k] for k in ["LDX", "XE", "SC", "GX", "GA", "SDY", "SDS"])
    W1D, W2D = G["W1D"], G["W2D"]
    BV, BA = G["BV"], G["BA"]
    (x_d, xres_b, sort_b, yeout_b, o_d, w1_d, w2_d, ag_in, ag_out, agout_sb) = (
        G[k] for k in ["x_d", "xres_b", "sort_b", "yeout_b", "o_d", "w1_d", "w2_d",
                       "ag_in", "ag_out", "agout_sb"])
    (iota8_sb, tri_sb, idf_sb, idb_sb, ones_sb, pmask_sb, wr_sb) = (
        G[k] for k in ["iota8_sb", "tri_sb", "idf_sb", "idb_sb", "ones_sb",
                       "pmask_sb", "wr_sb"])
    (p_v, eidx_v, lrank_v, pscale_v, sidx_v, gidx_v, oh_v, pref_v) = (
        G[k] for k in ["p_v", "eidx_v", "lrank_v", "pscale_v", "sidx_v", "gidx_v",
                       "oh_v", "pref_v"])
    (totrow, runrow, combrow, offs_row, hist_row, scr, scru) = (
        G[k] for k in ["totrow", "runrow", "combrow", "offs_row", "hist_row",
                       "scr", "scru"])
    dbg, debug = G["dbg"], G["debug"]
    bnd_reg = G["bnd_reg"]
    final = (call == 1)
    src_fp32 = (call == 0)
    xsrc = x_d if src_fp32 else xres_b

    with contextlib.ExitStack() as ph:
        E = ph.enter_context
        ybuf_sb = [E(nc.sbuf_tensor(f"c{call}_yb{i}", [128, DM], BF16)) for i in range(4)]
        xr_sb = [E(nc.sbuf_tensor(f"c{call}_xr{i}", [128, DM], BF16)) for i in range(2)]
        out_sb = [E(nc.sbuf_tensor(f"c{call}_out{i}", [128, DM], F32 if final else BF16))
                  for i in range(2)]
        p1s = ph.enter_context(contextlib.ExitStack())
        E1 = p1s.enter_context
        if src_fp32:
            xf_sb = [E1(nc.sbuf_tensor(f"c{call}_xf{i}", [128, DM], F32)) for i in range(4)]
        else:
            xf_sb = None
        xb_sb = [E1(nc.sbuf_tensor(f"c{call}_xb{i}", [128, DM], BF16)) for i in range(4)]
        xln_sb = [E1(nc.sbuf_tensor(f"c{call}_xln{i}", [128, DM], F32)) for i in range(4)]
        xlnb_sb = [E1(nc.sbuf_tensor(f"c{call}_xlnb{i}", [128, DM], BF16)) for i in range(nt)]
        xlnT_sb = [E1(nc.sbuf_tensor(f"c{call}_xlnT{i}", [128, DM], F32)) for i in range(2)]
        zero_sb = E1(nc.sbuf_tensor(f"c{call}_zero", [128, DM], BF16))
        tr_ps = [E1(nc.psum_tensor(f"c{call}_tr{i}", [128, DM], F32)) for i in range(2)]
        lg_ps = E1(nc.psum_tensor(f"c{call}_lg", [128, NEXP], F32))

        ldA = [0] * nt
        aeSQ = [0] * nt
        veLN = [0] * nt
        teTR = [0] * nt
        aeTR = [0] * nt
        teLG = [0] * nt
        aeEX = [0] * nt
        veOH = [0] * nt
        tePR = [0] * nt
        veSX = [0] * nt
        gdSC = [0] * nt
        gdXR = [0] * nt

        sd_prefill = SD.n
        if call == 0:
            @BV
            def _(e):
                VE.one(e.memset(zero_sb[:], 0.0))
            zv = VE.n

            @BA
            def _(e):
                e.wait_ge(VE.h, zv)
                for r in range(NSORT // 128):
                    SD.dma(e.dma_start(out=sort_b[r * 128:(r + 1) * 128, :], in_=zero_sb[:]))
            sd_prefill = SD.n

        veTR = [0] * nt
        veLG = [0] * nt
        lgall = G["lgall"]
        exall = G["exall"]
        mx8all = G["mx8all"]
        preB_ps = E1(nc.psum_tensor(f"c{call}_preB", [128, 32], F32))
        totB_ps = E1(nc.psum_tensor(f"c{call}_totB", [1, 32], F32))
        bc_ps = E1(nc.psum_tensor(f"c{call}_bc", [128, 32], F32))

        def stage_a(j):
            b4 = j % 4
            xin = xf_sb[b4] if src_fp32 else xb_sb[b4]
            sc = 32 * b4

            @blk.sync
            def _(e, j=j, b4=b4, xin=xin):
                e.wait_ge(VE.h, G["init_ve"])
                if j >= 4:
                    e.wait_ge(VE.h, veLN[j - 4])
                    if src_fp32:
                        e.wait_ge(GX[j % 2].h, gdXR[j - 4])
                if not src_fp32:
                    for sv, vv in G["sd_xres_done"]:
                        e.wait_ge(sv.h, vv)
                ldA[j] = LDX[j % 4].dma(e.dma_start(out=xin[:],
                                                    in_=xsrc[j * 128:(j + 1) * 128, :]))

            @BV
            def _(e, j=j, xin=xin, sc=sc):
                e.wait_ge(LDX[j % 4].h, ldA[j])
                e.bn_stats(scr[:, sc + 0:sc + 6], xin[:, 0:512])
                e.bn_stats(scr[:, sc + 6:sc + 12], xin[:, 512:1024])
                e.bn_aggr(scr[:, sc + 12:sc + 14], scr[:, sc + 0:sc + 12])
                VE.one(e.tensor_scalar_add(scr[:, sc + 14:sc + 15],
                                           scr[:, sc + 13:sc + 14], 1e-5))
            vStat = VE.n

            @BA
            def _(e, j=j, sc=sc, vStat=vStat):
                e.wait_ge(VE.h, vStat)
                e.activation(scr[:, sc + 15:sc + 16], scr[:, sc + 14:sc + 15], AF.Ln)
                aeSQ[j] = AE.one(e.activation(scr[:, sc + 16:sc + 17],
                                              scr[:, sc + 15:sc + 16], AF.Exp,
                                              scale=-0.5))

            @BV
            def _(e, j=j, b4=b4, xin=xin, sc=sc):
                e.wait_ge(AE.h, aeSQ[j])
                if j >= 4:
                    e.wait_ge(TE.h, teTR[j - 4])
                e.tensor_scalar(xln_sb[b4][:], xin[:], scr[:, sc + 12:sc + 13],
                                scr[:, sc + 16:sc + 17], ALU.subtract, ALU.mult)
                veLN[j] = VE.one(e.tensor_copy(xlnb_sb[j][:], xln_sb[b4][:]))

        def stage_b(j):
            b4 = j % 4
            b2 = j % 2

            @blk.tensor
            def _(e, j=j, b4=b4, b2=b2):
                e.wait_ge(VE.h, veLN[j])
                if j >= 2:
                    e.wait_ge(VE.h, veTR[j - 2])
                for dc in range(DCN):
                    ins = e.transpose(tr_ps[b2][:, dc * 128:(dc + 1) * 128],
                                      xln_sb[b4][:, dc * 128:(dc + 1) * 128], idf_sb[:])
                teTR[j] = TE.one(ins)

            @BV
            def _(e, j=j, b2=b2):
                e.wait_ge(TE.h, teTR[j])
                veTR[j] = VE.one(e.tensor_copy(xlnT_sb[b2][:], tr_ps[b2][:]))

        def stage_c(j):
            b2 = j % 2

            @blk.tensor
            def _(e, j=j, b2=b2):
                e.wait_ge(VE.h, veTR[j])
                if j >= 1:
                    e.wait_ge(VE.h, veLG[j - 1])
                for dc in range(DCN):
                    ins = e.matmul(lg_ps[:], xlnT_sb[b2][:, dc * 128:(dc + 1) * 128],
                                   wr_sb[:, dc * NEXP:(dc + 1) * NEXP],
                                   start=(dc == 0), stop=(dc == DCN - 1))
                teLG[j] = TE.one(ins)

            @BV
            def _(e, j=j):
                e.wait_ge(TE.h, teLG[j])
                veLG[j] = VE.one(e.tensor_copy(lgall[:, j * NEXP:(j + 1) * NEXP],
                                               lg_ps[:]))

            if src_fp32:
                @blk.gpsimd
                def _(e, j=j):
                    e.wait_ge(VE.h, veLN[j])
                    if j >= 2:
                        e.wait_ge(GX[j % 2].h, gdXR[j - 2])
                    gdXR[j] = GX[j % 2].dma(e.dma_start(
                        out=xres_b[j * 128:(j + 1) * 128, :], in_=xf_sb[j % 4][:]))

        for step in range(nt + 3):
            if step < nt:
                stage_a(step)
            if 0 <= step - 2 < nt:
                stage_b(step - 2)
            if 0 <= step - 3 < nt:
                stage_c(step - 3)

        @BA
        def _(e):
            e.wait_ge(VE.h, veLG[nt - 1])
            AE.one(e.activation(exall[:], lgall[:], AF.Exp))
        aeEXall = AE.n

        nblk = nt // 4
        veOH4 = [0] * nblk
        veSX4 = [0] * nblk
        tePR4 = [0] * nblk
        teBC4 = [0] * nblk
        veRR4 = [0] * nblk
        for B in range(nblk):
            @BV
            def _(e, B=B):
                if B == 0:
                    e.wait_ge(AE.h, aeEXall)
                for j in range(4 * B, 4 * B + 4):
                    bb8 = 8 * (j % 2)
                    e.max(mx8all[:, j * NEXP:(j + 1) * NEXP],
                          exall[:, j * NEXP:(j + 1) * NEXP])
                    e.max_index(scru[:, bb8:bb8 + 8], mx8all[:, j * NEXP:(j + 1) * NEXP],
                                exall[:, j * NEXP:(j + 1) * NEXP])
                    e.tensor_copy(eidx_v[:, j:j + 1], scru[:, bb8:bb8 + 1])
                    e.tensor_single_scalar(oh_v[:, j * NEXP:(j + 1) * NEXP],
                                           iota8_sb[:], eidx_v[:, j:j + 1], ALU.is_equal)
                s0 = 32 * B
                e.tensor_reduce(scr[:, 0:4].rearrange("p (g one) -> p g one", one=1),
                                exall[:, s0:s0 + 32].rearrange("p (g x) -> p g x", x=8),
                                axis=AX.X, op=ALU.add)
                e.reciprocal(scr[:, 4:8], scr[:, 0:4])
                VE.one(e.tensor_tensor(
                    p_v[:, 4 * B:4 * B + 4],
                    mx8all[:, s0:s0 + 32].rearrange("p (g x) -> p g x", x=8)[:, :, 0],
                    scr[:, 4:8], ALU.mult))
            veOH4[B] = VE.n

            @blk.tensor
            def _(e, B=B):
                e.wait_ge(VE.h, veOH4[B])
                if B >= 1:
                    e.wait_ge(VE.h, veRR4[B - 1])
                e.matmul(preB_ps[:], tri_sb[:], oh_v[:, 32 * B:32 * B + 32],
                         start=True, stop=True)
                tePR4[B] = TE.one(e.matmul(totB_ps[:], ones_sb[:, 0:1],
                                           oh_v[:, 32 * B:32 * B + 32],
                                           start=True, stop=True))

            @BV
            def _(e, B=B):
                e.wait_ge(TE.h, tePR4[B])
                e.tensor_copy(pref_v[:, 32 * B:32 * B + 32], preB_ps[:])
                e.tensor_copy(totrow[:, 32 * B:32 * B + 32], totB_ps[:])
                last = None
                for j in range(4 * B, 4 * B + 4):
                    last = e.tensor_add(runrow[:, (j + 1) * NEXP:(j + 2) * NEXP],
                                        runrow[:, j * NEXP:(j + 1) * NEXP],
                                        totrow[:, j * NEXP:(j + 1) * NEXP])
                VE.one(last)
            veRR4[B] = VE.n

            @blk.tensor
            def _(e, B=B):
                e.wait_ge(VE.h, veRR4[B])
                if B >= 1:
                    e.wait_ge(VE.h, veSX4[B - 1])
                teBC4[B] = TE.one(e.matmul(bc_ps[:], ones_sb[0:1, :],
                                           runrow[:, 32 * B:32 * B + 32],
                                           start=True, stop=True))

            @BV
            def _(e, B=B):
                e.wait_ge(TE.h, teBC4[B])
                s0 = 32 * B
                e.tensor_add(pref_v[:, s0:s0 + 32], pref_v[:, s0:s0 + 32], bc_ps[:])
                e.tensor_mul(scr[:, 32:64], oh_v[:, s0:s0 + 32], pref_v[:, s0:s0 + 32])
                e.tensor_reduce(lrank_v[:, 4 * B:4 * B + 4].rearrange(
                                    "p (g one) -> p g one", one=1),
                                scr[:, 32:64].rearrange("p (g x) -> p g x", x=8),
                                axis=AX.X, op=ALU.add)
                e.tensor_single_scalar(scr[:, 0:4], lrank_v[:, 4 * B:4 * B + 4],
                                       float(ppad), ALU.is_lt)
                e.tensor_scalar_mul(scr[:, 4:8], eidx_v[:, 4 * B:4 * B + 4], float(ppad))
                e.tensor_add(scr[:, 4:8], scr[:, 4:8], lrank_v[:, 4 * B:4 * B + 4])
                e.tensor_scalar(scr[:, 8:12], scr[:, 0:4], -BIGF, BIGF,
                                ALU.mult, ALU.add)
                e.tensor_add(scr[:, 4:8], scr[:, 4:8], scr[:, 8:12])
                VE.one(e.tensor_copy(sidx_v[:, 4 * B:4 * B + 4], scr[:, 4:8]))
            veSX4[B] = VE.n

            for j in range(4 * B, 4 * B + 4):
                @blk.gpsimd
                def _(e, j=j, B=B):
                    e.wait_ge(VE.h, veSX4[B])
                    if call == 0:
                        e.wait_ge(SD.h, sd_prefill)
                    if j >= 1:
                        e.wait_ge(SC[1 - j % 2].h, gdSC[j - 1])
                    gdSC[j] = SC[j % 2].dma(e.indirect_dma_start(
                        out=sort_b, out_offset=IndirectOffsetOnAxis(sidx_v[:, j:j + 1], 0),
                        in_=xlnb_sb[j][:], in_offset=None,
                        bounds_check=bnd_reg, oob_is_err=False))
        veSXall = VE.n

        @BV
        def _(e):
            VE.one(e.tensor_copy(hist_row[:], runrow[:, nt * NEXP:(nt + 1) * NEXP]))
        veHist = VE.n

        @blk.gpsimd
        def _(e):
            e.wait_ge(VE.h, veHist)
            GD.dma(e.dma_start(out=ag_in[call], in_=hist_row[:]))
            e.wait_ge(GD.h, GD.n)
            CC.one(e.collective_compute("AllGather", ALU.bypass, replica_groups=RG,
                                        ins=[ag_in[call]], outs=[ag_out[call]]))
            e.wait_ge(CC.h, CC.n)
            GD.dma(e.dma_start(out=agout_sb[call][:], in_=ag_out[call]))
        gdAg = GD.n

        @blk.tensor
        def _(e):
            e.wait_ge(GD.h, gdAg)
            e.wait_ge(VE.h, veSXall)
            TE.one(e.matmul(totB_ps[0:1, 0:NEXP], pmask_sb[:], agout_sb[call][:],
                            start=True, stop=True))
        teOffs = TE.n

        @BV
        def _(e):
            e.wait_ge(TE.h, teOffs)
            last = None
            for r in range(4):
                last = e.tensor_copy(offs_row[:] if False else
                                     combrow[:, r * NEXP:(r + 1) * NEXP],
                                     totB_ps[0:1, 0:NEXP])
            VE.one(last)
        veOffsC = VE.n

        @blk.tensor
        def _(e):
            e.wait_ge(VE.h, veOffsC)
            TE.one(e.matmul(bc_ps[:], ones_sb[0:1, :], combrow[:, 0:32],
                            start=True, stop=True))
        teBCoffs = TE.n

        vePC4 = [0] * nblk
        for B in range(nblk):
            @BV
            def _(e, B=B):
                e.wait_ge(TE.h, teBCoffs)
                s0 = 32 * B
                e.tensor_mul(scr[:, 32:64], oh_v[:, s0:s0 + 32], bc_ps[:])
                e.tensor_reduce(scr[:, 0:4].rearrange("p (g one) -> p g one", one=1),
                                scr[:, 32:64].rearrange("p (g x) -> p g x", x=8),
                                axis=AX.X, op=ALU.add)
                e.tensor_add(scr[:, 0:4], scr[:, 0:4], lrank_v[:, 4 * B:4 * B + 4])
                e.tensor_single_scalar(scr[:, 4:8], scr[:, 0:4], float(CAP), ALU.is_lt)
                e.tensor_single_scalar(scr[:, 8:12], lrank_v[:, 4 * B:4 * B + 4],
                                       float(ppad), ALU.is_lt)
                e.tensor_mul(scr[:, 4:8], scr[:, 4:8], scr[:, 8:12])
                e.tensor_mul(scr[:, 8:12], scr[:, 4:8], p_v[:, 4 * B:4 * B + 4])
                e.tensor_scalar_mul(pscale_v[:, 4 * B:4 * B + 4], scr[:, 8:12], 0.5)
                e.tensor_scalar_mul(scr[:, 8:12], eidx_v[:, 4 * B:4 * B + 4], float(ppad))
                e.tensor_add(scr[:, 8:12], scr[:, 8:12], lrank_v[:, 4 * B:4 * B + 4])
                e.tensor_scalar(scr[:, 12:16], scr[:, 4:8], -BIGF, BIGF,
                                ALU.mult, ALU.add)
                e.tensor_add(scr[:, 8:12], scr[:, 8:12], scr[:, 12:16])
                vePC4[B] = VE.one(e.tensor_copy(gidx_v[:, 4 * B:4 * B + 4], scr[:, 8:12]))
        vePCall = VE.n

        if debug and call == 0:
            @BA
            def _(e):
                e.wait_ge(VE.h, vePCall)
                for nm, src in [("p", p_v), ("eidx", eidx_v), ("lrank", lrank_v),
                                ("pscale", pscale_v), ("gidx", gidx_v), ("sidx", sidx_v)]:
                    SD.dma(e.dma_start(out=dbg[nm], in_=src[:]))
                SD.dma(e.dma_start(out=dbg["hist"], in_=agout_sb[call][:]))

        if G.get("stop_after") == "p1" and call == 0:
            return
        sctot = [(SC[0], SC[0].n), (SC[1], SC[1].n),
                 (GX[0], GX[0].n), (GX[1], GX[1].n)]
        p1s.close()
        with contextlib.ExitStack() as p2:
            E2 = p2.enter_context
            NXB = max(PSUB, 4)
            xe_sb = [E2(nc.sbuf_tensor(f"c{call}_xe{i}", [128, DM], BF16))
                     for i in range(NXB)]
            xeT_sb2 = [[E2(nc.sbuf_tensor(f"c{call}_xeT{p}_{d}", [128, ppad], BF16))
                        for d in range(DCN)] for p in range(2)]
            hT_sb = [E2(nc.sbuf_tensor(f"c{call}_hT{f}", [128, ppad], BF16))
                     for f in range(FCN)]
            w1b_sb = [E2(nc.sbuf_tensor(f"c{call}_w1b{i}", [128, DCN * 256], BF16))
                      for i in range(2)]
            w2a_sb = E2(nc.sbuf_tensor(f"c{call}_w2a", [128, FCN, DM], BF16))
            yeb_sb = [E2(nc.sbuf_tensor(f"c{call}_yeb{i}", [128, DM], BF16))
                      for i in range(2)]
            tr2_ps = E2(nc.psum_tensor(f"c{call}_tr2", [128, ppad], BF16))
            h_ps = [E2(nc.psum_tensor(f"c{call}_h{i}", [128, ppad], F32)) for i in range(2)]
            ye_ps = E2(nc.psum_tensor(f"c{call}_ye", [128, DM], F32))

            ngrp = FCN // 2
            ldXE = {}
            teXT = {}
            veXT = {}
            teH = {}
            evH = {}
            gdW1 = {}
            gdW2 = {}
            teYE = {}
            veYE = {}
            sdYE = {}
            sch = _splits(ppad, 512)

            for ex in range(NEXP):
                @blk.sync
                def _(e, ex=ex):
                    for sv, vv in sctot:
                        e.wait_ge(sv.h, vv)
                    if ex >= 1:
                        e.wait_ge(TE.h, teXT[(ex - 1, DCN - 1)])
                    for s in range(PSUB):
                        k = ex * PSUB + s
                        bb = k % NXB
                        ldXE[(ex, s)] = XE[ex % 2].dma(e.dma_start(
                            out=xe_sb[bb][:],
                            in_=sort_b[(ex * ppad + s * 128):(ex * ppad + (s + 1) * 128), :]))

                for dc in range(DCN):
                    @blk.tensor
                    def _(e, ex=ex, dc=dc):
                        if dc == 0:
                            e.wait_ge(XE[ex % 2].h, ldXE[(ex, PSUB - 1)])
                            if ex >= 2:
                                e.wait_ge(TE.h, teH[(ex - 2, 2 * ngrp - 1)])
                        if (ex, dc - 1) in veXT:
                            e.wait_ge(VE.h, veXT[(ex, dc - 1)])
                        elif ex >= 1:
                            e.wait_ge(VE.h, veXT[(ex - 1, DCN - 1)])
                        for s in range(PSUB):
                            bb = (ex * PSUB + s) % NXB
                            ins = e.transpose(tr2_ps[:, s * 128:(s + 1) * 128],
                                              xe_sb[bb][:, dc * 128:(dc + 1) * 128],
                                              idb_sb[:])
                        teXT[(ex, dc)] = TE.one(ins)

                    @BV
                    def _(e, ex=ex, dc=dc):
                        e.wait_ge(TE.h, teXT[(ex, dc)])
                        veXT[(ex, dc)] = VE.one(e.tensor_copy(
                            xeT_sb2[ex % 2][dc][:], tr2_ps[:]))


                for g in range(ngrp):
                    @blk.gpsimd
                    def _(e, ex=ex, g=g):
                        k = ex * ngrp + g
                        if k >= 2:
                            pk = k - 2
                            e.wait_ge(TE.h, teH[(pk // ngrp, 2 * (pk % ngrp) + 1)])
                        else:
                            for sv, vv in sctot:
                                e.wait_ge(sv.h, vv)
                        gdW1[(ex, g)] = W1D[k % 2].dma(e.dma_start(
                            out=w1b_sb[k % 2][:], in_=w1_d[ex, g]))
                        if g % 2 == 0 and g // 2 < FCN // 4:
                            f4 = (g // 2) * 4
                            if g == 0 and ex >= 1:
                                e.wait_ge(TE.h, teYE[(ex - 1, PSUB - 1)])
                            gdW2[(ex, f4)] = W2D.dma(e.dma_start(
                                out=w2a_sb[:, f4:f4 + 4, :],
                                in_=w2_d[ex, f4 * 128:(f4 + 4) * 128, :].rearrange(
                                    "(i p) c -> p i c", p=128)))

                    for fi in range(2):
                        fc = 2 * g + fi

                        @blk.tensor
                        def _(e, ex=ex, g=g, fi=fi, fc=fc):
                            if g == 0 and fi == 0:
                                e.wait_ge(VE.h, veXT[(ex, DCN - 1)])
                            prev = (ex, fc - 2)
                            if prev in evH:
                                eng, cnt = evH[prev]
                                e.wait_ge(eng.h, cnt)
                            elif ex >= 1:
                                eng, cnt = evH[(ex - 1, 2 * ngrp - 2 + fi)]
                                e.wait_ge(eng.h, cnt)
                            kb = (ex * ngrp + g) % 2
                            if fi == 0:
                                e.wait_ge(W1D[kb].h, gdW1[(ex, g)])
                            for dc in range(DCN):
                                for (so, sw) in sch:
                                    ins = e.matmul(h_ps[fc % 2][:, so:so + sw],
                                                   w1b_sb[kb][:, dc * 256 + fi * 128:dc * 256 + (fi + 1) * 128],
                                                   xeT_sb2[ex % 2][dc][:, so:so + sw],
                                                   start=(dc == 0), stop=(dc == DCN - 1))
                            teH[(ex, fc)] = TE.one(ins)

                        @BV
                        def _(e, ex=ex, fc=fc):
                            e.wait_ge(TE.h, teH[(ex, fc)])
                            evH[(ex, fc)] = (VE, VE.one(
                                e.tensor_scalar_max(hT_sb[fc][:], h_ps[fc % 2][:], 0.0)))

                for s in range(PSUB):
                    @blk.tensor
                    def _(e, ex=ex, s=s):
                        if s == 0:
                            eng, cnt = evH[(ex, 2 * ngrp - 1)]
                            e.wait_ge(eng.h, cnt)
                            e.wait_ge(W2D.h, gdW2[(ex, FCN - 4)])
                            e.wait_ge(W1D[0].h, W1D[0].n if False else gdW1[(ex, 0)])
                        if (ex, s - 1) in veYE:
                            e.wait_ge(VE.h, veYE[(ex, s - 1)])
                        elif ex >= 1:
                            e.wait_ge(VE.h, veYE[(ex - 1, PSUB - 1)])
                        for fc in range(FCN):
                            for (do, dw) in _splits(DM, 512):
                                ins = e.matmul(ye_ps[:, do:do + dw],
                                               hT_sb[fc][:, s * 128:(s + 1) * 128],
                                               w2a_sb[:, fc, do:do + dw],
                                               start=(fc == 0), stop=(fc == FCN - 1))
                        teYE[(ex, s)] = TE.one(ins)

                    @BV
                    def _(e, ex=ex, s=s):
                        e.wait_ge(TE.h, teYE[(ex, s)])
                        k = ex * PSUB + s
                        if k >= 2:
                            e.wait_ge(SDY[k % 2].h, sdYE[k - 2])
                        veYE[(ex, s)] = VE.one(e.tensor_copy(yeb_sb[k % 2][:], ye_ps[:]))

                    @BA
                    def _(e, ex=ex, s=s):
                        e.wait_ge(VE.h, veYE[(ex, s)])
                        k = ex * PSUB + s
                        sdYE[k] = SDY[k % 2].dma(e.dma_start(
                            out=yeout_b[(ex * ppad + s * 128):(ex * ppad + (s + 1) * 128), :],
                            in_=yeb_sb[k % 2][:]))

            yetot = [(SDY[0], SDY[0].n), (SDY[1], SDY[1].n)]

        if G.get("stop_after") == "p2" and call == 0:
            return
        veMS = [0] * nt
        gdGA = [0] * nt
        ldXR = [0] * nt
        veAD = [0] * nt
        sdST = [0] * nt

        def emit_ms(j):
            b4 = j % 4

            @BV
            def _(e, j=j, b4=b4):
                e.wait_ge(VE.h, vePCall)
                veMS[j] = VE.one(e.memset(ybuf_sb[b4][:], 0.0))

        for j in range(min(4, nt)):
            emit_ms(j)

        for j in range(nt):
            b = j % 2
            b4 = j % 4

            @blk.gpsimd
            def _(e, j=j, b4=b4):
                e.wait_ge(VE.h, veMS[j])
                for sv, vv in yetot:
                    e.wait_ge(sv.h, vv)
                gdGA[j] = GA[b4].dma(e.indirect_dma_start(
                    out=ybuf_sb[b4][:], out_offset=None, in_=yeout_b,
                    in_offset=IndirectOffsetOnAxis(gidx_v[:, j:j + 1], 0),
                    bounds_check=bnd_reg, oob_is_err=False))

            @blk.sync
            def _(e, j=j, b=b):
                if j >= 2:
                    e.wait_ge(VE.h, veAD[j - 2])
                if call == 0:
                    e.wait_ge(GX[0].h, GX[0].n)
                    e.wait_ge(GX[1].h, GX[1].n)
                ldXR[j] = LDX[b].dma(e.dma_start(out=xr_sb[b][:],
                                                 in_=xres_b[j * 128:(j + 1) * 128, :]))

            @BV
            def _(e, j=j, b=b, b4=b4):
                e.wait_ge(GA[b4].h, gdGA[j])
                e.wait_ge(LDX[b].h, ldXR[j])
                if j >= 2:
                    e.wait_ge(SDS[b].h, sdST[j - 2])
                veAD[j] = VE.one(e.scalar_tensor_tensor(
                    out_sb[b][:], ybuf_sb[b4][:], pscale_v[:, j:j + 1], xr_sb[b][:],
                    ALU.mult, ALU.add))
            if j + 4 < nt:
                emit_ms(j + 4)

            @BA
            def _(e, j=j, b=b):
                e.wait_ge(VE.h, veAD[j])
                dst = o_d if final else xres_b
                sdST[j] = SDS[b].dma(e.dma_start(out=dst[j * 128:(j + 1) * 128, :],
                                                 in_=out_sb[b][:]))

        if not final:
            G["sd_x1_done"] = [(SDS[0], SDS[0].n), (SDS[1], SDS[1].n)]
            if debug:
                emit_dump(G, xres_b, dbg["x1"], T, G["sd_x1_done"])


def emit_attention(G):
    nc, blk = G["nc"], G["blk"]
    nt, T, S, ST, QCH, DCN = G["nt"], G["T"], G["S"], G["ST"], G["QCH"], G["DCN"]
    LD, GD, SD, TE, VE, AE = (G[k] for k in ["LD", "GD", "SD", "TE", "VE", "AE"])
    LDX, SDS = G["LDX"], G["SDS"]
    BV, BA = G["BV"], G["BA"]
    xres_b, zsum_b, rr_d = G["xres_b"], G["zsum_b"], G["rr_d"]
    idb_sb, ones_sb, onesb_sb, wqkv_sb, bout_sb = (
        G[k] for k in ["idb_sb", "ones_sb", "onesb_sb", "wqkv_sb", "bout_sb"])
    woutT_d = G["woutT_d"]
    scr = G["scr"]
    dbg, debug = G["dbg"], G["debug"]
    x1_done = G["sd_x1_done"]

    with contextlib.ExitStack() as ph:
        E = ph.enter_context
        xlnT_sb = [E(nc.sbuf_tensor(f"a_xlnT{d}", [128, S], BF16)) for d in range(DCN)]
        attnT_sb = [E(nc.sbuf_tensor(f"a_attnT{d}", [128, S], BF16)) for d in range(DCN)]
        wout_sb = [E(nc.sbuf_tensor(f"a_wout{d}", [128, DM], BF16)) for d in range(DCN)]
        qt_sb = [E(nc.sbuf_tensor(f"a_qt{i}", [128, S], BF16)) for i in range(2)]
        kt_sb = [E(nc.sbuf_tensor(f"a_kt{i}", [128, S], BF16)) for i in range(2)]
        vx_sb = [E(nc.sbuf_tensor(f"a_vx{i}", [128, ST, 65], BF16)) for i in range(2)]
        qv_ps = [E(nc.psum_tensor(f"a_qv{i}", [128, 512], F32)) for i in range(2)]
        pb_sb = [E(nc.sbuf_tensor(f"a_pb{i}", [128, 1024], BF16)) for i in range(6)]
        xt_sb = [E(nc.sbuf_tensor(f"a_xt{i}", [128, DM], BF16)) for i in range(2)]
        xlnb_sb = [E(nc.sbuf_tensor(f"a_xlnb{i}", [128, DM], BF16)) for i in range(2)]
        zrow_sb = E(nc.sbuf_tensor("a_zrow", [1, (8 if NH % 8 == 0 else NH) * S], BF16))
        rrow_sb = E(nc.sbuf_tensor("a_rrow", [1, (8 if NH % 8 == 0 else NH) * S], BF16))
        zT_sb = E(nc.sbuf_tensor("a_zT", [128, (8 if NH % 8 == 0 else NH) * ST], BF16))
        rT_sb = E(nc.sbuf_tensor("a_rT", [128, (8 if NH % 8 == 0 else NH) * ST], F32))
        rTb_sb = E(nc.sbuf_tensor("a_rTb", [128, (8 if NH % 8 == 0 else NH) * ST], BF16))
        rb128_sb = E(nc.sbuf_tensor("a_rb128", [128, 128], BF16))
        xo_sb = [E(nc.sbuf_tensor(f"a_xo{i}", [128, DM], BF16)) for i in range(2)]

        teQall, veQall = [], []

        def qkv_steps(h):
            steps = []
            for (qo, qw) in _splits(S, 512):
                steps.append((h, "q", qo, qw))
                steps.append((h, "k", qo, qw))
            for t in range(ST):
                steps.append((h, "v", t, 0))
            return steps

        def emit_qkv_step(step, first_wait=None, first_ve_wait=None):
            h, kind, qo, qw = step
            qg = len(teQall)
            teQall.append(None)
            par = h % 2
            ct = h // 2
            zb = 192 * par

            @blk.tensor
            def _(e, qg=qg, kind=kind, qo=qo, qw=qw, ct=ct, zb=zb):
                if first_ve_wait is not None:
                    e.wait_ge(VE.h, first_ve_wait)
                if qg >= 2:
                    e.wait_ge(VE.h, veQall[qg - 2])
                buf = qv_ps[qg % 2]
                if kind == "q":
                    ins = e.matmul(buf[0:64, 0:qw], wqkv_sb[:, zb:zb + 64],
                                   xlnT_sb[ct][:, qo:qo + qw], start=True, stop=True)
                elif kind == "k":
                    ins = e.matmul(buf[0:64, 0:qw], wqkv_sb[:, zb + 64:zb + 128],
                                   xlnT_sb[ct][:, qo:qo + qw], start=True, stop=True)
                else:
                    t = qo
                    ins = e.matmul(buf[:, 0:64], xlnT_sb[ct][:, t * 128:(t + 1) * 128],
                                   wqkv_sb[:, zb + 128:zb + 192], start=True, stop=True)
                teQall[qg] = TE.one(ins)

            @BV
            def _(e, qg=qg, kind=kind, qo=qo, qw=qw, par=par, first_wait=first_wait):
                e.wait_ge(TE.h, teQall[qg])
                if first_wait is not None:
                    e.wait_ge(TE.h, first_wait)
                buf = qv_ps[qg % 2]
                if kind == "q":
                    ins = e.tensor_scalar_mul(qt_sb[par][0:64, qo:qo + qw],
                                              buf[0:64, 0:qw], 0.03125)
                elif kind == "k":
                    ins = e.tensor_copy(kt_sb[par][0:64, qo:qo + qw], buf[0:64, 0:qw])
                else:
                    ins = e.tensor_copy(vx_sb[par][:, qo, 0:64], buf[:, 0:64])
                veQall.append(VE.one(ins))

        gd_prior = GD.n

        @blk.gpsimd
        def _(e):
            e.wait_ge(GD.h, gd_prior)
            for sv, vv in x1_done:
                e.wait_ge(sv.h, vv)
            for d in range(DCN):
                GD.dma(e.dma_start(out=wout_sb[d][:], in_=woutT_d[d * 128:(d + 1) * 128, :]))
        gdWo = GD.n

        prev_batch_ve = G["init_ve"]
        sdXO_prev = {0: 0, 1: 0}
        for b in range(2):
            base = b * ST

            with contextlib.ExitStack() as pa:
                trb_ps = pa.enter_context(nc.psum_tensor(f"a{b}_trb", [128, 1024], BF16))
                ldX = [0] * ST
                aeS2 = [0] * ST
                veL2 = [0] * ST
                teT2 = [0] * ST
                veE2 = [0] * ST
                for t in range(ST):
                    j = base + t
                    bb = t % 2
                    sc = 32 * bb

                    @blk.sync
                    def _(e, t=t, j=j, bb=bb):
                        for sv, vv in x1_done:
                            e.wait_ge(sv.h, vv)
                        e.wait_ge(VE.h, prev_batch_ve)
                        if t >= 2:
                            e.wait_ge(VE.h, veL2[t - 2])
                        ldX[t] = LDX[bb].dma(e.dma_start(out=xt_sb[bb][:],
                                                         in_=xres_b[j * 128:(j + 1) * 128, :]))

                    @BV
                    def _(e, t=t, bb=bb, sc=sc):
                        e.wait_ge(LDX[bb].h, ldX[t])
                        e.bn_stats(scr[:, sc + 0:sc + 6], xt_sb[bb][:, 0:512])
                        e.bn_stats(scr[:, sc + 6:sc + 12], xt_sb[bb][:, 512:1024])
                        e.bn_aggr(scr[:, sc + 12:sc + 14], scr[:, sc + 0:sc + 12])
                        VE.one(e.tensor_scalar_add(scr[:, sc + 14:sc + 15],
                                                   scr[:, sc + 13:sc + 14], 1e-5))
                    vS = VE.n

                    @BA
                    def _(e, t=t, sc=sc, vS=vS):
                        e.wait_ge(VE.h, vS)
                        aeS2[t] = AE.one(e.activation(scr[:, sc + 15:sc + 16],
                                                      scr[:, sc + 14:sc + 15], AF.Sqrt))

                    @BV
                    def _(e, t=t, bb=bb, sc=sc):
                        e.wait_ge(AE.h, aeS2[t])
                        if t >= 2:
                            e.wait_ge(TE.h, teT2[t - 2])
                        e.reciprocal(scr[:, sc + 16:sc + 17], scr[:, sc + 15:sc + 16])
                        veL2[t] = VE.one(e.tensor_scalar(
                            xlnb_sb[bb][:], xt_sb[bb][:], scr[:, sc + 12:sc + 13],
                            scr[:, sc + 16:sc + 17], ALU.subtract, ALU.mult))

                    @blk.tensor
                    def _(e, t=t, bb=bb):
                        e.wait_ge(VE.h, veL2[t])
                        if t >= 1:
                            e.wait_ge(VE.h, veE2[t - 1])
                        for dc in range(DCN):
                            ins = e.transpose(trb_ps[:, dc * 128:(dc + 1) * 128],
                                              xlnb_sb[bb][:, dc * 128:(dc + 1) * 128],
                                              idb_sb[:])
                        teT2[t] = TE.one(ins)

                    @BV
                    def _(e, t=t):
                        e.wait_ge(TE.h, teT2[t])
                        for dc in range(DCN):
                            ins = e.tensor_copy(xlnT_sb[dc][:, t * 128:(t + 1) * 128],
                                                trb_ps[:, dc * 128:(dc + 1) * 128])
                        veE2[t] = VE.one(ins)
                xlnT_done = VE.n
            if G.get("stop_after") == "aln":
                return

            HPG = 8 if NH % 8 == 0 else NH
            NHG = NH // HPG
            pvEv = {}
            qkvDoneIdx = {}
            nmLast = xlnT_done
            veOnes_done = False
            for hg in range(NHG):
                with contextlib.ExitStack() as pb_:
                    Eb = pb_.enter_context
                    if not veOnes_done:
                        @BV
                        def _(e):
                            e.wait_ge(VE.h, xlnT_done)
                            e.wait_ge(GD.h, gdWo)
                            for i in range(2):
                                e.memset(vx_sb[i][:, :, 64:65], 1.0)
                                e.memset(qt_sb[i][64:128, :], 0.0)
                                ins = e.memset(kt_sb[i][64:128, :], 0.0)
                            VE.one(ins)
                        veOnes_done = True

                    for h in range(hg * HPG, (hg + 1) * HPG):
                        p64 = (h % 2) * 64
                        ct = h // 2
                        hh = h % HPG
                        par = h % 2

                        if (b, h) not in qkvDoneIdx:
                            fw = TE.n
                            for si, stp in enumerate(qkv_steps(h)):
                                emit_qkv_step(
                                    stp,
                                    first_wait=(fw if si == 0 else None),
                                    first_ve_wait=(xlnT_done if si == 0 else None))
                            qkvDoneIdx[(b, h)] = len(veQall) - 1

                        nxt = h + 1 if (h + 1) < NH else None
                        nsteps = qkv_steps(nxt) if nxt is not None else []
                        fw_key = (b, h - 1) if h >= 1 else (b - 1, NH - 1)
                        fw0 = G.get("a_pvdone", {}).get(fw_key)
                        tot_slots = len(QCH) * ST
                        slot = 0
                        nemit = 0

                        with contextlib.ExitStack() as psl:
                            s_ps = [psl.enter_context(
                                nc.psum_tensor(f"a{b}h{h}_s{i}", [128, 1024], F32))
                                for i in range(2)]
                            pv_ps = psl.enter_context(
                                nc.psum_tensor(f"a{b}h{h}_pv", [128, 1024], F32))

                            qkvDone = veQall[qkvDoneIdx[(b, h)]]
                            aplv = G.get("a_pvev_last", {}).get(fw_key)
                            prevAE = G.get("a_aePall", {}).get(fw_key)
                            NCH = len(QCH)
                            aePall = [AE.n + i + 1 for i in range(NCH * ST)]
                            teSall = []
                            tePVall = []
                            for ci, (qo, qw) in enumerate(QCH):
                                teS = [0] * ST
                                tePV = [0] * ST
                                aeP = aePall[ci * ST:(ci + 1) * ST]

                                def emit_pv(ktp, qw=qw, ci=ci):
                                    @blk.tensor
                                    def _(e, ktp=ktp, qw=qw, ci=ci):
                                        e.wait_ge(AE.h, aeP[ktp])
                                        if ktp == 0 and ci > 0:
                                            e.wait_ge(VE.h, pvEv[(b, h, ci - 1)])
                                        if ktp == 0 and ci == 0 and aplv is not None:
                                            e.wait_ge(VE.h, aplv)
                                        for (o2, w2) in _splits(qw, 512):
                                            ins = e.matmul(pv_ps[0:65, o2:o2 + w2],
                                                           vx_sb[par][:, ktp, :],
                                                           pb_sb[ktp % 6][:, o2:o2 + w2],
                                                           start=(ktp == 0),
                                                           stop=(ktp == ST - 1))
                                        tePV[ktp] = TE.one(ins)

                                for kt in range(ST):
                                    gi = ci * ST + kt

                                    @blk.tensor
                                    def _(e, kt=kt, gi=gi, qo=qo, qw=qw, ci=ci, h=h,
                                          qkvDone=qkvDone, par=par, hh=hh,
                                          nmLast=nmLast):
                                        if gi == 0:
                                            e.wait_ge(VE.h, qkvDone)
                                            if hh == 0:
                                                e.wait_ge(VE.h, nmLast)
                                            if prevAE is None and aplv is not None:
                                                e.wait_ge(VE.h, aplv)
                                        if gi >= 2:
                                            e.wait_ge(AE.h, aePall[gi - 2])
                                        elif prevAE is not None:
                                            e.wait_ge(AE.h, prevAE[gi - 2])
                                        for (o2, w2) in _splits(qw, 512):
                                            ins = e.matmul(s_ps[kt % 2][:, o2:o2 + w2],
                                                           kt_sb[par][:, kt * 128:(kt + 1) * 128],
                                                           qt_sb[par][:, qo + o2:qo + o2 + w2],
                                                           start=True, stop=True)
                                        teS[kt] = TE.one(ins)

                                    if kt >= 2:
                                        emit_pv(kt - 2)
                                    slot += 1
                                    tgt = (slot * len(nsteps)) // tot_slots
                                    while nemit < tgt:
                                        emit_qkv_step(
                                            nsteps[nemit],
                                            first_wait=(fw0 if nemit == 0 else None))
                                        nemit += 1
                                emit_pv(ST - 2)
                                emit_pv(ST - 1)
                                teSall.extend(teS)
                                tePVall.extend(tePV)
                                G.setdefault("a_pvdone", {})[(b, h)] = tePV[ST - 1]

                                @BV
                                def _(e, h=h, ct=ct, p64=p64, qo=qo, qw=qw, ci=ci,
                                      hh=hh, last=tePV[ST - 1]):
                                    e.wait_ge(TE.h, last)
                                    e.tensor_copy(attnT_sb[ct][p64:p64 + 64, qo:qo + qw],
                                                  pv_ps[0:64, 0:qw])
                                    pvEv[(b, h, ci)] = VE.one(e.tensor_copy(
                                        zrow_sb[:, hh * S + qo:hh * S + qo + qw],
                                        pv_ps[64:65, 0:qw]))

                            @BA
                            def _(e, teSall=teSall, tePVall=tePVall):
                                for gi in range(NCH * ST):
                                    ci2, kt2 = divmod(gi, ST)
                                    qw2 = QCH[ci2][1]
                                    e.wait_ge(TE.h, teSall[gi])
                                    if gi >= 6:
                                        e.wait_ge(TE.h, tePVall[gi - 6])
                                    e.activation(pb_sb[kt2 % 6][:, 0:qw2],
                                                 s_ps[kt2 % 2][:, 0:qw2], AF.Exp)
                            assert AE.n == aePall[-1], (AE.n, aePall[-1])
                            G.setdefault("a_aePall", {})[(b, h)] = aePall
                            G.setdefault("a_pvev_last", {})[(b, h)] = pvEv[(b, h, len(QCH) - 1)]
                        if nxt is not None:
                            assert nemit == len(nsteps)
                            qkvDoneIdx[(b, nxt)] = len(veQall) - 1
                    zEvAll = VE.n

                with contextlib.ExitStack() as pn:
                    En = pn.enter_context
                    tr2_ps = En(nc.psum_tensor(f"a{b}g{hg}_tr2", [128, 128], F32))
                    bc_ps = [En(nc.psum_tensor(f"a{b}g{hg}_bc{i}", [64, 1024], F32))
                             for i in range(2)]
                    zreg = zsum_b[:, hg * HPG * S:(hg + 1) * HPG * S]
                    NCOL = HPG * ST

                    @blk.gpsimd
                    def _(e, zreg=zreg):
                        e.wait_ge(VE.h, zEvAll)
                        e.wait_ge(LD.h, LD.n)
                        GD.dma(e.dma_start(out=zreg, in_=zrow_sb[:]))
                    gdZ = GD.n

                    @blk.sync
                    def _(e, zreg=zreg):
                        e.wait_ge(GD.h, gdZ)
                        LD.dma(e.dma_start(
                            out=zT_sb[:],
                            in_=zreg.rearrange("one (r c) -> (one r) c", c=128),
                            transpose=True))
                    ldZ = LD.n

                    @BV
                    def _(e):
                        e.wait_ge(LD.h, ldZ)
                        e.reciprocal(rT_sb[:], zT_sb[:])
                        VE.one(e.tensor_copy(rTb_sb[:], rT_sb[:]))
                    veR = VE.n

                    teT2r = []

                    @blk.tensor
                    def _(e, NCOL=NCOL):
                        e.wait_ge(VE.h, veR)
                        teT2r.append(TE.one(e.matmul(
                            tr2_ps[0:NCOL, 0:128], rTb_sb[:, 0:NCOL], idb_sb[:],
                            start=True, stop=True)))

                    @BV
                    def _(e, NCOL=NCOL, gprior=GD.n):
                        e.wait_ge(TE.h, teT2r[0])
                        e.wait_ge(GD.h, gprior)
                        VE.one(e.tensor_copy(rb128_sb[0:NCOL, :],
                                             tr2_ps[0:NCOL, 0:128]))
                    veT2r = VE.n

                    @blk.gpsimd
                    def _(e, NCOL=NCOL, lprior=LD.n):
                        e.wait_ge(VE.h, veT2r)
                        e.wait_ge(LD.h, lprior)
                        GD.dma(e.dma_start(
                            out=rr_d.rearrange("one p q -> (one p) q")[0:NCOL, :],
                            in_=rb128_sb[0:NCOL, :]))
                    gdRR = GD.n

                    @blk.sync
                    def _(e, NCOL=NCOL):
                        e.wait_ge(GD.h, gdRR)
                        LD.dma(e.dma_start(
                            out=rrow_sb[0:1, 0:NCOL * 128],
                            in_=rr_d.rearrange("one p q -> one (p q)")[0:1, 0:NCOL * 128]))
                    ldRR = LD.n

                    teBCn = []
                    veNM = []
                    for h in range(hg * HPG, (hg + 1) * HPG):
                        p64 = (h % 2) * 64
                        ct = h // 2
                        hh = h % HPG
                        for (qo, qw) in _splits(S, 1024):
                            @blk.tensor
                            def _(e, hh=hh, qo=qo, qw=qw, idx=len(teBCn)):
                                e.wait_ge(LD.h, ldRR)
                                if idx >= 2:
                                    e.wait_ge(VE.h, veNM[idx - 2])
                                for (o2, w2) in _splits(qw, 512):
                                    ins = e.matmul(
                                        bc_ps[idx % 2][:, o2:o2 + w2], onesb_sb[0:1, 0:64],
                                        rrow_sb[:, hh * S + qo + o2:hh * S + qo + o2 + w2],
                                        start=True, stop=True)
                                teBCn.append(TE.one(ins))

                            @BV
                            def _(e, ct=ct, p64=p64, qo=qo, qw=qw, idx=len(veNM)):
                                e.wait_ge(TE.h, teBCn[idx])
                                veNM.append(VE.one(e.tensor_mul(
                                    attnT_sb[ct][p64:p64 + 64, qo:qo + qw],
                                    attnT_sb[ct][p64:p64 + 64, qo:qo + qw],
                                    bc_ps[idx % 2][:, 0:qw])))
                    nmLast = veNM[-1]
            nmDone = nmLast

            with contextlib.ExitStack() as po:
                op_ps = [po.enter_context(
                    nc.psum_tensor(f"a{b}_op{i}", [128, 1024], F32)) for i in range(2)]
                ldXO = [0] * ST
                teOP = [0] * ST
                veAO = [0] * ST
                sdXO = [0] * ST
                for t in range(ST):
                    j = base + t
                    bb = t % 2

                    @blk.sync
                    def _(e, t=t, j=j, bb=bb):
                        e.wait_ge(VE.h, nmDone)
                        if t >= 2:
                            e.wait_ge(VE.h, veAO[t - 2])
                        ldXO[t] = LDX[bb].dma(e.dma_start(out=xt_sb[bb][:],
                                                          in_=xres_b[j * 128:(j + 1) * 128, :]))

                    @blk.tensor
                    def _(e, t=t, bb=bb):
                        e.wait_ge(VE.h, nmDone)
                        e.wait_ge(GD.h, gdWo)
                        if t >= 2:
                            e.wait_ge(VE.h, veAO[t - 2])
                        for (do, dw) in _splits(DM, 512):
                            for dc in range(DCN):
                                e.matmul(op_ps[bb][:, do:do + dw],
                                         attnT_sb[dc][:, t * 128:(t + 1) * 128],
                                         wout_sb[dc][:, do:do + dw],
                                         start=(dc == 0), stop=False)
                            ins = e.matmul(op_ps[bb][:, do:do + dw], onesb_sb[:],
                                           bout_sb[:, do:do + dw],
                                           start=False, stop=True)
                        teOP[t] = TE.one(ins)

                    @BV
                    def _(e, t=t, bb=bb):
                        e.wait_ge(TE.h, teOP[t])
                        e.wait_ge(LDX[bb].h, ldXO[t])
                        e.wait_ge(SDS[bb].h, sdXO[t - 2] if t >= 2 else sdXO_prev[bb])
                        veAO[t] = VE.one(e.tensor_add(xo_sb[bb][:], xt_sb[bb][:],
                                                      op_ps[bb][:]))

                    @BA
                    def _(e, t=t, j=j, bb=bb):
                        e.wait_ge(VE.h, veAO[t])
                        sdXO[t] = SDS[bb].dma(e.dma_start(out=xres_b[j * 128:(j + 1) * 128, :],
                                                          in_=xo_sb[bb][:]))

            prev_batch_ve = VE.n
            for bb2 in range(2):
                last = [sdXO[t2] for t2 in range(ST) if t2 % 2 == bb2]
                if last:
                    sdXO_prev[bb2] = last[-1]
        G["sd_xres_done"] = [(SDS[0], SDS[0].n), (SDS[1], SDS[1].n)]
        if debug:
            emit_dump(G, xres_b, dbg["x2"], T, G["sd_xres_done"])


def kernel(**inputs):
    x = np.asarray(inputs["L0_attn_x"], np.float32)
    Wq = np.asarray(inputs["L0_attn_Wq"], np.float32)
    Wk = np.asarray(inputs["L0_attn_Wk"], np.float32)
    Wv = np.asarray(inputs["L0_attn_Wv"], np.float32)
    Wout = np.asarray(inputs["L0_attn_W_out"], np.float32)
    bout = np.asarray(inputs["L0_attn_b_out"], np.float32)
    Wr = np.asarray(inputs["L0_ffn_Wr"], np.float32)
    W1 = np.asarray(inputs["L0_ffn_W1"], np.float32)
    W2 = np.asarray(inputs["L0_ffn_W2"], np.float32)

    nt = NT
    T = 128 * nt
    nc = build(nt=nt, ppad=PPAD, debug=False)
    in_maps = make_inputs(x, Wq, Wk, Wv, Wout, bout, Wr, W1, W2, nt)
    res = run_bass_kernel_spmd(nc, in_maps, core_ids=list(range(NCORE)))
    out = np.stack([r["o"].reshape(2, T // 2, DM) for r in res.results], axis=0)
    return np.ascontiguousarray(out.reshape(16, T // 2, DM)).astype(np.float32)


def make_inputs(x, Wq, Wk, Wv, Wout, bout, Wr, W1, W2, nt):
    T = 128 * nt
    w3 = np.concatenate([Wq.T, Wk.T, Wv.T], axis=1)
    z = np.zeros_like(w3)
    wqkv = np.stack([np.concatenate([w3, z], axis=0),
                     np.concatenate([z, w3], axis=0)])
    ngrp = FF // 256
    w1p = np.ascontiguousarray(
        W1.reshape(NEXP, DM // 128, 128, ngrp, 256).transpose(0, 3, 2, 1, 4)
    ).reshape(NEXP, ngrp, 128, (DM // 128) * 256)
    in_common = dict(
        wr=np.ascontiguousarray(Wr),
        w1=w1p,
        w2=np.ascontiguousarray(W2),
        wqkv=np.ascontiguousarray(wqkv),
        woutT=np.ascontiguousarray(Wout.T),
        bout=np.ascontiguousarray(bout.reshape(1, DM)),
        iota8=np.tile(np.arange(NEXP, dtype=np.float32), (128, 1)),
        tri=(np.arange(128)[:, None] < np.arange(128)[None, :]).astype(np.float32),
        idf=np.eye(128, dtype=np.float32),
        idb=np.eye(128).astype(ml_dtypes.bfloat16),
        onesin=np.ones((128, 128), np.float32),
    )
    in_maps = []
    for c in range(NCORE):
        m = dict(in_common)
        m["x"] = np.ascontiguousarray(x[2 * c:2 * c + 2].reshape(T, DM))
        m["pmask"] = (np.arange(NCORE) < c).astype(np.float32).reshape(NCORE, 1)
        in_maps.append(m)
    return in_maps
